# version 28
# baseline (speedup 1.0000x reference)
import numpy as np
import ml_dtypes
import concourse.bass as bass
import concourse.mybir as mybir
from concourse.bass_utils import run_bass_kernel_spmd

F32 = mybir.dt.float32
BF16 = mybir.dt.bfloat16
AF = mybir.ActivationFunctionType
ALU = mybir.AluOpType
AX = mybir.AxisListType
BF = ml_dtypes.bfloat16

D = 1024
T = 2048
C = 8
TT = 16
NST = 16
DCONV = 4
F_FF = 1408
FC = 11
NEXP = 8
NSH = 2
NHEAD = 16
DH = 64
LN_EPS = 1e-5

SB_BASE = 16896
SBUF_BYTES = 229376 - 512 - SB_BASE


class _SBWrap:
    __slots__ = ("t", "_region")

    def __init__(self, t, region):
        self.t = t
        self._region = region

    def ap(self):
        return self.t.ap()


class SBAlloc:

    def __init__(self, nc):
        self.nc = nc
        self.free = [(SB_BASE, SBUF_BYTES)]
        self.high = 0
        self.n = 0

    def alloc(self, name, shape, dtype):
        nbytes = int(np.prod(shape[1:])) * mybir.dt.size(dtype)
        nbytes = (nbytes + 31) & ~31
        for i, (off, sz) in enumerate(self.free):
            if sz >= nbytes:
                self.free[i] = (off + nbytes, sz - nbytes)
                if self.free[i][1] == 0:
                    self.free.pop(i)
                self.n += 1
                t = self.nc.alloc_sbuf_tensor_at(
                    f"{name}_{self.n}", shape, dtype, offset=off)
                t = _SBWrap(t, (off, nbytes))
                self.high = max(self.high, off + nbytes)
                return t
        raise MemoryError(f"SBUF alloc failed for {name} {shape} ({nbytes}B); "
                          f"free={self.free}")

    def free_(self, t):
        off, nbytes = t._region
        self.free.append((off, nbytes))
        self.free.sort()
        merged = []
        for o, s in self.free:
            if merged and merged[-1][0] + merged[-1][1] == o:
                merged[-1] = (merged[-1][0], merged[-1][1] + s)
            else:
                merged.append((o, s))
        self.free = [tuple(x) for x in merged]


class Sched:
    COMPUTE = ("T", "V", "S", "P")
    QUEUES = {"Q": "QD", "P": "PD", "S": "SD"}

    def __init__(self):
        self.q = {e: [] for e in "TVSPQ"}
        self.ms = {e: 0 for e in self.COMPUTE}
        self.dmacnt = {k: 0 for k in self.QUEUES.values()}
        self.pending = {e: [] for e in "TVSPQ"}
        self.active = {e: False for e in "TVSPQ"}

    def op(self, e, fn, deps=(), ms=False, sdeps=()):
        deps = list(deps) + self.pending[e]
        sdeps = list(sdeps)
        self.pending[e] = []
        self.active[e] = True
        tok = None
        if ms:
            self.ms[e] += 1
            tok = (e, self.ms[e])

        def emit(eng, sems, waited, own):
            for (sk, val) in deps:
                if sk == own:
                    continue
                if waited.get(sk, 0) >= val:
                    continue
                eng.wait_ge(sems[sk], val)
                waited[sk] = val
            for (sk, val) in sdeps:
                if waited.get(sk, 0) >= val:
                    continue
                eng.wait_ge(sems[sk], val)
                waited[sk] = val
            inst = fn(eng)
            if tok is not None:
                inst.then_inc(sems[e], 1)
        self.q[e].append(emit)
        return tok

    def dma(self, e, fn, deps=(), fence=True):
        dk = self.QUEUES[e]
        deps = list(deps) + self.pending[e]
        self.pending[e] = []
        self.active[e] = True
        prev = self.dmacnt[dk] if fence else 0
        self.dmacnt[dk] += 16
        tok = (dk, self.dmacnt[dk])

        def emit(eng, sems, waited, own):
            for (sk, val) in [(dk, prev)] + list(deps):
                if sk == own:
                    continue
                if val <= 0 or waited.get(sk, 0) >= val:
                    continue
                eng.wait_ge(sems[sk], val)
                waited[sk] = val
            fn(eng).then_inc(sems[dk], 16)
        self.q[e].append(emit)
        return tok

    def barrier(self):
        toks = []
        for e in self.COMPUTE:
            if self.active[e]:
                self.ms[e] += 1
                v = self.ms[e]

                def emit(eng, sems, waited, own, _e=e, _v=v):
                    eng.drain().then_inc(sems[_e], 1)
                self.q[e].append(emit)
                toks.append((e, v))
        for qe, dk in self.QUEUES.items():
            if self.dmacnt[dk] > 0:
                toks.append((dk, self.dmacnt[dk]))
        for e in "TVSPQ":
            self.pending[e] = list(toks)
            self.active[e] = False
        return toks

    def build(self, nc):
        sems = {}
        ctxs = []
        for key in list(self.COMPUTE) + list(self.QUEUES.values()):
            ctxs.append(nc.semaphore(f"sem_{key}"))
        handles = [c.__enter__() for c in ctxs]
        for key, h in zip(list(self.COMPUTE) + list(self.QUEUES.values()), handles):
            sems[key] = h
        blk_ctx = nc.Block()
        blk = blk_ctx.__enter__()
        own_of = {"T": "T", "V": "V", "S": "S", "P": "P", "Q": None}

        def mk_runner(e):
            def run(eng):
                waited = {}
                for emit in self.q[e]:
                    emit(eng, sems, waited, own_of[e])
            return run
        if self.q["T"]:
            blk.tensor(mk_runner("T"))
        if self.q["V"]:
            blk.vector(mk_runner("V"))
        if self.q["S"]:
            blk.scalar(mk_runner("S"))
        if self.q["P"]:
            blk.gpsimd(mk_runner("P"))
        if self.q["Q"]:
            blk.sync(mk_runner("Q"))
        blk_ctx.__exit__(None, None, None)
        for c in ctxs:
            c.__exit__(None, None, None)


class Ctx:
    def __init__(self, nc, cfg):
        self.nc = nc
        self.cfg = cfg
        self.sch = Sched()
        self.sb = SBAlloc(nc)
        self.dram = {}
        self.psum = [nc.alloc_psum_tensor(f"pb{i}", [128, 512], F32)
                     for i in range(8)]
        self.ps_tok = [()] * 8
        self.const = {}

    def dram_t(self, name, shape, dtype):
        if name in self.dram:
            return self.dram[name]
        kind = "Internal"
        if name in self.cfg.get("ins", ()):
            kind = "ExternalInput"
        elif name in self.cfg.get("outs", ()):
            kind = "ExternalOutput"
        t = self.nc.dram_tensor(name, shape, dtype, kind=kind)
        self.dram[name] = t
        return t

    def bank(self, i, shape=None, dtype=F32):
        ap = self.psum[i].ap()
        if dtype != F32:
            ap = ap.bitcast(dtype)
        if shape is not None:
            ap = ap[:shape[0], :int(np.prod(shape[1:]))]
            if len(shape) > 2:
                ap = ap.rearrange(
                    "p (a b) -> p a b", a=shape[1])
        return ap


def phase_ln0(ctx):
    nc, sch, sb = ctx.nc, ctx.sch, ctx.sb
    x_in = ctx.dram_t("x", [T, D], F32).ap()
    xn_d = ctx.dram_t("xn_bf", [128, C * T], BF16).ap()
    lg_d = ctx.dram_t("logits_tm", [128, TT * NEXP], F32).ap()
    gate_w_d = ctx.dram_t("gate_w", [128, C, NEXP], F32).ap()
    id_d = ctx.dram_t("ident_f32", [128, 128], F32).ap()

    x0t = [sb.alloc("x0t", [128, D], F32) for _ in range(2)]
    sqb = sb.alloc("sqb", [128, D], F32)
    stat = sb.alloc("stat", [128, 8], F32)
    xn_tm = sb.alloc("xn_tm", [128, D], F32)
    tmpT = [sb.alloc("tmpT", [128, 128], F32) for _ in range(4)]
    xn_fm = sb.alloc("xn_fm", [128, C, T], BF16)
    gate_w = sb.alloc("gate_w", [128, C, NEXP], F32)
    ident = sb.alloc("ident", [128, 128], F32)
    logits_sb = sb.alloc("logits_sb", [128, TT, NEXP], F32)
    wc_st = [sb.alloc("wc_st", [128, 2048], F32) for _ in range(2)]
    wc_bf = [sb.alloc("wc_bf", [128, 2048], BF16) for _ in range(2)]
    wc_pieces = []
    for wname in ("wq", "wk", "wv", "wo"):
        wsrc = ctx.dram_t(wname + "T", [128, C, D], F32).ap().rearrange(
            "p a b -> p (a b)")
        wdst = ctx.dram_t(wname + "bf", [128, C * D], BF16).ap()
        for pc in range(4):
            wc_pieces.append((wsrc, wdst, pc))
    wc_out_tok = [None, None]

    def emit_wcast(idx):
        if idx >= len(wc_pieces):
            return
        wsrc, wdst, pc = wc_pieces[idx]
        a = idx % 2
        dd = [wc_out_tok[a]] if wc_out_tok[a] else []
        ldw = sch.dma("Q", lambda q, wsrc=wsrc, pc=pc, a=a: q.dma_start(
            out=wc_st[a].ap()[:], in_=wsrc[:, 2048 * pc:2048 * (pc + 1)]),
            deps=dd)
        cst = sch.op("V", lambda v, a=a: v.tensor_copy(
            wc_bf[a].ap()[:], wc_st[a].ap()[:]), deps=[ldw], ms=True)
        wc_out_tok[a] = sch.dma("Q", lambda q, wdst=wdst, pc=pc, a=a:
                                q.dma_start(
            out=wdst[:, 2048 * pc:2048 * (pc + 1)], in_=wc_bf[a].ap()[:]),
            deps=[cst])

    sch.dma("Q", lambda q: q.dma_start(out=gate_w.ap()[:], in_=gate_w_d[:]))
    sch.dma("Q", lambda q: q.dma_start(out=ident.ap()[:], in_=id_d[:]))
    ld_tok = [None] * TT
    ld_tok[0] = sch.dma("Q", lambda q: q.dma_start(
        out=x0t[0].ap()[:], in_=x_in[0:128, :]))
    sch.barrier()

    for i in range(TT):
        emit_wcast(i)
        bi = i % 2
        if i + 1 < TT:
            ld_tok[i + 1] = sch.dma("Q", lambda q, i=i: q.dma_start(
                out=x0t[(i + 1) % 2].ap()[:],
                in_=x_in[128 * (i + 1):128 * (i + 2), :]))
        dl = [ld_tok[i]]
        t_sum = sch.op("V", lambda v, bi=bi: v.reduce_sum(
            stat.ap()[:, 0:1], x0t[bi].ap()[:], axis=AX.X), deps=dl, ms=True)
        t_sq0 = sch.op("S", lambda s, bi=bi: s.activation(
            sqb.ap()[:], x0t[bi].ap()[:], AF.Square), deps=dl, ms=True)
        t_sq = sch.op("V", lambda v: v.reduce_sum(
            stat.ap()[:, 1:2], sqb.ap()[:], axis=AX.X), deps=[t_sq0], ms=True)
        t_mean = sch.op("V", lambda v: v.tensor_scalar(
            stat.ap()[:, 2:3], stat.ap()[:, 0:1], 1.0 / D, None, ALU.mult),
            sdeps=[t_sum], ms=True)
        t_msq = sch.op("V", lambda v: v.tensor_mul(
            stat.ap()[:, 3:4], stat.ap()[:, 2:3], stat.ap()[:, 2:3]),
            sdeps=[t_mean], ms=True)
        t_v0 = sch.op("V", lambda v: v.tensor_scalar(
            stat.ap()[:, 4:5], stat.ap()[:, 1:2], 1.0 / D, None, ALU.mult),
            deps=[t_sq], sdeps=[t_sq], ms=True)
        t_var = sch.op("V", lambda v: v.tensor_scalar(
            stat.ap()[:, 5:6], stat.ap()[:, 4:5], stat.ap()[:, 3:4],
            float(LN_EPS), ALU.subtract, ALU.add),
            sdeps=[t_msq, t_v0], ms=True)
        t_ln = sch.op("S", lambda s: s.activation(
            stat.ap()[:, 6:7], stat.ap()[:, 5:6], AF.Ln),
            deps=[t_var], ms=True)
        t_half = sch.op("V", lambda v: v.tensor_scalar(
            stat.ap()[:, 6:7], stat.ap()[:, 6:7], -0.5, None, ALU.mult),
            deps=[t_ln], ms=True)
        t_rstd = sch.op("S", lambda s: s.activation(
            stat.ap()[:, 7:8], stat.ap()[:, 6:7], AF.Exp),
            deps=[t_half], ms=True)
        t_xn = sch.op("V", lambda v, bi=bi: v.tensor_scalar(
            xn_tm.ap()[:], x0t[bi].ap()[:], stat.ap()[:, 2:3],
            stat.ap()[:, 7:8], ALU.subtract, ALU.mult), deps=[t_rstd],
            sdeps=[t_mean], ms=True)
        if ctx.cfg.get("dbg_ln0"):
            dbg = ctx.dram_t("xn_tm_dbg", [T, D], F32).ap()
            sch.dma("Q", lambda q, i=i: q.dma_start(
                out=dbg[128 * i:128 * (i + 1), :], in_=xn_tm.ap()[:]),
                deps=[t_xn])
            dbg2 = ctx.dram_t("stat_dbg", [T, 8], F32).ap()
            sch.dma("Q", lambda q, i=i: q.dma_start(
                out=dbg2[128 * i:128 * (i + 1), :], in_=stat.ap()[:]),
                deps=[t_xn])
        ev = {}
        casts = {}
        for c in range(C):
            pb = c % 2
            tb = c % 4
            t_tr = sch.op("T", lambda t, c=c, pb=pb: t.transpose(
                ctx.bank(pb, [128, 128]), xn_tm.ap()[:, 128 * c:128 * (c + 1)],
                ident.ap()[:]),
                deps=[t_xn] + ([ev[c - 2]] if c >= 2 else []), ms=True)
            t_ev = sch.op("S", lambda s, pb=pb, tb=tb: s.copy(
                tmpT[tb].ap()[:], ctx.bank(pb, [128, 128])),
                deps=[t_tr] + ([casts[c - 4]] if c >= 4 else []), ms=True)
            ev[c] = t_ev
            casts[c] = sch.op("V", lambda v, i=i, c=c, tb=tb: v.tensor_copy(
                xn_fm.ap()[:, c, 128 * i:128 * (i + 1)], tmpT[tb].ap()[:]),
                deps=[t_ev], ms=True)
            sch.op("T", lambda t, i=i, c=c, tb=tb: t.matmul(
                ctx.bank(2, [128, TT, NEXP])[:, i, :], tmpT[tb].ap()[:],
                gate_w.ap()[:, c, :], start=(c == 0), stop=(c == C - 1)),
                deps=[t_ev], ms=True)
        sch.op("S", lambda s, i=i: s.copy(
            logits_sb.ap()[:, i, :], ctx.bank(2, [128, TT, NEXP])[:, i, :]),
            deps=[("T", sch.ms["T"])], ms=True)
        sch.barrier()
    sch.dma("Q", lambda q: q.dma_start(out=xn_d[:], in_=xn_fm.ap()[:]))
    sch.dma("Q", lambda q: q.dma_start(
        out=lg_d[:], in_=logits_sb.ap().rearrange("p a b -> p (a b)")))
    sch.barrier()
    for t in (x0t + [sqb, stat, xn_tm, gate_w, ident, logits_sb, xn_fm]
              + tmpT + wc_st + wc_bf):
        sb.free_(t)


def _load_cast_w(ctx, dname, cchunks, xwidth, name, deps=()):
    sch, sb = ctx.sch, ctx.sb
    half = cchunks // 2
    wd = ctx.dram_t(dname, [128, cchunks, xwidth], F32).ap()
    stage = [sb.alloc(f"{name}_st", [128, half * xwidth], F32)
             for _ in range(2)]
    wbf = sb.alloc(f"{name}_bf", [128, cchunks, xwidth], BF16)
    ld0 = sch.dma("Q", lambda q: q.dma_start(
        out=stage[0].ap()[:],
        in_=wd[:, 0:half, :].rearrange("p a b -> p (a b)")), deps=list(deps))
    ld1 = sch.dma("Q", lambda q: q.dma_start(
        out=stage[1].ap()[:],
        in_=wd[:, half:cchunks, :].rearrange("p a b -> p (a b)")),
        fence=False)
    t0 = sch.op("V", lambda v: v.tensor_copy(
        wbf.ap()[:, 0:half, :].rearrange("p a b -> p (a b)"),
        stage[0].ap()[:]), deps=[ld1], ms=True)
    t1 = sch.op("V", lambda v: v.tensor_copy(
        wbf.ap()[:, half:cchunks, :].rearrange("p a b -> p (a b)"),
        stage[1].ap()[:]), deps=[ld1], ms=True)
    for s in stage:
        sb.free_(s)
    return wbf, t1, None


def phase_mprep(ctx):
    nc, sch, sb = ctx.nc, ctx.sch, ctx.sb
    xn_d = ctx.dram_t("xn_bf", [128, C * T], BF16).ap()
    gg_d = ctx.dram_t("ggate_d", [128, C * T], BF16).ap()
    xc_d = ctx.dram_t("xconv_d", [128, C * T], BF16).ap()
    dt_d = ctx.dram_t("dt_d", [128, C * T], BF16).ap()
    u_d = ctx.dram_t("u_d", [128, C * T], BF16).ap()
    bc_d = ctx.dram_t("bc_d", [32, T], F32).ap()

    xn = sb.alloc("xn", [128, C, T], BF16)
    t_xn = sch.dma("Q", lambda q: q.dma_start(out=xn.ap()[:], in_=xn_d[:]))
    bg_d = ctx.dram_t("b_in_gate", [128, C], F32).ap()
    bs_d = ctx.dram_t("b_in_ssm", [128, C], F32).ap()
    bdt_d = ctx.dram_t("b_dt", [128, C], F32).ap()
    cw_d = ctx.dram_t("conv_w", [128, C, DCONV], F32).ap()
    cb_d = ctx.dram_t("conv_b", [128, C], F32).ap()
    bbc_d = ctx.dram_t("b_bc", [32, 1], F32).ap()
    small = sb.alloc("small", [128, 8 * C], F32)
    cw = sb.alloc("cw", [128, C, DCONV], F32)
    bbc = sb.alloc("bbc", [32, 1], F32)
    t_sm = [sch.dma("Q", lambda q, i=i, dd=dd: q.dma_start(
        out=small.ap()[:, i * C:(i + 1) * C], in_=dd))
        for i, dd in enumerate([bg_d[:], bs_d[:], bdt_d[:], cb_d[:]])]
    t_sm.append(sch.dma("Q", lambda q: q.dma_start(out=cw.ap()[:], in_=cw_d[:])))
    t_sm.append(sch.dma("Q", lambda q: q.dma_start(out=bbc.ap()[:], in_=bbc_d[:])))
    bg = lambda c: small.ap()[:, 0 * C + c:0 * C + c + 1]
    bs = lambda c: small.ap()[:, 1 * C + c:1 * C + c + 1]
    bdt = lambda c: small.ap()[:, 2 * C + c:2 * C + c + 1]
    cb = lambda c: small.ap()[:, 3 * C + c:3 * C + c + 1]

    def fm_matmul_out(wbf, wtok, rhs_tile, rhs_tok, n_mo, evac_fn):
        for mo in range(n_mo):
            for tt in range(4):
                b = (mo * 4 + tt) % 4
                t_mm = None
                for kc in range(C):
                    dd = [wtok, rhs_tok] if kc == 0 else []
                    if kc == 0:
                        dd += list(ctx.ps_tok[b])
                    t_mm = sch.op("T", lambda t, mo=mo, tt=tt, kc=kc, b=b: t.matmul(
                        ctx.bank(b), wbf.ap()[:, kc, 128 * mo:128 * (mo + 1)],
                        rhs_tile.ap()[:, kc, 512 * tt:512 * (tt + 1)],
                        start=(kc == 0), stop=(kc == C - 1)),
                        deps=dd, ms=(kc == C - 1))
                tok = evac_fn(mo, tt, b, [t_mm])
                ctx.ps_tok[b] = (tok,)

    wg, wg_tok, _ = _load_cast_w(ctx, "w_in_gate", C, 1024, "wg")
    if ctx.cfg.get("dbg_wg"):
        wg_d = ctx.dram_t("wg_dbg", [128, C * 1024], BF16).ap()
        sch.dma("Q", lambda q: q.dma_start(
            out=wg_d[:], in_=wg.ap().rearrange("p a b -> p (a b)")),
            deps=[wg_tok])
    ggate = sb.alloc("ggate", [128, C, T], BF16)

    def evac_gate(mo, tt, b, deps):
        return sch.op("S", lambda s, mo=mo, tt=tt, b=b: s.activation(
            ggate.ap()[:, mo, 512 * tt:512 * (tt + 1)], ctx.bank(b)[:, 0:512],
            AF.Silu, bias=bg(mo)), deps=deps + t_sm, ms=True)
    fm_matmul_out(wg, wg_tok, xn, t_xn, C, evac_gate)
    sch.dma("Q", lambda q: q.dma_start(out=gg_d[:], in_=ggate.ap()[:]),
            deps=[("S", sch.ms["S"])])
    sch.barrier()
    sb.free_(ggate)
    sb.free_(wg)

    ws, ws_tok, _ = _load_cast_w(ctx, "w_in_ssm", C, 1024, "ws")
    TP = T + DCONV - 1 + 1
    xsp = sb.alloc("xsp", [128, C, TP], BF16)
    t_z = sch.op("V", lambda v: v.memset(xsp.ap()[:, :, 0:DCONV - 1], 0.0),
                 ms=True)

    def evac_ssm(mo, tt, b, deps):
        return sch.op("V", lambda v, mo=mo, tt=tt, b=b: v.tensor_scalar(
            xsp.ap()[:, mo, 3 + 512 * tt:3 + 512 * (tt + 1)],
            ctx.bank(b)[:, 0:512], bs(mo), None, ALU.add),
            deps=deps + t_sm + [t_z], ms=True)
    fm_matmul_out(ws, ws_tok, xn, t_xn, C, evac_ssm)
    sch.barrier()
    sb.free_(ws)
    sb.free_(xn)

    xconv = sb.alloc("xconv", [128, C, T], BF16)
    acc = [sb.alloc("acc", [128, T], F32) for _ in range(2)]
    acc_use = [None, None]
    for c in range(C):
        a = c % 2
        dd = [acc_use[a]] if acc_use[a] else []
        t0 = sch.op("V", lambda v, c=c, a=a: v.tensor_scalar(
            acc[a].ap()[:], xsp.ap()[:, c, 0:T], cw.ap()[:, c, 0:1], None,
            ALU.mult), deps=dd, ms=True)
        for j in range(1, DCONV):
            t0 = sch.op("V", lambda v, c=c, j=j, a=a: v.scalar_tensor_tensor(
                acc[a].ap()[:], xsp.ap()[:, c, j:j + T], cw.ap()[:, c, j:j + 1],
                acc[a].ap()[:], ALU.mult, ALU.add), ms=True)
        t_cv = sch.op("S", lambda s, c=c, a=a: s.activation(
            xconv.ap()[:, c, :], acc[a].ap()[:], AF.Silu, bias=cb(c)),
            deps=[t0] + t_sm, ms=True)
        acc_use[a] = t_cv
    sch.dma("Q", lambda q: q.dma_start(out=xc_d[:], in_=xconv.ap()[:]),
            deps=[("S", sch.ms["S"])])
    sch.barrier()
    sb.free_(xsp)
    for t in acc:
        sb.free_(t)

    wdt, wdt_tok, _ = _load_cast_w(ctx, "w_dt", C, 1024, "wdt")
    dt = sb.alloc("dt", [128, C, T], BF16)
    sp = [sb.alloc("sp", [128, 512], F32) for _ in range(2)]
    sp_use = [None, None]

    def evac_dt(mo, tt, b, deps):
        a = (mo * 4 + tt) % 2
        dd = deps + t_sm + ([sp_use[a]] if sp_use[a] else [])
        t_e = sch.op("S", lambda s, b=b, a=a, mo=mo: s.activation(
            sp[a].ap()[:], ctx.bank(b)[:, 0:512], AF.Exp, bias=bdt(mo)),
            deps=dd, ms=True)
        t_p = sch.op("V", lambda v, a=a: v.tensor_scalar(
            sp[a].ap()[:], sp[a].ap()[:], 1.0, None, ALU.add),
            deps=[t_e], ms=True)
        t_l = sch.op("S", lambda s, mo=mo, tt=tt, a=a: s.activation(
            dt.ap()[:, mo, 512 * tt:512 * (tt + 1)], sp[a].ap()[:], AF.Ln),
            deps=[t_p], ms=True)
        sp_use[a] = t_l
        return t_e
    fm_matmul_out(wdt, wdt_tok, xconv, ("S", 0), C, evac_dt)
    sch.dma("Q", lambda q: q.dma_start(out=dt_d[:], in_=dt.ap()[:]),
            deps=[("S", sch.ms["S"])])
    sch.barrier()
    sb.free_(wdt)
    for t in sp:
        sb.free_(t)

    wbc_d = ctx.dram_t("w_bc", [128, C, 32], F32).ap()
    bcbf_d = ctx.dram_t("bc_bf", [32, T], BF16).ap()
    wbc_st = sb.alloc("wbc_st", [128, C, 32], F32)
    wbc = sb.alloc("wbc", [128, C, 32], BF16)
    t_wb = sch.dma("Q", lambda q: q.dma_start(out=wbc_st.ap()[:], in_=wbc_d[:]))
    t_wbc = sch.op("V", lambda v: v.tensor_copy(wbc.ap()[:], wbc_st.ap()[:]),
                   deps=[t_wb], ms=True)
    bc_sb = sb.alloc("bc_sb", [32, T], F32)
    bcbf_sb = sb.alloc("bcbf_sb", [32, T], BF16)
    for tt in range(4):
        b = tt % 2
        t_mm = None
        for kc in range(C):
            dd = ([t_wbc] if kc == 0 else []) + \
                (list(ctx.ps_tok[b]) if kc == 0 else [])
            t_mm = sch.op("T", lambda t, tt=tt, kc=kc, b=b: t.matmul(
                ctx.bank(b, [32, 512]), wbc.ap()[:, kc, :],
                xconv.ap()[:, kc, 512 * tt:512 * (tt + 1)],
                start=(kc == 0), stop=(kc == C - 1)), deps=dd,
                ms=(kc == C - 1))
        t_ev = sch.op("V", lambda v, tt=tt, b=b: v.tensor_scalar(
            bc_sb.ap()[:, 512 * tt:512 * (tt + 1)], ctx.bank(b, [32, 512]),
            bbc.ap()[:], None, ALU.add), deps=[t_mm] + t_sm, ms=True)
        sch.op("S", lambda s, tt=tt: s.copy(
            bcbf_sb.ap()[:, 512 * tt:512 * (tt + 1)],
            bc_sb.ap()[:, 512 * tt:512 * (tt + 1)]), deps=[t_ev], ms=True)
        ctx.ps_tok[b] = (t_ev,)
    sch.dma("Q", lambda q: q.dma_start(out=bc_d[:], in_=bc_sb.ap()[:]),
            deps=[("V", sch.ms["V"])])
    sch.dma("Q", lambda q: q.dma_start(out=bcbf_d[:], in_=bcbf_sb.ap()[:]),
            deps=[("S", sch.ms["S"])])

    u = sb.alloc("u", [128, C, T], BF16)
    t_u = None
    for c in range(C):
        t_u = sch.op("V", lambda v, c=c: v.tensor_mul(
            u.ap()[:, c, :], dt.ap()[:, c, :], xconv.ap()[:, c, :]),
            ms=(c == C - 1))
    sch.dma("Q", lambda q: q.dma_start(out=u_d[:], in_=u.ap()[:]),
            deps=[t_u])
    sch.barrier()
    for t in (dt, xconv, u, bc_sb, bcbf_sb, wbc, wbc_st, small, cw, bbc):
        sb.free_(t)


def phase_moescan(ctx):
    nc, sch, sb = ctx.nc, ctx.sch, ctx.sb
    TQ = 512
    NQ = T // TQ
    dt_d = ctx.dram_t("dt_d", [128, C * T], BF16).ap().rearrange(
        "p (c t) -> p c t", c=C)
    u_d = ctx.dram_t("u_d", [128, C * T], BF16).ap().rearrange(
        "p (c t) -> p c t", c=C)
    bcbf_d = ctx.dram_t("bc_bf", [32, T], BF16).ap()
    al_d = ctx.dram_t("A_log", [128, C, NST], F32).ap()
    y_dr = ctx.dram_t("y32_d", [128, C * T], F32).ap().rearrange(
        "p (c t) -> p c t", c=C)
    xn_d = ctx.dram_t("xn_bf", [128, C * T], BF16).ap()
    lg_d = ctx.dram_t("logits_tm", [128, TT * NEXP], F32).ap()
    moe_d = ctx.dram_t("moe_d", [128, C * T], BF16).ap()
    eg_d = ctx.dram_t("exp_gate", [NEXP, 128, C, F_FF], F32).ap()
    eu_d = ctx.dram_t("exp_up", [NEXP, 128, C, F_FF], F32).ap()
    ed_d = ctx.dram_t("exp_down", [NEXP, 128, FC, D], F32).ap()
    sg_d = ctx.dram_t("sh_gate", [NSH, 128, C, F_FF], F32).ap()
    su_d = ctx.dram_t("sh_up", [NSH, 128, C, F_FF], F32).ap()
    sd_d = ctx.dram_t("sh_down", [NSH, 128, FC, D], F32).ap()
    wte_r = ctx.dram_t("wte_r", [8, T], BF16).ap()

    xn = sb.alloc("xn", [128, C, T], BF16)
    acc = sb.alloc("acc", [128, C, T], BF16)
    hsb = sb.alloc("hsb", [128, FC, 512], BF16)
    gsil = [sb.alloc("gsil", [128, 512], BF16) for _ in range(4)]
    dtmp = [sb.alloc("dtmp", [128, 512], BF16) for _ in range(2)]
    wteb = sb.alloc("wteb", [128, 512], BF16)
    dtq = sb.alloc("dtq", [128, C, TQ], BF16)
    uq = sb.alloc("uq", [128, C, TQ], BF16)
    y_acc = sb.alloc("y_acc", [128, C, TQ], F32)
    ab = [sb.alloc("a", [128, TQ], F32) for _ in range(3)]
    bb = [sb.alloc("b", [128, TQ], BF16) for _ in range(2)]
    hb = [sb.alloc("h", [128, TQ], BF16) for _ in range(2)]
    zb = [sb.alloc("z", [128, TQ], BF16)]
    Bb = [sb.alloc("Bb", [128, TQ], BF16) for _ in range(2)]
    Cb = [sb.alloc("Cb", [128, TQ], BF16) for _ in range(2)]
    alog = sb.alloc("alog", [128, C, NST], F32)
    asb = alog
    hlast = sb.alloc("hlast", [128, NST * C], F32)

    sch.dma("Q", lambda q: q.dma_start(out=xn.ap()[:], in_=xn_d[:]))
    sch.dma("Q", lambda q: q.dma_start(out=alog.ap()[:], in_=al_d[:]),
            fence=False)

    lg = sb.alloc("lg", [128, TT, NEXP], F32)
    rt = sb.alloc("rt", [128, 16 * TT], F32)
    eq1 = sb.alloc("eq1", [128, TT, NEXP], F32)
    eq2 = sb.alloc("eq2", [128, TT, NEXP], F32)
    msk = sb.alloc("msk", [128, TT, NEXP], F32)
    wte = sb.alloc("wte", [128, TT, NEXP], BF16)
    wteT = sb.alloc("wteT", [8, TT, 128], BF16)
    idb = sb.alloc("idb", [128, 128], BF16)
    idb_d = ctx.dram_t("ident_bf", [128, 128], BF16).ap()
    sch.dma("Q", lambda q: q.dma_start(
        out=lg.ap().rearrange("p a b -> p (a b)"), in_=lg_d[:]), fence=False)
    t_all = sch.dma("Q", lambda q: q.dma_start(out=idb.ap()[:], in_=idb_d[:]),
                    fence=False)
    t_xn = t_idb = t_all

    m1 = rt.ap()[:, 0:TT]
    m2 = rt.ap()[:, TT:2 * TT]
    d12 = rt.ap()[:, 2 * TT:3 * TT]
    w1 = rt.ap()[:, 3 * TT:4 * TT]
    w2 = rt.ap()[:, 4 * TT:5 * TT]
    lg3 = lg.ap()
    r0 = sch.op("V", lambda v: v.tensor_reduce(
        m1.rearrange("p (a b) -> p a b", b=1), lg3, AX.X, ALU.max),
        deps=[t_idb], ms=True)
    r1 = sch.op("V", lambda v: v.tensor_tensor(
        eq1.ap()[:], lg3, m1.rearrange("p (a b) -> p a b", b=1)
        .broadcast_to([128, TT, NEXP]), ALU.is_equal), sdeps=[r0], ms=True)
    r2 = sch.op("V", lambda v: v.scalar_tensor_tensor(
        msk.ap()[:], eq1.ap()[:], -1e30, lg3, ALU.mult, ALU.add),
        sdeps=[r1], ms=True)
    r3 = sch.op("V", lambda v: v.tensor_reduce(
        m2.rearrange("p (a b) -> p a b", b=1), msk.ap()[:], AX.X, ALU.max),
        sdeps=[r2], ms=True)
    r4 = sch.op("V", lambda v: v.tensor_sub(d12, m1, m2), sdeps=[r3], ms=True)
    r5 = sch.op("S", lambda s: s.activation(w1, d12, AF.Sigmoid),
                deps=[r4], ms=True)
    r6 = sch.op("V", lambda v: v.tensor_scalar(
        w2, w1, -1.0, 1.0, ALU.mult, ALU.add), deps=[r5], ms=True)
    r7 = sch.op("V", lambda v: v.tensor_tensor(
        eq2.ap()[:], lg3, m2.rearrange("p (a b) -> p a b", b=1)
        .broadcast_to([128, TT, NEXP]), ALU.is_equal), sdeps=[r3], ms=True)
    r8 = sch.op("V", lambda v: v.tensor_scalar(
        msk.ap()[:], eq1.ap()[:], -1.0, 1.0, ALU.mult, ALU.add),
        sdeps=[r2], ms=True)
    r9 = sch.op("V", lambda v: v.tensor_mul(
        eq2.ap()[:], eq2.ap()[:], msk.ap()[:]), sdeps=[r7, r8], ms=True)
    rA = sch.op("V", lambda v: v.tensor_tensor(
        eq1.ap()[:], eq1.ap()[:], w1.rearrange("p (a b) -> p a b", b=1)
        .broadcast_to([128, TT, NEXP]), ALU.mult), sdeps=[r6, r9], ms=True)
    rB = sch.op("V", lambda v: v.tensor_tensor(
        eq2.ap()[:], eq2.ap()[:], w2.rearrange("p (a b) -> p a b", b=1)
        .broadcast_to([128, TT, NEXP]), ALU.mult), sdeps=[rA], ms=True)
    rC = sch.op("V", lambda v: v.tensor_add(
        wte.ap()[:], eq1.ap()[:], eq2.ap()[:]), sdeps=[rB], ms=True)
    wtr_ev = []
    for i in range(TT):
        pb = i % 2
        t_tr = sch.op("T", lambda t, i=i, pb=pb: t.transpose(
            ctx.bank(pb, [8, 128], BF16), wte.ap()[:, i, :], idb.ap()[:]),
            deps=[rC] + list(ctx.ps_tok[pb]), ms=True)
        tok = sch.op("S", lambda s, i=i, pb=pb: s.copy(
            wteT.ap()[:, i, :], ctx.bank(pb, [8, 128], BF16)),
            deps=[t_tr], ms=True)
        ctx.ps_tok[pb] = (tok,)
        wtr_ev.append(tok)
    t_wtr = sch.dma("P", lambda q: q.dma_start(
        out=wte_r[:], in_=wteT.ap().rearrange("p a b -> p (a b)")),
        deps=[wtr_ev[-1]])
    rt_done = [rC, wtr_ev[-1], t_wtr]
    for t_ in [lg, rt, eq1, eq2, msk, wte, wteT, idb]:
        sb.free_(t_)

    t_e = sch.op("S", lambda s: s.activation(
        asb.ap().rearrange("p a b -> p (a b)"),
        alog.ap().rearrange("p a b -> p (a b)"), AF.Exp),
        deps=[t_all], ms=True)
    t_A = sch.op("V", lambda v: v.tensor_scalar(
        asb.ap().rearrange("p a b -> p (a b)"),
        asb.ap().rearrange("p a b -> p (a b)"), -1.0, None, ALU.mult),
        deps=[t_e], ms=True)
    t_hl0 = sch.op("V", lambda v: v.memset(hlast.ap()[:], 0.0), ms=True)
    ld_tok = {}
    ld_tok[0] = [
        sch.dma("P", lambda qq: qq.dma_start(
            out=dtq.ap()[:], in_=dt_d[:, :, 0:TQ])),
        sch.dma("P", lambda qq: qq.dma_start(
            out=uq.ap()[:], in_=u_d[:, :, 0:TQ]), fence=False)]

    wg = sb.alloc("wg", [128, C, F_FF], BF16)
    wu = sb.alloc("wu", [128, C, F_FF], BF16)
    wd = sb.alloc("wd", [128, FC, D], BF16)
    stg = [sb.alloc("stg", [128, 1408], F32) for _ in range(2)]
    stg_use = [None, None]
    free_tok = {"g": {}, "u": {}, "d": {}}
    load_tok = {"g": {}, "u": {}, "d": {}}
    first_load = [True]
    GU_FC = {hh: (176 * (hh + 1) - 1) // 128 for hh in range(8)}
    FC_HH = {fc: (128 * (fc + 1) - 1) // 176 for fc in range(FC)}

    SRCS = {"eg": eg_d, "eu": eu_d, "ed": ed_d,
            "sg": sg_d, "su": su_d, "sd": sd_d}

    def load_expert(kind, e):
        plan = {"g": (("sg" if kind == "sh" else "eg"), wg, C, F_FF // 8),
                "u": (("su" if kind == "sh" else "eu"), wu, C, F_FF // 8),
                "d": (("sd" if kind == "sh" else "ed"), wd, FC, D // 8)}
        pieces = []
        for hh in range(8):
            for key in ("g", "u"):
                src, dst, nch, qw = plan[key]
                pieces.append((src, dst, nch, qw, hh,
                               free_tok[key].get(GU_FC[hh])))
        for hh in range(8):
            src, dst, nch, qw = plan["d"]
            pieces.append((src, dst, nch, qw, hh, free_tok["d"].get(hh)))
        for gi in range(0, len(pieces), 2):
            pair = pieces[gi:gi + 2]
            toks = []
            for j, (src, dst, nch, qw, hh, dep) in enumerate(pair):
                a = (gi + j) % 2
                dd = [dep] if dep else []
                if first_load[0]:
                    dd += rt_done
                    first_load[0] = False
                if stg_use[a]:
                    dd += [stg_use[a]]
                toks.append(sch.dma(
                    "Q", lambda q, src=src, e=e, nch=nch, qw=qw, hh=hh, a=a:
                    q.dma_start(
                        out=stg[a].ap().rearrange("p (a b) -> p a b", a=nch),
                        in_=SRCS[src][e, :, :, qw * hh:qw * (hh + 1)]),
                    deps=dd, fence=(j == 0)))
            grp = toks[-1]
            for j, (src, dst, nch, qw, hh, dep) in enumerate(pair):
                a = (gi + j) % 2
                tok = sch.op("V", lambda v, dst=dst, nch=nch, qw=qw, hh=hh,
                             a=a: v.tensor_copy(
                                 dst.ap()[:, :, qw * hh:qw * (hh + 1)],
                                 stg[a].ap().rearrange("p (a b) -> p a b",
                                                       a=nch)),
                             deps=[grp], ms=True)
                stg_use[a] = tok
                key = "g" if dst is wg else ("u" if dst is wu else "d")
                load_tok[key][hh] = tok

    load_expert("sh", 0)

    t_scan, t_hl, t_z, t_y = {}, {}, {}, {}
    bc_tok = {}
    last_tb = {}
    last_tz = {}
    ystore = {}
    ta_last = {}

    def emit_bcast(g):
        q_, n = divmod(g, 16)
        slot = g % 2
        depB = [last_tb[g - 2]] if g - 2 in last_tb else []
        depC = [last_tz[g - 2]] if g - 2 in last_tz else []
        tB = sch.dma("P", lambda qq, q_=q_, n=n, slot=slot: qq.dma_start(
            out=Bb[slot].ap()[:],
            in_=bcbf_d[n:n + 1, TQ * q_:TQ * (q_ + 1)]
            .broadcast_to([128, TQ])), deps=depB)
        tC = sch.dma("P", lambda qq, q_=q_, n=n, slot=slot: qq.dma_start(
            out=Cb[slot].ap()[:],
            in_=bcbf_d[NST + n:NST + n + 1, TQ * q_:TQ * (q_ + 1)]
            .broadcast_to([128, TQ])), deps=depC)
        bc_tok[g] = (tB, tC)

    def emit_granule(g, clo=0, chi=C):
        q_, n = divmod(g, 16)
        slot = g % 2
        for c in range(clo, chi):
            i = g * C + c
            pi = i % 2
            col = hlast.ap()[:, n * C + c:n * C + c + 1]
            dd_a = [t_A] + ([t_scan[i - 3]] if i >= 3 else [])
            t_a = sch.op("S", lambda s, c=c, n=n, i=i: s.activation(
                ab[i % 3].ap()[:], dtq.ap()[:, c, :], AF.Exp,
                scale=asb.ap()[:, c, n:n + 1]),
                deps=dd_a + ld_tok[q_], ms=True)
            ta_last[q_] = t_a
            t_b = sch.op("V", lambda v, c=c, pi=pi, slot=slot: v.tensor_mul(
                bb[pi].ap()[:], uq.ap()[:, c, :], Bb[slot].ap()[:]),
                deps=[bc_tok[g][0]] + ld_tok[q_], ms=True)
            last_tb[g] = t_b
            dd_s = [t_a, t_hl0]
            if q_ > 0:
                dd_s.append(t_hl[(q_ - 1) * NST * C + n * C + c])
            if i >= 2:
                dd_s.append(t_hl[i - 2])
            t_scan[i] = sch.op("V", lambda v, pi=pi, i=i, col=col:
                               v.tensor_tensor_scan(
                                   hb[pi].ap()[:], ab[i % 3].ap()[:],
                                   bb[pi].ap()[:], col,
                                   ALU.mult, ALU.add), deps=dd_s, ms=True)
            t_hl[i] = sch.op("S", lambda s, pi=pi, col=col: s.copy(
                col, hb[pi].ap()[:, TQ - 1:TQ]), deps=[t_scan[i]], ms=True)
            dd_z = [bc_tok[g][1]]
            t_z[i] = sch.op("V", lambda v, pi=pi, slot=slot: v.tensor_mul(
                zb[0].ap()[:], hb[pi].ap()[:], Cb[slot].ap()[:]),
                deps=dd_z, ms=True)
            last_tz[g] = t_z[i]
            if n == 0:
                ddy = []
                if q_ - 1 in ystore:
                    ddy.append(ystore[q_ - 1])
                t_y[i] = sch.op("V", lambda v, c=c: v.tensor_copy(
                    y_acc.ap()[:, c, :], zb[0].ap()[:]), deps=ddy, ms=True)
            else:
                t_y[i] = sch.op("V", lambda v, c=c: v.tensor_add(
                    y_acc.ap()[:, c, :], y_acc.ap()[:, c, :],
                    zb[0].ap()[:]), ms=True)
        if n == NST - 1 and chi == C:
            ystore[q_] = sch.dma("P", lambda qq, q_=q_: qq.dma_start(
                out=y_dr[:, :, TQ * q_:TQ * (q_ + 1)], in_=y_acc.ap()[:]),
                deps=[t_y[g * C + c] for c in range(C)])
            if q_ + 1 < NQ:
                dd = [last_tb[g], ta_last[q_]]
                ld_tok[q_ + 1] = [
                    sch.dma("P", lambda qq, q_=q_: qq.dma_start(
                        out=dtq.ap()[:],
                        in_=dt_d[:, :, TQ * (q_ + 1):TQ * (q_ + 2)]),
                        deps=dd),
                    sch.dma("P", lambda qq, q_=q_: qq.dma_start(
                        out=uq.ap()[:],
                        in_=u_d[:, :, TQ * (q_ + 1):TQ * (q_ + 2)]),
                        fence=False)]

    NG = NQ * NST
    NHG = 2 * NG
    emitted_b = [0]
    emitted_hg = [0]

    def pump_scan(target):
        target = min(target, NHG)
        while emitted_hg[0] < target:
            hg = emitted_hg[0]
            g, part = hg // 2, hg % 2
            if part == 0:
                while emitted_b[0] < min(g + 2, NG):
                    emit_bcast(emitted_b[0])
                    emitted_b[0] += 1
                emit_granule(g, 0, C // 2)
            else:
                emit_granule(g, C // 2, C)
            emitted_hg[0] += 1

    experts = [("sh", 0), ("sh", 1)] + [("exp", e) for e in range(NEXP)]
    prev_exp_d = None
    h_tok = [None] * 4
    acc_tok = {}
    wteb_rd = [None] * 4
    wteb_tok = [None] * 4
    unit = 0
    pstep = [0]
    NPUMP = 72

    def pump_tick():
        pstep[0] += 1
        pump_scan((pstep[0] * NHG + NPUMP - 1) // NPUMP)

    for ei, (kind, e) in enumerate(experts):
        routed = kind == "exp"
        wg_tok, wu_tok, wd_tok = (dict(load_tok["g"]), dict(load_tok["u"]),
                                  dict(load_tok["d"]))
        def emit_wteb(tt):
            dd = [t_wtr] + ([wteb_rd[0]] if wteb_rd[0] else [])
            wteb_tok[tt] = sch.dma(
                "P", lambda qq, e=e, tt=tt: qq.dma_start(
                    out=wteb.ap()[:],
                    in_=wte_r[e:e + 1, 512 * tt:512 * (tt + 1)]
                    .broadcast_to([128, 512])), deps=dd)
        if routed:
            emit_wteb(0)
        g_last = u_last = d_last = None
        g_fc, u_fc, d_mo = {}, {}, {}
        prev_half_d = prev_exp_d
        for half in range(4):
            tt = half
            for fc in range(FC):
                bg = (fc + tt) % 2
                bu = 2 + (fc + tt) % 2
                t_g = t_u = None
                for kc in range(C):
                    dd = ([wg_tok[FC_HH[fc]], t_xn] if kc == 0 else []) + \
                        (list(ctx.ps_tok[bg]) if kc == 0 else [])
                    t_g = sch.op("T", lambda t, fc=fc, tt=tt, kc=kc,
                                 bg=bg: t.matmul(
                        ctx.bank(bg), wg.ap()[:, kc, 128 * fc:128 * (fc + 1)],
                        xn.ap()[:, kc, 512 * tt:512 * (tt + 1)],
                        start=(kc == 0), stop=(kc == C - 1)),
                        deps=dd, ms=(kc == C - 1))
                    dd = ([wu_tok[FC_HH[fc]]] if kc == 0 else []) + \
                        (list(ctx.ps_tok[bu]) if kc == 0 else [])
                    t_u = sch.op("T", lambda t, fc=fc, tt=tt, kc=kc,
                                 bu=bu: t.matmul(
                        ctx.bank(bu), wu.ap()[:, kc, 128 * fc:128 * (fc + 1)],
                        xn.ap()[:, kc, 512 * tt:512 * (tt + 1)],
                        start=(kc == 0), stop=(kc == C - 1)),
                        deps=dd, ms=(kc == C - 1))
                g_last, u_last = t_g, t_u
                if tt == 3:
                    g_fc[fc] = t_g
                    u_fc[fc] = t_u
                a = (fc + tt) % 4
                t_sil = sch.op("S", lambda s, a=a, bg=bg: s.activation(
                    gsil[a].ap()[:], ctx.bank(bg), AF.Silu),
                    deps=[t_g] + ([h_tok[a]] if h_tok[a] else []), ms=True)
                if routed:
                    t_s2 = sch.op("V", lambda v, a=a: v.tensor_mul(
                        gsil[a].ap()[:], gsil[a].ap()[:], wteb.ap()[:]),
                        deps=[t_sil, wteb_tok[tt]], ms=True)
                    wteb_rd[0] = t_s2
                    hsrc = gsil[a]
                    hdep = [t_s2, t_u]
                else:
                    hsrc = gsil[a]
                    hdep = [t_sil, t_u]
                t_h = sch.op("V", lambda v, fc=fc, a=a, bu=bu, hsrc=hsrc:
                             v.tensor_mul(
                    hsb.ap()[:, fc, :], hsrc.ap()[:], ctx.bank(bu)),
                    deps=hdep + ([prev_half_d] if prev_half_d else []),
                    ms=True)
                h_tok[a] = t_h
                ctx.ps_tok[bg] = (t_sil,)
                ctx.ps_tok[bu] = (t_h,)
            if routed and half < 3:
                emit_wteb(half + 1)
            pump_tick()
            prev_half_d = None
            for mo in range(C):
                bd = 4 + (mo + tt) % 2
                t_d = None
                for fc in range(FC):
                    dd = ([wd_tok[mo], t_h] if fc == 0 else []) + \
                        (list(ctx.ps_tok[bd]) if fc == 0 else [])
                    t_d = sch.op("T", lambda t, mo=mo, fc=fc, bd=bd: t.matmul(
                        ctx.bank(bd), wd.ap()[:, fc, 128 * mo:128 * (mo + 1)],
                        hsb.ap()[:, fc, :],
                        start=(fc == 0), stop=(fc == FC - 1)),
                        deps=dd, ms=(fc == FC - 1))
                d_last = t_d
                if tt == 3:
                    d_mo[mo] = t_d
                prev_half_d = t_d
                a = (mo + tt) % 2
                prev = acc_tok.get((mo, tt))
                if ei == 0:
                    t_acc = sch.op("S", lambda s, mo=mo, tt=tt, bd=bd: s.copy(
                        acc.ap()[:, mo, 512 * tt:512 * (tt + 1)],
                        ctx.bank(bd)), deps=[t_d], ms=True)
                    ctx.ps_tok[bd] = (t_acc,)
                else:
                    t_cp = sch.op("S", lambda s, a=a, bd=bd: s.copy(
                        dtmp[a].ap()[:], ctx.bank(bd)),
                        deps=[t_d] + ([acc_tok[("tmp", a)]]
                                      if ("tmp", a) in acc_tok else []),
                        ms=True)
                    t_acc = sch.op("P", lambda p, mo=mo, tt=tt, a=a:
                                   p.tensor_add(
                        acc.ap()[:, mo, 512 * tt:512 * (tt + 1)],
                        acc.ap()[:, mo, 512 * tt:512 * (tt + 1)],
                        dtmp[a].ap()[:]),
                        deps=[t_cp] + ([prev] if prev else []), ms=True)
                    acc_tok[("tmp", a)] = t_acc
                    ctx.ps_tok[bd] = (t_cp,)
                acc_tok[(mo, tt)] = t_acc
            unit += 1
            if half < 3:
                pump_tick()
        prev_exp_d = d_last
        if ei + 1 < len(experts):
            nkind, ne = experts[ei + 1]
            free_tok["g"] = g_fc
            free_tok["u"] = u_fc
            free_tok["d"] = d_mo
            load_expert(nkind, ne)
        pump_tick()
    pump_scan(NHG)
    sch.dma("Q", lambda q: q.dma_start(
        out=moe_d[:], in_=acc.ap().rearrange("p a b -> p (a b)")),
        deps=[tok for k, tok in acc_tok.items() if not isinstance(k, tuple)
              or k[0] != "tmp"])
    sch.barrier()
    for t_ in ([xn, acc, hsb, wteb, dtq, uq, y_acc, alog, hlast]
               + gsil + dtmp + ab + bb + hb + zb + Bb + Cb
               + [wg, wu, wd] + stg):
        sb.free_(t_)


def phase_scan_old(ctx):
    nc, sch, sb = ctx.nc, ctx.sch, ctx.sb
    TQ = 512
    NQ = T // TQ
    dt_d = ctx.dram_t("dt_d", [128, C * T], BF16).ap().rearrange(
        "p (c t) -> p c t", c=C)
    u_d = ctx.dram_t("u_d", [128, C * T], BF16).ap().rearrange(
        "p (c t) -> p c t", c=C)
    bc_d = ctx.dram_t("bc_d", [32, T], F32).ap()
    al_d = ctx.dram_t("A_log", [128, C, NST], F32).ap()
    sel_d = ctx.dram_t("sel32", [32, 32 * 128], BF16).ap()
    y_dr = ctx.dram_t("y_d", [128, C * T], BF16).ap().rearrange(
        "p (c t) -> p c t", c=C)

    bcb = sb.alloc("bcb", [32, T], BF16)
    alog = sb.alloc("alog", [128, C, NST], F32)
    asb = sb.alloc("asb", [128, C, NST], F32)
    sel = sb.alloc("sel", [32, 32, 128], BF16)
    hlast = sb.alloc("hlast", [128, NST * C], F32)
    y_acc = sb.alloc("y_acc", [128, C, TQ], F32)
    ytmp = [sb.alloc("ytmp", [128, C, TQ], BF16) for _ in range(2)]
    Bball = sb.alloc("Bball", [128, NST, TQ], BF16)
    Cball = sb.alloc("Cball", [128, NST, TQ], BF16)
    dtq = [sb.alloc("dtq", [128, C, TQ], BF16) for _ in range(2)]
    uq = [sb.alloc("uq", [128, C, TQ], BF16) for _ in range(2)]
    ab = [sb.alloc("a", [128, TQ], F32) for _ in range(2)]
    bb = [sb.alloc("b", [128, TQ], BF16) for _ in range(2)]
    hb = [sb.alloc("h", [128, TQ], BF16) for _ in range(2)]
    zb = [sb.alloc("z", [128, TQ], BF16)]

    bct = sb.alloc("bct", [32, T], F32)
    for t_, dd in [(bct, bc_d), (alog, al_d)]:
        sch.dma("Q", lambda q, t_=t_, dd=dd: q.dma_start(
            out=t_.ap()[:], in_=dd[:]))
    sch.dma("Q", lambda q: q.dma_start(
        out=sel.ap().rearrange("p a b -> p (a b)"), in_=sel_d[:]))
    sch.barrier()
    t_e = sch.op("S", lambda s: s.activation(
        asb.ap().rearrange("p a b -> p (a b)"),
        alog.ap().rearrange("p a b -> p (a b)"), AF.Exp), ms=True)
    t_A = sch.op("V", lambda v: v.tensor_scalar(
        asb.ap().rearrange("p a b -> p (a b)"),
        asb.ap().rearrange("p a b -> p (a b)"), -1.0, None, ALU.mult),
        deps=[t_e], ms=True)
    t_bcb = sch.op("V", lambda v: v.tensor_copy(bcb.ap()[:], bct.ap()[:]),
                   ms=True)
    t_hl0 = sch.op("V", lambda v: v.memset(hlast.ap()[:], 0.0), ms=True)
    sch.barrier()
    sb.free_(bct)

    t_scan = {}
    t_z = {}
    t_y = {}
    t_hl = {}
    ld_tok = {}
    bball_read = None
    cball_read = None
    ytmp_read = [None, None]
    prev_st = {}

    def load_q(q):
        qb = q % 2
        dd = []
        if q >= 2:
            dd = [t_scan.get((q - 2) * NST * C + NST * C - 1)]
            dd = [x for x in dd if x]
        ld_tok[q] = [
            sch.dma("Q", lambda qq, q=q, qb=qb: qq.dma_start(
                out=dtq[qb].ap()[:], in_=dt_d[:, :, TQ * q:TQ * (q + 1)]),
                deps=dd),
            sch.dma("Q", lambda qq, q=q, qb=qb: qq.dma_start(
                out=uq[qb].ap()[:], in_=u_d[:, :, TQ * q:TQ * (q + 1)]),
                deps=dd)]

    load_q(0)
    for q in range(NQ):
        qb = q % 2
        if q + 1 < NQ:
            load_q(q + 1)
        evs = []
        for n in range(NST):
            for (row, dst) in ((n, Bball), (NST + n, Cball)):
                pb = len(evs) % 2
                dd = [t_bcb] + list(ctx.ps_tok[pb])
                if q > 0:
                    dd += [bball_read, cball_read]
                t_mm = sch.op("T", lambda t, row=row, q=q, pb=pb: t.matmul(
                    ctx.bank(pb), sel.ap()[:, row, :],
                    bcb.ap()[0:32, TQ * q:TQ * (q + 1)],
                    start=True, stop=True), deps=dd, ms=True)
                tok = sch.op("S", lambda s, dst=dst, n=n, pb=pb: s.copy(
                    dst.ap()[:, n, :], ctx.bank(pb)), deps=[t_mm], ms=True)
                ctx.ps_tok[pb] = (tok,)
                evs.append(tok)
        ev_all = evs[-1]
        for n in range(NST):
            for c in range(C):
                i = (q * NST + n * 0 + 0) * 0 + q * NST * C + n * C + c
                pi = i % 2
                col = hlast.ap()[:, n * C + c:n * C + c + 1]
                dd_a = [t_A] + ([t_scan[i - 2]] if i >= 2 else [])
                t_a = sch.op("S", lambda s, c=c, n=n, pi=pi, qb=qb:
                             s.activation(
                                 ab[pi].ap()[:], dtq[qb].ap()[:, c, :], AF.Exp,
                                 scale=asb.ap()[:, c, n:n + 1]),
                             deps=dd_a + ld_tok[q], ms=True)
                t_b = sch.op("V", lambda v, c=c, n=n, pi=pi, qb=qb:
                             v.tensor_mul(
                                 bb[pi].ap()[:], uq[qb].ap()[:, c, :],
                                 Bball.ap()[:, n, :]),
                             deps=[ev_all] + ld_tok[q], ms=True)
                dd_s = [t_a, t_hl0]
                if q > 0:
                    dd_s.append(t_hl[(q - 1) * NST * C + n * C + c])
                if i >= 2:
                    dd_s.append(t_hl[i - 2])
                t_scan[i] = sch.op("V", lambda v, pi=pi, col=col:
                                   v.tensor_tensor_scan(
                                       hb[pi].ap()[:], ab[pi].ap()[:],
                                       bb[pi].ap()[:], col,
                                       ALU.mult, ALU.add), deps=dd_s, ms=True)
                t_hl[i] = sch.op("S", lambda s, pi=pi, col=col: s.copy(
                    col, hb[pi].ap()[:, TQ - 1:TQ]), deps=[t_scan[i]], ms=True)
                dd_z = [ev_all] + ([t_y[i - 2]] if i >= 2 else [])
                t_z[i] = sch.op("V", lambda v, n=n, pi=pi: v.tensor_mul(
                    zb[pi].ap()[:], hb[pi].ap()[:], Cball.ap()[:, n, :]),
                    deps=dd_z, ms=True)
                if n == 0:
                    ddy = [t_z[i]]
                    if q > 0:
                        ddy.append(prev_st[c])
                    t_y[i] = sch.op("P", lambda p, c=c, pi=pi: p.tensor_copy(
                        y_acc.ap()[:, c, :], zb[pi].ap()[:]),
                        deps=ddy, ms=True)
                else:
                    t_y[i] = sch.op("P", lambda p, c=c, pi=pi: p.tensor_add(
                        y_acc.ap()[:, c, :], y_acc.ap()[:, c, :],
                        zb[pi].ap()[:]), deps=[t_z[i]], ms=True)
            bball_read = t_b
            cball_read = t_z[i]
        st_dep = []
        for c in range(C):
            dd = [t_y[q * NST * C + (NST - 1) * C + c]]
            if ytmp_read[qb]:
                dd.append(ytmp_read[qb])
            st_dep.append(sch.op("V", lambda v, c=c, qb=qb: v.tensor_copy(
                ytmp[qb].ap()[:, c, :], y_acc.ap()[:, c, :]), deps=dd,
                ms=True))
        ytmp_read[qb] = sch.dma("Q", lambda qq, q=q, qb=qb: qq.dma_start(
            out=y_dr[:, :, TQ * q:TQ * (q + 1)], in_=ytmp[qb].ap()[:]),
            deps=st_dep)
        prev_st = {c: st_dep[c] for c in range(C)}
    sch.barrier()
    for t_ in ([bcb, alog, asb, sel, hlast, y_acc, Bball, Cball]
               + ytmp + dtq + uq + ab + bb + hb + zb):
        sb.free_(t_)


def phase_mout(ctx):
    nc, sch, sb = ctx.nc, ctx.sch, ctx.sb
    y_d = ctx.dram_t("y32_d", [128, C * T], F32).ap().rearrange(
        "p (c t) -> p c t", c=C)
    gg_d = ctx.dram_t("ggate_d", [128, C * T], BF16).ap()
    xc_d = ctx.dram_t("xconv_d", [128, C * T], BF16).ap()
    xn_d = ctx.dram_t("xn_bf", [128, C * T], BF16).ap()
    dv_d = ctx.dram_t("Dvec", [128, C], F32).ap()
    s1_d = ctx.dram_t("s1_d", [128, C * T], BF16).ap()
    bo_d = ctx.dram_t("b_out", [128, C], F32).ap()
    lg_d = ctx.dram_t("ln_g", [128, C], F32).ap()
    lb_d = ctx.dram_t("ln_b", [128, C], F32).ap()

    gg = sb.alloc("gg", [128, C, T], BF16)
    xc = sb.alloc("xc", [128, C, T], BF16)
    xn = sb.alloc("xn", [128, C, T], BF16)
    small = sb.alloc("small", [128, 4 * C], F32)
    ld = [sch.dma("Q", lambda q, t_=t_, dd=dd: q.dma_start(
        out=t_.ap()[:], in_=dd[:]))
        for t_, dd in [(gg, gg_d), (xc, xc_d), (xn, xn_d)]]
    for i, dd in enumerate([dv_d, bo_d, lg_d, lb_d]):
        ld.append(sch.dma("Q", lambda q, i=i, dd=dd: q.dma_start(
            out=small.ap()[:, i * C:(i + 1) * C], in_=dd[:])))
    dvc = lambda c: small.ap()[:, 0 * C + c:0 * C + c + 1]
    boc = lambda c: small.ap()[:, 1 * C + c:1 * C + c + 1]
    lgc = lambda c: small.ap()[:, 2 * C + c:2 * C + c + 1]
    lbc = lambda c: small.ap()[:, 3 * C + c:3 * C + c + 1]
    ygate = sb.alloc("ygate", [128, C, T], BF16)
    yf = [sb.alloc("yf", [128, T], F32) for _ in range(2)]
    sch.barrier()

    yf_use = [None, None]
    for c in range(C):
        a = c % 2
        ldy = sch.dma("Q", lambda q, c=c, a=a: q.dma_start(
            out=yf[a].ap()[:], in_=y_d[:, c, :]),
            deps=[yf_use[a]] if yf_use[a] else [])
        t1 = sch.op("V", lambda v, c=c, a=a: v.scalar_tensor_tensor(
            yf[a].ap()[:], xc.ap()[:, c, :], dvc(c), yf[a].ap()[:],
            ALU.mult, ALU.add), deps=[ldy], ms=True)
        yf_use[a] = sch.op("V", lambda v, c=c, a=a: v.tensor_mul(
            ygate.ap()[:, c, :], yf[a].ap()[:], gg.ap()[:, c, :]),
            sdeps=[t1], ms=True)
    sch.barrier()
    for t_ in [gg, xc] + yf:
        sb.free_(t_)

    wout, wtok, _ = _load_cast_w(ctx, "w_out", C, 1024, "wout")
    res = sb.alloc("res", [128, C, T], BF16)
    for mo in range(C):
        for tt in range(4):
            b = (mo * 4 + tt) % 4
            t_mm = None
            for kc in range(C):
                dd = ([wtok] if kc == 0 else []) + \
                    (list(ctx.ps_tok[b]) if kc == 0 else [])
                t_mm = sch.op("T", lambda t, mo=mo, tt=tt, kc=kc, b=b: t.matmul(
                    ctx.bank(b), wout.ap()[:, kc, 128 * mo:128 * (mo + 1)],
                    ygate.ap()[:, kc, 512 * tt:512 * (tt + 1)],
                    start=(kc == 0), stop=(kc == C - 1)),
                    deps=dd, ms=(kc == C - 1))
            tok = sch.op("V", lambda v, mo=mo, tt=tt, b=b: v.scalar_tensor_tensor(
                res.ap()[:, mo, 512 * tt:512 * (tt + 1)], ctx.bank(b), boc(mo),
                xn.ap()[:, mo, 512 * tt:512 * (tt + 1)], ALU.add, ALU.add),
                deps=[t_mm], ms=True)
            ctx.ps_tok[b] = (tok,)
    sch.barrier()
    sb.free_(ygate)
    sb.free_(wout)

    ones_bf = sb.alloc("ones_bf", [128, 1], BF16)
    ones_f = sb.alloc("ones_f", [1, 128], F32)
    rows = sb.alloc("rows", [1, 8 * T], F32)
    sq = [sb.alloc("sq", [128, 512], BF16) for _ in range(2)]
    t_o1 = sch.op("V", lambda v: v.memset(ones_bf.ap()[:], 1.0), ms=True)
    t_o2 = sch.op("V", lambda v: v.memset(ones_f.ap()[:], 1.0), ms=True)
    row = lambda r, tt=None: (rows.ap()[:, r * T:(r + 1) * T] if tt is None
                              else rows.ap()[:, r * T + 512 * tt:
                                             r * T + 512 * (tt + 1)])
    for tt in range(4):
        bs_, bq = 4 + (tt % 2) * 2, 5 + (tt % 2) * 2
        t_mm = t_mq = None
        mqs = {}
        for c in range(C):
            a = c % 2
            dd = [list(ctx.ps_tok[bs_]) if c == 0 else []][0]
            t_mm = sch.op("T", lambda t, c=c, tt=tt, bs_=bs_: t.matmul(
                ctx.bank(bs_, [1, 512]), ones_bf.ap()[:],
                res.ap()[:, c, 512 * tt:512 * (tt + 1)],
                start=(c == 0), stop=(c == C - 1)),
                deps=[t_o1] + dd, ms=(c == C - 1))
            t_sq = sch.op("S", lambda s, c=c, tt=tt, a=a: s.activation(
                sq[a].ap()[:], res.ap()[:, c, 512 * tt:512 * (tt + 1)],
                AF.Square), deps=[mqs[c - 2]] if c >= 2 else [], ms=True)
            dd = [list(ctx.ps_tok[bq]) if c == 0 else []][0]
            t_mq = sch.op("T", lambda t, c=c, tt=tt, bq=bq, a=a: t.matmul(
                ctx.bank(bq, [1, 512]), ones_bf.ap()[:], sq[a].ap()[:],
                start=(c == 0), stop=(c == C - 1)),
                deps=[t_sq] + dd, ms=True)
            mqs[c] = t_mq
        e1 = sch.op("S", lambda s, tt=tt, bs_=bs_: s.copy(
            row(0, tt), ctx.bank(bs_, [1, 512])), deps=[t_mm], ms=True)
        e2 = sch.op("S", lambda s, tt=tt, bq=bq: s.copy(
            row(1, tt), ctx.bank(bq, [1, 512])), deps=[t_mq], ms=True)
        ctx.ps_tok[bs_] = (e1,)
        ctx.ps_tok[bq] = (e2,)
    sch.barrier()
    t1 = sch.op("V", lambda v: v.tensor_scalar(
        row(2), row(0), 1.0 / D, None, ALU.mult), ms=True)
    t2 = sch.op("V", lambda v: v.tensor_mul(row(3), row(2), row(2)),
                sdeps=[t1], ms=True)
    t3 = sch.op("V", lambda v: v.tensor_scalar(
        row(4), row(1), 1.0 / D, None, ALU.mult), sdeps=[t2], ms=True)
    t4 = sch.op("V", lambda v: v.tensor_sub(row(4), row(4), row(3)),
                sdeps=[t3], ms=True)
    t5 = sch.op("V", lambda v: v.tensor_scalar(
        row(4), row(4), float(LN_EPS), None, ALU.add), sdeps=[t4], ms=True)
    t6 = sch.op("S", lambda s: s.activation(row(5), row(4), AF.Ln),
                deps=[t5], ms=True)
    t7 = sch.op("V", lambda v: v.tensor_scalar(
        row(5), row(5), -0.5, None, ALU.mult), deps=[t6], ms=True)
    t8 = sch.op("S", lambda s: s.activation(row(6), row(5), AF.Exp),
                deps=[t7], ms=True)
    s1 = sb.alloc("s1", [128, C, T], BF16)
    tm1 = [sb.alloc("tm1", [128, 512], F32) for _ in range(2)]
    for tt in range(4):
        bm, br = (tt % 2) * 2, (tt % 2) * 2 + 1
        dd = [t8, t_o2] + list(ctx.ps_tok[bm]) + list(ctx.ps_tok[br])
        t_bm = sch.op("T", lambda t, tt=tt, bm=bm: t.matmul(
            ctx.bank(bm), ones_f.ap()[:], row(2, tt), start=True, stop=True),
            deps=dd, ms=True)
        t_br = sch.op("T", lambda t, tt=tt, br=br: t.matmul(
            ctx.bank(br), ones_f.ap()[:], row(6, tt), start=True, stop=True),
            ms=True)
        apl = []
        for c in range(C):
            a = c % 2
            w1 = sch.op("V", lambda v, c=c, tt=tt, bm=bm, a=a: v.tensor_sub(
                tm1[a].ap()[:], res.ap()[:, c, 512 * tt:512 * (tt + 1)],
                ctx.bank(bm)), deps=[t_bm, t_br], ms=True)
            w2 = sch.op("V", lambda v, a=a, br=br: v.tensor_mul(
                tm1[a].ap()[:], tm1[a].ap()[:], ctx.bank(br)), ms=True)
            w3 = sch.op("V", lambda v, c=c, tt=tt, a=a: v.tensor_scalar(
                s1.ap()[:, c, 512 * tt:512 * (tt + 1)], tm1[a].ap()[:],
                lgc(c), lbc(c), ALU.mult, ALU.add), ms=True)
            apl.append(w3)
        ctx.ps_tok[bm] = (apl[-1],)
        ctx.ps_tok[br] = (apl[-1],)
    sch.dma("Q", lambda q: q.dma_start(out=s1_d[:], in_=s1.ap()[:]),
            deps=[("V", sch.ms["V"])])
    sch.barrier()
    for t_ in [xn, small, res, ones_bf, ones_f, rows, s1] + sq + tm1:
        sb.free_(t_)


def phase_moe(ctx):
    nc, sch, sb = ctx.nc, ctx.sch, ctx.sb
    xn_d = ctx.dram_t("xn_bf", [128, C * T], BF16).ap()
    lg_d = ctx.dram_t("logits_tm", [128, TT * NEXP], F32).ap()
    moe_d = ctx.dram_t("moe_d", [128, C * T], F32).ap()
    eg_d = ctx.dram_t("exp_gate", [NEXP, 128, C, F_FF], F32).ap()
    eu_d = ctx.dram_t("exp_up", [NEXP, 128, C, F_FF], F32).ap()
    ed_d = ctx.dram_t("exp_down", [NEXP, 128, FC, D], F32).ap()
    sg_d = ctx.dram_t("sh_gate", [NSH, 128, C, F_FF], F32).ap()
    su_d = ctx.dram_t("sh_up", [NSH, 128, C, F_FF], F32).ap()
    sd_d = ctx.dram_t("sh_down", [NSH, 128, FC, D], F32).ap()
    sel8_d = ctx.dram_t("sel8", [8, 8 * 128], BF16).ap()

    xn = sb.alloc("xn", [128, C, T], BF16)
    lg = sb.alloc("lg", [128, TT, NEXP], F32)
    sel8 = sb.alloc("sel8", [8, 8, 128], BF16)
    t_xn = sch.dma("Q", lambda q: q.dma_start(out=xn.ap()[:], in_=xn_d[:]))
    t_lg = sch.dma("Q", lambda q: q.dma_start(
        out=lg.ap().rearrange("p a b -> p (a b)"), in_=lg_d[:]))
    sch.dma("Q", lambda q: q.dma_start(
        out=sel8.ap().rearrange("p a b -> p (a b)"), in_=sel8_d[:]))
    sch.barrier()

    rt = sb.alloc("rt", [128, 16 * TT], F32)
    m1 = rt.ap()[:, 0:TT]
    m2 = rt.ap()[:, TT:2 * TT]
    d12 = rt.ap()[:, 2 * TT:3 * TT]
    w1 = rt.ap()[:, 3 * TT:4 * TT]
    w2 = rt.ap()[:, 4 * TT:5 * TT]
    eq1 = sb.alloc("eq1", [128, TT, NEXP], F32)
    eq2 = sb.alloc("eq2", [128, TT, NEXP], F32)
    msk = sb.alloc("msk", [128, TT, NEXP], F32)
    wte = sb.alloc("wte", [128, TT, NEXP], BF16)
    lg3 = lg.ap()
    r0 = sch.op("V", lambda v: v.tensor_reduce(
        m1.rearrange("p (a b) -> p a b", b=1), lg3, AX.X, ALU.max), ms=True)
    r1 = sch.op("V", lambda v: v.tensor_tensor(
        eq1.ap()[:], lg3, m1.rearrange("p (a b) -> p a b", b=1)
        .broadcast_to([128, TT, NEXP]), ALU.is_equal), sdeps=[r0], ms=True)
    r2 = sch.op("V", lambda v: v.scalar_tensor_tensor(
        msk.ap()[:], eq1.ap()[:], -1e30, lg3, ALU.mult, ALU.add),
        sdeps=[r1], ms=True)
    r3 = sch.op("V", lambda v: v.tensor_reduce(
        m2.rearrange("p (a b) -> p a b", b=1), msk.ap()[:], AX.X, ALU.max),
        sdeps=[r2], ms=True)
    r4 = sch.op("V", lambda v: v.tensor_sub(d12, m1, m2), sdeps=[r3], ms=True)
    r5 = sch.op("S", lambda s: s.activation(w1, d12, AF.Sigmoid),
                deps=[r4], ms=True)
    r6 = sch.op("V", lambda v: v.tensor_scalar(
        w2, w1, -1.0, 1.0, ALU.mult, ALU.add), deps=[r5], ms=True)
    r7 = sch.op("V", lambda v: v.tensor_tensor(
        eq2.ap()[:], lg3, m2.rearrange("p (a b) -> p a b", b=1)
        .broadcast_to([128, TT, NEXP]), ALU.is_equal), sdeps=[r3], ms=True)
    r8 = sch.op("V", lambda v: v.tensor_scalar(
        msk.ap()[:], eq1.ap()[:], -1.0, 1.0, ALU.mult, ALU.add),
        sdeps=[r2], ms=True)
    r9 = sch.op("V", lambda v: v.tensor_mul(
        eq2.ap()[:], eq2.ap()[:], msk.ap()[:]), sdeps=[r7, r8], ms=True)
    rA = sch.op("V", lambda v: v.tensor_tensor(
        eq1.ap()[:], eq1.ap()[:], w1.rearrange("p (a b) -> p a b", b=1)
        .broadcast_to([128, TT, NEXP]), ALU.mult), sdeps=[r6, r9], ms=True)
    rB = sch.op("V", lambda v: v.tensor_tensor(
        eq2.ap()[:], eq2.ap()[:], w2.rearrange("p (a b) -> p a b", b=1)
        .broadcast_to([128, TT, NEXP]), ALU.mult), sdeps=[rA], ms=True)
    rC = sch.op("V", lambda v: v.tensor_add(
        wte.ap()[:], eq1.ap()[:], eq2.ap()[:]), sdeps=[rB], ms=True)
    wteT = sb.alloc("wteT", [8, TT, 128], BF16)
    idb_d = ctx.dram_t("ident_bf", [128, 128], BF16).ap()
    idb = sb.alloc("idb", [128, 128], BF16)
    sch.dma("Q", lambda q: q.dma_start(out=idb.ap()[:], in_=idb_d[:]))
    sch.barrier()
    for i in range(TT):
        pb = i % 2
        t_tr = sch.op("T", lambda t, i=i, pb=pb: t.transpose(
            ctx.bank(pb, [8, 128], BF16), wte.ap()[:, i, :], idb.ap()[:]),
            deps=[rC] + list(ctx.ps_tok[pb]), ms=True)
        tok = sch.op("S", lambda s, i=i, pb=pb: s.copy(
            wteT.ap()[:, i, :], ctx.bank(pb, [8, 128], BF16)),
            deps=[t_tr], ms=True)
        ctx.ps_tok[pb] = (tok,)
    sch.barrier()
    for t_ in [rt, eq1, eq2, msk, lg]:
        sb.free_(t_)

    acc = sb.alloc("acc", [128, C, T], F32)
    wg = sb.alloc("wg", [128, C, F_FF], BF16)
    wu = sb.alloc("wu", [128, C, F_FF], BF16)
    wd = sb.alloc("wd", [128, FC, D], BF16)
    stg = [sb.alloc("stg", [128, 2 * F_FF], F32)]
    hsb = sb.alloc("hsb", [128, FC, 512], BF16)
    gsil = [sb.alloc("gsil", [128, 512], BF16) for _ in range(4)]
    tmp = [sb.alloc("tmp", [128, 512], F32) for _ in range(2)]
    wteb = sb.alloc("wteb", [128, T], BF16)

    experts = [("sh", 0), ("sh", 1)] + [("exp", e) for e in range(NEXP)]
    prev_exp_d = None
    last_tm = None
    h_tok = [None, None]
    free_tok = {"g": None, "u": None, "d": None}
    load_tok = {"g": None, "u": None, "d": None}
    stg_use = [None]

    def load_mat(kind, e, dst, nch, wid, key):
        srcs = {"eg": eg_d, "eu": eu_d, "ed": ed_d,
                "sg": sg_d, "su": su_d, "sd": sd_d}
        dd = [free_tok[key]] if free_tok[key] else []
        tok = None
        npair = (nch + 1) // 2
        for pp in range(npair):
            c0 = 2 * pp
            nn = min(2, nch - c0)
            sdep = [stg_use[0]] if stg_use[0] else []
            ld = sch.dma("Q", lambda q, kind=kind, e=e, c0=c0, nn=nn, wid=wid:
                         q.dma_start(
                             out=stg[0].ap()[:, 0:nn * wid],
                             in_=srcs[kind][e, :, c0:c0 + nn, :].rearrange(
                                 "p a b -> p (a b)")),
                         deps=dd + sdep)
            tok = sch.op("V", lambda v, c0=c0, nn=nn, wid=wid, dst=dst:
                         v.tensor_copy(
                             dst.ap()[:, c0:c0 + nn, :].rearrange(
                                 "p a b -> p (a b)"),
                             stg[0].ap()[:, 0:nn * wid]),
                         deps=[ld], ms=True)
            stg_use[0] = tok
            dd = []
        load_tok[key] = tok
        return tok

    kind0, e0 = experts[0]
    load_mat("sg" if kind0 == "sh" else "eg", e0, wg, C, F_FF, "g")
    load_mat("su" if kind0 == "sh" else "eu", e0, wu, C, F_FF, "u")
    load_mat("sd" if kind0 == "sh" else "ed", e0, wd, FC, D, "d")

    acc_tok = {}
    for ei, (kind, e) in enumerate(experts):
        routed = kind == "exp"
        wg_tok, wu_tok, wd_tok = (load_tok["g"], load_tok["u"],
                                  load_tok["d"])
        if routed:
            ebt = []
            for tt in range(4):
                dd = list(ctx.ps_tok[6]) + ([last_tm] if last_tm else [])
                t_mm = sch.op("T", lambda t, e=e, tt=tt: t.matmul(
                    ctx.bank(6), sel8.ap()[:, e, :],
                    wteT.ap().rearrange("p a b -> p (a b)")[0:8,
                        512 * tt:512 * (tt + 1)],
                    start=True, stop=True), deps=dd, ms=True)
                tok = sch.op("S", lambda s, tt=tt: s.copy(
                    wteb.ap()[:, 512 * tt:512 * (tt + 1)], ctx.bank(6)),
                    deps=[t_mm], ms=True)
                ctx.ps_tok[6] = (tok,)
                ebt.append(tok)
            wteb_tok = ebt[-1]
        g_last = u_last = d_last = None
        prev_half_d = prev_exp_d
        for half in range(4):
            tts = (half,)
            for tt in tts:
                for fc in range(FC):
                    bg = (fc + tt) % 2
                    bu = 2 + (fc + tt) % 2
                    t_g = t_u = None
                    for kc in range(C):
                        dd = ([wg_tok, t_xn] if kc == 0 else []) + \
                            (list(ctx.ps_tok[bg]) if kc == 0 else [])
                        t_g = sch.op("T", lambda t, e=e, fc=fc, tt=tt, kc=kc,
                                     bg=bg: t.matmul(
                            ctx.bank(bg), wg.ap()[:, kc, 128 * fc:128 * (fc + 1)],
                            xn.ap()[:, kc, 512 * tt:512 * (tt + 1)],
                            start=(kc == 0), stop=(kc == C - 1)),
                            deps=dd, ms=(kc == C - 1))
                        dd = ([wu_tok] if kc == 0 else []) + \
                            (list(ctx.ps_tok[bu]) if kc == 0 else [])
                        t_u = sch.op("T", lambda t, e=e, fc=fc, tt=tt, kc=kc,
                                     bu=bu: t.matmul(
                            ctx.bank(bu), wu.ap()[:, kc, 128 * fc:128 * (fc + 1)],
                            xn.ap()[:, kc, 512 * tt:512 * (tt + 1)],
                            start=(kc == 0), stop=(kc == C - 1)),
                            deps=dd, ms=(kc == C - 1))
                    g_last, u_last = t_g, t_u
                    a = (fc + tt) % 2
                    t_sil = sch.op("S", lambda s, a=a, bg=bg: s.activation(
                        gsil[a].ap()[:], ctx.bank(bg), AF.Silu),
                        deps=[t_g] + ([h_tok[a]] if h_tok[a] else []), ms=True)
                    t_h = sch.op("V", lambda v, fc=fc, tt=tt, a=a, bu=bu:
                                 v.tensor_mul(
                        hsb.ap()[:, fc, :],
                        gsil[a].ap()[:], ctx.bank(bu)),
                        deps=[t_sil, t_u] +
                        ([prev_half_d] if prev_half_d else []), ms=True)
                    h_tok[a] = t_h
                    ctx.ps_tok[bg] = (t_sil,)
                    ctx.ps_tok[bu] = (t_h,)
            prev_half_d = None
            for tt in tts:
                for mo in range(C):
                    bd = 4 + (mo + tt) % 2
                    t_d = None
                    for fc in range(FC):
                        dd = ([wd_tok, t_h] if fc == 0 else []) + \
                            (list(ctx.ps_tok[bd]) if fc == 0 else [])
                        t_d = sch.op("T", lambda t, e=e, mo=mo, tt=tt, fc=fc,
                                     bd=bd: t.matmul(
                            ctx.bank(bd), wd.ap()[:, fc, 128 * mo:128 * (mo + 1)],
                            hsb.ap()[:, fc, :],
                            start=(fc == 0), stop=(fc == FC - 1)),
                            deps=dd, ms=(fc == FC - 1))
                    d_last = t_d
                    prev_half_d = t_d
                    a = (mo + tt) % 2
                    prev = acc_tok.get((mo, tt))
                    if routed:
                        t_m = sch.op("V", lambda v, a=a, bd=bd, tt=tt:
                                     v.tensor_mul(
                            tmp[a].ap()[:], ctx.bank(bd),
                            wteb.ap()[:, 512 * tt:512 * (tt + 1)]),
                            deps=[t_d, wteb_tok] +
                            ([acc_tok[("tmp", a)]] if ("tmp", a) in acc_tok
                             else []), ms=True)
                        t_acc = sch.op("P", lambda p, mo=mo, tt=tt, a=a:
                                       p.tensor_add(
                            acc.ap()[:, mo, 512 * tt:512 * (tt + 1)],
                            acc.ap()[:, mo, 512 * tt:512 * (tt + 1)],
                            tmp[a].ap()[:]),
                            deps=[t_m] + ([prev] if prev else []), ms=True)
                        acc_tok[("tmp", a)] = t_acc
                        last_tm = t_m
                        ctx.ps_tok[bd] = (t_m,)
                    else:
                        if ei == 0:
                            t_acc = sch.op("V", lambda v, mo=mo, tt=tt, bd=bd:
                                           v.tensor_copy(
                                acc.ap()[:, mo, 512 * tt:512 * (tt + 1)],
                                ctx.bank(bd)), deps=[t_d], ms=True)
                        else:
                            t_acc = sch.op("V", lambda v, mo=mo, tt=tt, bd=bd:
                                           v.tensor_add(
                                acc.ap()[:, mo, 512 * tt:512 * (tt + 1)],
                                acc.ap()[:, mo, 512 * tt:512 * (tt + 1)],
                                ctx.bank(bd)),
                                deps=[t_d] + ([prev] if prev else []), ms=True)
                        ctx.ps_tok[bd] = (t_acc,)
                    acc_tok[(mo, tt)] = t_acc
        prev_exp_d = d_last
        if ei + 1 < len(experts):
            nkind, ne = experts[ei + 1]
            free_tok["g"] = g_last
            free_tok["u"] = u_last
            free_tok["d"] = d_last
            load_mat("sg" if nkind == "sh" else "eg", ne, wg, C, F_FF, "g")
            load_mat("su" if nkind == "sh" else "eu", ne, wu, C, F_FF, "u")
            load_mat("sd" if nkind == "sh" else "ed", ne, wd, FC, D, "d")
    t_st = sch.dma("Q", lambda q: q.dma_start(
        out=moe_d[:], in_=acc.ap().rearrange("p a b -> p (a b)")),
        deps=[tok for tok in acc_tok.values()])
    sch.barrier()
    for t_ in [xn, sel8, wte, wteT, idb, acc, wg, wu, wd, hsb, wteb] + \
            stg + gsil + tmp:
        sb.free_(t_)


def _tm_ln_stats(ctx, xt, stat, deps):
    sch = ctx.sch
    t_sum = sch.op("V", lambda v: v.reduce_sum(
        stat.ap()[:, 0:1], xt.ap()[:], axis=AX.X), deps=deps, ms=True)
    t_sq0 = sch.op("S", lambda s: s.activation(
        ctx.sq_buf.ap()[:], xt.ap()[:], AF.Square), deps=deps, ms=True)
    t_sq = sch.op("V", lambda v: v.reduce_sum(
        stat.ap()[:, 1:2], ctx.sq_buf.ap()[:], axis=AX.X), deps=[t_sq0],
        ms=True)
    t_mean = sch.op("V", lambda v: v.tensor_scalar(
        stat.ap()[:, 2:3], stat.ap()[:, 0:1], 1.0 / D, None, ALU.mult),
        sdeps=[t_sum], ms=True)
    t_msq = sch.op("V", lambda v: v.tensor_mul(
        stat.ap()[:, 3:4], stat.ap()[:, 2:3], stat.ap()[:, 2:3]),
        sdeps=[t_mean], ms=True)
    t_v0 = sch.op("V", lambda v: v.tensor_scalar(
        stat.ap()[:, 4:5], stat.ap()[:, 1:2], 1.0 / D, None, ALU.mult),
        sdeps=[t_sq], ms=True)
    t_var = sch.op("V", lambda v: v.tensor_scalar(
        stat.ap()[:, 5:6], stat.ap()[:, 4:5], stat.ap()[:, 3:4],
        float(LN_EPS), ALU.subtract, ALU.add), sdeps=[t_msq, t_v0], ms=True)
    t_ln = sch.op("S", lambda s: s.activation(
        stat.ap()[:, 6:7], stat.ap()[:, 5:6], AF.Ln), deps=[t_var], ms=True)
    t_half = sch.op("V", lambda v: v.tensor_scalar(
        stat.ap()[:, 6:7], stat.ap()[:, 6:7], -0.5, None, ALU.mult),
        deps=[t_ln], ms=True)
    t_rstd = sch.op("S", lambda s: s.activation(
        stat.ap()[:, 7:8], stat.ap()[:, 6:7], AF.Exp), deps=[t_half], ms=True)
    return stat.ap()[:, 2:3], stat.ap()[:, 7:8], (t_mean, t_rstd)


def _tm_ln_stats2(ctx, xt, stat, deps, sq_war=()):
    sch = ctx.sch
    t_sum = sch.op("V", lambda v: v.reduce_sum(
        stat.ap()[:, 0:1], xt.ap()[:], axis=AX.X), deps=deps, ms=True)
    t_sq0 = sch.op("S", lambda s: s.activation(
        ctx.sq_buf.ap()[:], xt.ap()[:], AF.Square),
        deps=list(deps) + [d for d in sq_war if d], ms=True)
    t_sqred = sch.op("V", lambda v: v.reduce_sum(
        stat.ap()[:, 1:2], ctx.sq_buf.ap()[:], axis=AX.X), deps=[t_sq0],
        ms=True)
    t_mean = sch.op("V", lambda v: v.tensor_scalar(
        stat.ap()[:, 2:3], stat.ap()[:, 0:1], 1.0 / D, None, ALU.mult),
        sdeps=[t_sum], ms=True)
    t_msq = sch.op("V", lambda v: v.tensor_mul(
        stat.ap()[:, 3:4], stat.ap()[:, 2:3], stat.ap()[:, 2:3]),
        sdeps=[t_mean], ms=True)
    t_v0 = sch.op("V", lambda v: v.tensor_scalar(
        stat.ap()[:, 4:5], stat.ap()[:, 1:2], 1.0 / D, None, ALU.mult),
        sdeps=[t_sqred], ms=True)
    t_var = sch.op("V", lambda v: v.tensor_scalar(
        stat.ap()[:, 5:6], stat.ap()[:, 4:5], stat.ap()[:, 3:4],
        float(LN_EPS), ALU.subtract, ALU.add), sdeps=[t_msq, t_v0], ms=True)
    t_ln = sch.op("S", lambda s: s.activation(
        stat.ap()[:, 6:7], stat.ap()[:, 5:6], AF.Ln), deps=[t_var], ms=True)
    t_half = sch.op("V", lambda v: v.tensor_scalar(
        stat.ap()[:, 6:7], stat.ap()[:, 6:7], -0.5, None, ALU.mult),
        deps=[t_ln], ms=True)
    t_rstd = sch.op("S", lambda s: s.activation(
        stat.ap()[:, 7:8], stat.ap()[:, 6:7], AF.Exp), deps=[t_half], ms=True)
    return stat.ap()[:, 2:3], stat.ap()[:, 7:8], (t_mean, t_rstd, t_sqred)


def phase_combine1(ctx):
    nc, sch, sb = ctx.nc, ctx.sch, ctx.sb
    x_in = ctx.dram_t("x", [T, D], F32).ap()
    s1_d = ctx.dram_t("s1_d", [128, C * T], BF16).ap()
    moe_d = ctx.dram_t("moe_d", [128, C * T], BF16).ap().rearrange(
        "p (c t) -> p c t", c=C)
    x1_d = ctx.dram_t("x1_d", [T, D], F32).ap()
    xn1_d = ctx.dram_t("xn1_bf", [128, C * T], BF16).ap()
    idb_d = ctx.dram_t("ident_bf", [128, 128], BF16).ap()

    p = sb.alloc("p", [128, C, T], BF16)
    mt = [sb.alloc("mt", [128, T], BF16) for _ in range(2)]
    gt = [sb.alloc("gt", [128, T], BF16) for _ in range(2)]
    idb = sb.alloc("idb", [128, 128], BF16)
    t_s1 = sch.dma("Q", lambda q: q.dma_start(out=p.ap()[:], in_=s1_d[:]))
    sch.dma("Q", lambda q: q.dma_start(out=idb.ap()[:], in_=idb_d[:]))
    mt_use = [None, None]
    p_tok = []
    for c in range(C):
        a = c % 2
        ld = sch.dma("Q", lambda q, c=c, a=a: q.dma_start(
            out=mt[a].ap()[:], in_=moe_d[:, c, :]),
            deps=[mt_use[a]] if mt_use[a] else [])
        t_g = sch.op("S", lambda s, a=a: s.activation(
            gt[a].ap()[:], mt[a].ap()[:], AF.Silu), deps=[ld], ms=True)
        mt_use[a] = t_g
        t_p = sch.op("V", lambda v, c=c, a=a: v.tensor_mul(
            p.ap()[:, c, :], p.ap()[:, c, :], gt[a].ap()[:]),
            deps=[t_g, t_s1], ms=True)
        p_tok.append(t_p)
    sch.barrier()
    for t_ in mt + gt:
        sb.free_(t_)

    x0t = [sb.alloc("x0t", [128, D], F32) for _ in range(2)]
    x1t = [sb.alloc("x1t", [128, D], F32) for _ in range(2)]
    xn1_tm = [sb.alloc("xn1_tm", [128, D], BF16) for _ in range(2)]
    stat = [sb.alloc("stat", [128, 8], F32) for _ in range(2)]
    ctx.sq_buf = sb.alloc("sqb", [128, D], F32)
    xn1_fm = sb.alloc("xn1_fm", [128, C, T], BF16)
    ld_tok = [None] * TT
    x0_use = [None, None]
    x1_use = [None, None]
    x1_dma = [None, None]
    xn1_use = [None, None]
    stat_use = [None, None]
    sq_use = None
    ld_tok[0] = sch.dma("Q", lambda q: q.dma_start(
        out=x0t[0].ap()[:], in_=x_in[0:128, :]))
    for i in range(TT):
        bi = i % 2
        if i + 1 < TT:
            ld_tok[i + 1] = sch.dma("Q", lambda q, i=i: q.dma_start(
                out=x0t[(i + 1) % 2].ap()[:],
                in_=x_in[128 * (i + 1):128 * (i + 2), :]),
                deps=[x0_use[(i + 1) % 2]] if x0_use[(i + 1) % 2] else [])
        adds = []
        for c in range(C):
            pb = c % 2
            t_tr = sch.op("T", lambda t, i=i, c=c, pb=pb: t.transpose(
                ctx.bank(pb, [128, 128], BF16), p.ap()[:, c, 128 * i:128 * (i + 1)],
                idb.ap()[:]), deps=list(ctx.ps_tok[pb]) + p_tok, ms=True)
            t_add = sch.op("V", lambda v, c=c, bi=bi, pb=pb: v.tensor_add(
                x1t[bi].ap()[:, 128 * c:128 * (c + 1)],
                x0t[bi].ap()[:, 128 * c:128 * (c + 1)],
                ctx.bank(pb, [128, 128], BF16)),
                deps=[t_tr, ld_tok[i]] +
                ([x1_use[bi], x1_dma[bi]] if (c == 0 and x1_use[bi]) else []),
                ms=True)
            ctx.ps_tok[pb] = (t_add,)
            adds.append(t_add)
        x0_use[bi] = adds[-1]
        x1_dma[bi] = sch.dma("Q", lambda q, i=i, bi=bi: q.dma_start(
            out=x1_d[128 * i:128 * (i + 1), :], in_=x1t[bi].ap()[:]),
            deps=[adds[-1]])
        sdep = [stat_use[bi]] if stat_use[bi] else []
        mean, rstd, (tm_, tr_, tsq_) = _tm_ln_stats2(
            ctx, x1t[bi], stat[bi], [adds[-1]] + sdep,
            [sq_use] if sq_use else [])
        sq_use = tsq_
        t_xn = sch.op("V", lambda v, bi=bi, mean=mean, rstd=rstd:
                      v.tensor_scalar(
            xn1_tm[bi].ap()[:], x1t[bi].ap()[:], mean, rstd,
            ALU.subtract, ALU.mult), deps=[tr_],
            sdeps=[tm_] + ([xn1_use[bi]] if xn1_use[bi] else []), ms=True)
        x1_use[bi] = t_xn
        stat_use[bi] = t_xn
        trs = []
        for c in range(C):
            pb = c % 2
            t_tr = sch.op("T", lambda t, c=c, bi=bi, pb=pb: t.transpose(
                ctx.bank(pb, [128, 128], BF16),
                xn1_tm[bi].ap()[:, 128 * c:128 * (c + 1)], idb.ap()[:]),
                deps=[t_xn] + list(ctx.ps_tok[pb]), ms=True)
            tok = sch.op("S", lambda s, i=i, c=c, pb=pb: s.copy(
                xn1_fm.ap()[:, c, 128 * i:128 * (i + 1)],
                ctx.bank(pb, [128, 128], BF16)), deps=[t_tr], ms=True)
            ctx.ps_tok[pb] = (tok,)
            trs.append(t_tr)
        xn1_use[bi] = trs[-1]
    sch.dma("Q", lambda q: q.dma_start(out=xn1_d[:], in_=xn1_fm.ap()[:]),
            deps=[("S", sch.ms["S"])])
    sch.barrier()
    for t_ in [p, idb, ctx.sq_buf, xn1_fm] + x0t + x1t + xn1_tm + stat:
        sb.free_(t_)


def phase_combine2(ctx):
    nc, sch, sb = ctx.nc, ctx.sch, ctx.sb
    x1_d = ctx.dram_t("x1_d", [T, D], F32).ap()
    s2_d = ctx.dram_t("s2_d", [128, C * T], BF16).ap()
    g2_d = ctx.dram_t("g2_d", [128, C * T], F32).ap().rearrange(
        "p (c t) -> p c t", c=C)
    out_d = ctx.dram_t("out", [T, D], F32).ap()
    idb_d = ctx.dram_t("ident_bf", [128, 128], BF16).ap()

    p = sb.alloc("p", [128, C, T], BF16)
    mt = [sb.alloc("mt", [128, T], F32) for _ in range(2)]
    gt = [sb.alloc("gt", [128, T], BF16) for _ in range(2)]
    idb = sb.alloc("idb", [128, 128], BF16)
    t_s2 = sch.dma("Q", lambda q: q.dma_start(out=p.ap()[:], in_=s2_d[:]))
    sch.dma("Q", lambda q: q.dma_start(out=idb.ap()[:], in_=idb_d[:]))
    mt_use = [None, None]
    for c in range(C):
        a = c % 2
        ld = sch.dma("Q", lambda q, c=c, a=a: q.dma_start(
            out=mt[a].ap()[:], in_=g2_d[:, c, :]),
            deps=[mt_use[a]] if mt_use[a] else [])
        t_g = sch.op("S", lambda s, a=a: s.activation(
            gt[a].ap()[:], mt[a].ap()[:], AF.Silu), deps=[ld], ms=True)
        mt_use[a] = t_g
        sch.op("V", lambda v, c=c, a=a: v.tensor_mul(
            p.ap()[:, c, :], p.ap()[:, c, :], gt[a].ap()[:]),
            deps=[t_g, t_s2], ms=True)
    sch.barrier()
    for t_ in mt + gt:
        sb.free_(t_)

    x1t = [sb.alloc("x1t", [128, D], F32) for _ in range(2)]
    ot = [sb.alloc("ot", [128, D], F32) for _ in range(2)]
    ld_tok = [None] * TT
    ot_use = [None, None]
    x1_use = [None, None]
    ld_tok[0] = sch.dma("Q", lambda q: q.dma_start(
        out=x1t[0].ap()[:], in_=x1_d[0:128, :]))
    st_tok = []
    for i in range(TT):
        bi = i % 2
        if i + 1 < TT:
            ld_tok[i + 1] = sch.dma("Q", lambda q, i=i: q.dma_start(
                out=x1t[(i + 1) % 2].ap()[:],
                in_=x1_d[128 * (i + 1):128 * (i + 2), :]),
                deps=[x1_use[(i + 1) % 2]] if x1_use[(i + 1) % 2] else [])
        adds = []
        for c in range(C):
            pb = c % 2
            t_tr = sch.op("T", lambda t, i=i, c=c, pb=pb: t.transpose(
                ctx.bank(pb, [128, 128], BF16), p.ap()[:, c, 128 * i:128 * (i + 1)],
                idb.ap()[:]), deps=list(ctx.ps_tok[pb]), ms=True)
            t_add = sch.op("V", lambda v, c=c, bi=bi, pb=pb: v.tensor_add(
                ot[bi].ap()[:, 128 * c:128 * (c + 1)],
                x1t[bi].ap()[:, 128 * c:128 * (c + 1)],
                ctx.bank(pb, [128, 128], BF16)),
                deps=[t_tr, ld_tok[i]] + ([ot_use[bi]] if ot_use[bi] else []),
                ms=True)
            ctx.ps_tok[pb] = (t_add,)
            adds.append(t_add)
        x1_use[bi] = adds[-1]
        ot_use[bi] = sch.dma("Q", lambda q, i=i, bi=bi: q.dma_start(
            out=out_d[128 * i:128 * (i + 1), :], in_=ot[bi].ap()[:]),
            deps=[adds[-1]])
        st_tok.append(ot_use[bi])
    sch.barrier()
    for t_ in [p, idb] + x1t + ot:
        sb.free_(t_)


def phase_l1attn(ctx):
    nc, sch, sb = ctx.nc, ctx.sch, ctx.sb
    xn_d = ctx.dram_t("xn1_bf", [128, C * T], BF16).ap()
    s2_d = ctx.dram_t("s2_d", [128, C * T], BF16).ap()
    wq_d = ctx.dram_t("wqbf", [128, C * D], BF16).ap()
    wk_d = ctx.dram_t("wkbf", [128, C * D], BF16).ap()
    wv_d = ctx.dram_t("wvbf", [128, C * D], BF16).ap()
    wo_d = ctx.dram_t("wobf", [128, C * D], BF16).ap()
    bq_d = ctx.dram_t("bq", [128, C], F32).ap()
    bk_d = ctx.dram_t("bk", [128, C], F32).ap()
    bkr_d = ctx.dram_t("bk_row", [1, D], F32).ap()
    bvr_d = ctx.dram_t("bv_row", [1, D], F32).ap()
    bo_d = ctx.dram_t("bo", [128, C], F32).ap()
    msk_d = ctx.dram_t("maskUT", [128, 128], BF16).ap()

    HT = 1024
    xn = sb.alloc("xn", [128, C, T], BF16)
    qT = sb.alloc("qT", [128, C, HT], BF16)
    kT = sb.alloc("kT", [128, C, HT], BF16)
    ktm = sb.alloc("ktm", [128, 8, HT], BF16)
    vtm = sb.alloc("vtm", [128, 8, HT], BF16)
    Ofm = sb.alloc("Ofm", [128, C, HT], BF16)
    s2 = sb.alloc("s2", [128, C, T], BF16)
    S_sb = sb.alloc("S_sb", [128, 8, DH], BF16)
    small = sb.alloc("small", [128, 3 * C], F32)
    rows = sb.alloc("rows", [1, 2 * D], F32)
    mask = sb.alloc("mask", [128, 128], BF16)
    zero = sb.alloc("zero", [128, 512], F32)
    wbuf = [sb.alloc("wbuf", [128, C, D], BF16) for _ in range(2)]
    phis = [sb.alloc("phis", [128, 512], F32) for _ in range(4)]
    ATm = [sb.alloc("ATm", [128, 128], BF16) for _ in range(2)]
    otmp = [sb.alloc("otmp", [128, 128], BF16) for _ in range(2)]

    t_xn = sch.dma("Q", lambda q: q.dma_start(out=xn.ap()[:], in_=xn_d[:]))
    for i, dd in enumerate([bq_d, bk_d, bo_d]):
        sch.dma("Q", lambda q, i=i, dd=dd: q.dma_start(
            out=small.ap()[:, i * C:(i + 1) * C], in_=dd[:]))
    sch.dma("Q", lambda q: q.dma_start(out=rows.ap()[:, 0:D], in_=bkr_d[:]))
    sch.dma("Q", lambda q: q.dma_start(out=rows.ap()[:, D:2 * D], in_=bvr_d[:]))
    sch.dma("Q", lambda q: q.dma_start(out=mask.ap()[:], in_=msk_d[:]))
    rowbf = sb.alloc("rowbf", [1, 2 * D], BF16)
    sch.barrier()
    sch.op("V", lambda v: v.memset(zero.ap()[:], 0.0), ms=True)
    t_rbf = sch.op("V", lambda v: v.tensor_copy(rowbf.ap()[:], rows.ap()[:]),
                   ms=True)
    ones1 = sb.alloc("ones1", [1, 128], BF16)
    t_o1 = sch.op("V", lambda v: v.memset(ones1.ap()[:], 1.0), ms=True)
    sch.barrier()
    bqc = lambda c: small.ap()[:, 0 * C + c:0 * C + c + 1]
    bkc = lambda c: small.ap()[:, 1 * C + c:1 * C + c + 1]
    boc = lambda c: small.ap()[:, 2 * C + c:2 * C + c + 1]

    wbuf_rd = [None, None]

    def loadw(j, dram):
        dd = [wbuf_rd[j]] if wbuf_rd[j] else []
        return sch.dma("Q", lambda q, j=j, dram=dram: q.dma_start(
            out=wbuf[j].ap().rearrange("p a b -> p (a b)"), in_=dram[:]),
            deps=dd)

    phi_war = {}

    def phi_evac(dst_ap, b, bias_col, deps, pp):
        t1 = sch.op("V", lambda v, b=b, pp=pp: v.scalar_tensor_tensor(
            phis[pp].ap()[:], ctx.bank(b), bias_col, zero.ap()[:],
            ALU.add, ALU.min),
            deps=deps + ([phi_war[pp]] if pp in phi_war else []), ms=True)
        t2 = sch.op("S", lambda s, pp=pp: s.activation(
            phis[pp + 1].ap()[:], phis[pp].ap()[:], AF.Exp),
            deps=[t1], ms=True)
        phi_war[pp] = t2
        t3 = sch.op("V", lambda v, b=b, pp=pp, dst_ap=dst_ap:
                    v.scalar_tensor_tensor(
            dst_ap, ctx.bank(b), bias_col, zero.ap()[:],
            ALU.add, ALU.max), ms=True)
        t4 = sch.op("V", lambda v, pp=pp, dst_ap=dst_ap: v.tensor_add(
            dst_ap, dst_ap, phis[pp + 1].ap()[:]), deps=[t2], ms=True)
        return t4

    s2_tok = []
    chT = {}
    for half in range(2):
        t0 = half * HT
        wtok = loadw(0, wq_d)
        for mo in range(C):
            for tt in range(2):
                b = (mo * 2 + tt) % 2
                t_mm = None
                for kc in range(C):
                    dd = ([wtok, t_xn] if kc == 0 else []) + \
                        (list(ctx.ps_tok[b]) if kc == 0 else [])
                    t_mm = sch.op("T", lambda t, mo=mo, tt=tt, kc=kc, b=b,
                                  t0=t0: t.matmul(
                        ctx.bank(b),
                        wbuf[0].ap()[:, kc, 128 * mo:128 * (mo + 1)],
                        xn.ap()[:, kc, t0 + 512 * tt:t0 + 512 * (tt + 1)],
                        start=(kc == 0), stop=(kc == C - 1)),
                        deps=dd, ms=(kc == C - 1))
                tok = phi_evac(qT.ap()[:, mo, 512 * tt:512 * (tt + 1)], b,
                               bqc(mo), [t_mm], (mo * 2 + tt) % 2 * 2)
                ctx.ps_tok[b] = (tok,)
        wbuf_rd[0] = t_mm
        wtok = loadw(1, wk_d)
        for mo in range(C):
            for tt in range(2):
                b = (mo * 2 + tt) % 2
                t_mm = None
                for kc in range(C):
                    dd = ([wtok] if kc == 0 else []) + \
                        (list(ctx.ps_tok[b]) if kc == 0 else [])
                    t_mm = sch.op("T", lambda t, mo=mo, tt=tt, kc=kc, b=b,
                                  t0=t0: t.matmul(
                        ctx.bank(b),
                        wbuf[1].ap()[:, kc, 128 * mo:128 * (mo + 1)],
                        xn.ap()[:, kc, t0 + 512 * tt:t0 + 512 * (tt + 1)],
                        start=(kc == 0), stop=(kc == C - 1)),
                        deps=dd, ms=(kc == C - 1))
                tok = phi_evac(kT.ap()[:, mo, 512 * tt:512 * (tt + 1)], b,
                               bkc(mo), [t_mm], (mo * 2 + tt) % 2 * 2)
                ctx.ps_tok[b] = (tok,)
        wtok_v = loadw(0, wv_d)
        for it in range(8):
            for nn in range(2):
                b = (it * 2 + nn) % 2
                t_mm = None
                for kc in range(C):
                    dd = (list(ctx.ps_tok[b]) if kc == 0 else [])
                    t_mm = sch.op("T", lambda t, it=it, nn=nn, kc=kc, b=b,
                                  t0=t0: t.matmul(
                        ctx.bank(b), xn.ap()[:, kc,
                                             t0 + 128 * it:t0 + 128 * (it + 1)],
                        wbuf[1].ap()[:, kc, 512 * nn:512 * (nn + 1)],
                        start=(kc == 0), stop=False),
                        deps=dd, ms=False)
                t_mm = sch.op("T", lambda t, nn=nn, b=b: t.matmul(
                    ctx.bank(b), ones1.ap()[:],
                    rowbf.ap()[:, 512 * nn:512 * (nn + 1)],
                    start=False, stop=True), deps=[t_rbf, t_o1], ms=True)
                tok = phi_evac(
                    ktm.ap()[:, it, 512 * nn:512 * (nn + 1)], b,
                    zero.ap()[:, 0:1], [t_mm], (it * 2 + nn) % 2 * 2)
                ctx.ps_tok[b] = (tok,)
        wbuf_rd[1] = t_mm
        for it in range(8):
            for nn in range(2):
                b = (it * 2 + nn) % 2
                t_mm = None
                for kc in range(C):
                    dd = ([wtok_v] if kc == 0 else []) + \
                        (list(ctx.ps_tok[b]) if kc == 0 else [])
                    t_mm = sch.op("T", lambda t, it=it, nn=nn, kc=kc, b=b,
                                  t0=t0: t.matmul(
                        ctx.bank(b), xn.ap()[:, kc,
                                             t0 + 128 * it:t0 + 128 * (it + 1)],
                        wbuf[0].ap()[:, kc, 512 * nn:512 * (nn + 1)],
                        start=(kc == 0), stop=False),
                        deps=dd, ms=False)
                t_mm = sch.op("T", lambda t, nn=nn, b=b: t.matmul(
                    ctx.bank(b), ones1.ap()[:],
                    rowbf.ap()[:, D + 512 * nn:D + 512 * (nn + 1)],
                    start=False, stop=True), ms=True)
                tok = sch.op("S", lambda s, it=it, nn=nn, b=b: s.copy(
                    vtm.ap()[:, it, 512 * nn:512 * (nn + 1)], ctx.bank(b)),
                    deps=[t_mm], ms=True)
                ctx.ps_tok[b] = (tok,)
        wbuf_rd[0] = t_mm
        sch.barrier()
        for cc in range(8):
            gc = half * 8 + cc
            for h in range(NHEAD):
                i = gc * NHEAD + h
                bh = (h % 2) * 64
                ch = h // 2
                hsl = slice(64 * h - 1024 * (h // 16), 64 * h + 64 - 0)
                bA = i % 2
                bI = 2 + i % 2
                bJ = 4 + i % 2
                bS = 6 + i % 2
                t_at = sch.op("T", lambda t, cc=cc, ch=ch, bh=bh, bA=bA:
                              t.matmul(
                    ctx.bank(bA)[:, 0:128],
                    kT.ap()[:, ch, 128 * cc:128 * (cc + 1)]
                    [bh:bh + 64, :],
                    qT.ap()[:, ch, 128 * cc:128 * (cc + 1)][bh:bh + 64, :],
                    start=True, stop=True),
                    deps=list(ctx.ps_tok[bA]), ms=True)
                t_msk = sch.op("V", lambda v, i=i, bA=bA: v.tensor_mul(
                    ATm[i % 2].ap()[:], ctx.bank(bA)[:, 0:128], mask.ap()[:]),
                    deps=[t_at] + ([chT.get(("atm", i % 2))] if
                                   chT.get(("atm", i % 2)) else []), ms=True)
                chT[("atm", i % 2)] = t_msk
                ctx.ps_tok[bA] = (t_msk,)
                t_in = sch.op("T", lambda t, cc=cc, h=h, bh=bh, bI=bI, i=i:
                              t.matmul(
                    ctx.bank(bI)[bh:bh + 64, 0:128],
                    vtm.ap()[:, cc, 64 * h:64 * h + 64], ATm[i % 2].ap()[:],
                    start=True, stop=True),
                    deps=[t_msk] + list(ctx.ps_tok[bI]), ms=True)
                prev_s = chT.get(("s", h))
                if gc > 0:
                    t_it = sch.op("T", lambda t, cc=cc, ch=ch, bh=bh, bJ=bJ,
                                  h=h: t.matmul(
                        ctx.bank(bJ)[bh:bh + 64, 0:128],
                        S_sb.ap()[:, ch, :][bh:bh + 64, :],
                        qT.ap()[:, ch, 128 * cc:128 * (cc + 1)][bh:bh + 64, :],
                        start=True, stop=True),
                        deps=[prev_s] + list(ctx.ps_tok[bJ]), ms=True)
                    t_oc = sch.op("S", lambda s, bh=bh, bJ=bJ, i=i: s.copy(
                        otmp[i % 2].ap()[bh:bh + 64, :],
                        ctx.bank(bJ)[bh:bh + 64, 0:128]),
                        deps=[t_it] + ([chT.get(("ot", i % 2))] if
                                       chT.get(("ot", i % 2)) else []),
                        ms=True)
                    chT[("ot", i % 2)] = t_oc
                    ctx.ps_tok[bJ] = (t_oc,)
                    t_o = sch.op("V", lambda v, cc=cc, h=h, bh=bh, bI=bI, i=i:
                                 v.tensor_add(
                        Ofm.ap()[:, h // 2, 128 * cc:128 * (cc + 1)]
                        [bh:bh + 64, :], ctx.bank(bI)[bh:bh + 64, 0:128],
                        otmp[i % 2].ap()[bh:bh + 64, :]),
                        deps=[t_in, t_oc], ms=True)
                else:
                    t_o = sch.op("V", lambda v, cc=cc, h=h, bh=bh, bI=bI:
                                 v.tensor_copy(
                        Ofm.ap()[:, h // 2, 128 * cc:128 * (cc + 1)]
                        [bh:bh + 64, :], ctx.bank(bI)[bh:bh + 64, 0:128]),
                        deps=[t_in], ms=True)
                ctx.ps_tok[bI] = (t_o,)
                t_ds = sch.op("T", lambda t, cc=cc, h=h, bh=bh, bS=bS:
                              t.matmul(
                    ctx.bank(bS)[bh:bh + 64, 0:64],
                    ktm.ap()[:, cc, 64 * h:64 * h + 64],
                    vtm.ap()[:, cc, 64 * h:64 * h + 64],
                    start=True, stop=True),
                    deps=list(ctx.ps_tok[bS]), ms=True)
                if gc > 0:
                    t_s = sch.op("V", lambda v, h=h, ch=ch, bh=bh, bS=bS:
                                 v.tensor_add(
                        S_sb.ap()[:, ch, :][bh:bh + 64, :],
                        S_sb.ap()[:, ch, :][bh:bh + 64, :],
                        ctx.bank(bS)[bh:bh + 64, 0:64]),
                        deps=[t_ds, t_it], ms=True)
                else:
                    t_s = sch.op("V", lambda v, h=h, ch=ch, bh=bh, bS=bS:
                                 v.tensor_copy(
                        S_sb.ap()[:, ch, :][bh:bh + 64, :],
                        ctx.bank(bS)[bh:bh + 64, 0:64]),
                        deps=[t_ds], ms=True)
                chT[("s", h)] = t_s
                ctx.ps_tok[bS] = (t_s,)
        sch.barrier()
        wtok = loadw(1, wo_d)
        for mo in range(C):
            for tt in range(2):
                b = (mo * 2 + tt) % 2
                t_mm = None
                for kc in range(C):
                    dd = ([wtok] if kc == 0 else []) + \
                        (list(ctx.ps_tok[b]) if kc == 0 else [])
                    t_mm = sch.op("T", lambda t, mo=mo, tt=tt, kc=kc, b=b:
                                  t.matmul(
                        ctx.bank(b),
                        wbuf[1].ap()[:, kc, 128 * mo:128 * (mo + 1)],
                        Ofm.ap()[:, kc, 512 * tt:512 * (tt + 1)],
                        start=(kc == 0), stop=(kc == C - 1)),
                        deps=dd, ms=(kc == C - 1))
                tok = sch.op("V", lambda v, mo=mo, tt=tt, b=b, t0=t0:
                             v.tensor_scalar(
                    s2.ap()[:, mo, t0 + 512 * tt:t0 + 512 * (tt + 1)],
                    ctx.bank(b), boc(mo), None, ALU.add),
                    deps=[t_mm], ms=True)
                ctx.ps_tok[b] = (tok,)
                s2_tok.append(tok)
        wbuf_rd[1] = t_mm
        sch.barrier()
    sch.dma("Q", lambda q: q.dma_start(out=s2_d[:], in_=s2.ap()[:]),
            deps=[s2_tok[-1]])
    sch.barrier()
    for t_ in ([xn, qT, kT, ktm, vtm, Ofm, s2, S_sb, small, rows, mask, zero,
                rowbf, ones1] + wbuf + phis + ATm + otmp):
        sb.free_(t_)


def _row_chain(ctx, ops):
    sch = ctx.sch
    tok = None
    for eng, fn in ops:
        if eng == "S":
            tok = sch.op("S", fn, deps=[tok] if tok else [], ms=True)
        else:
            tok = sch.op("V", fn, deps=[tok] if tok else [],
                         sdeps=[tok] if tok else [], ms=True)
    return tok


def phase_l1ffn(ctx):
    nc, sch, sb = ctx.nc, ctx.sch, ctx.sb
    ES = 256
    NE, NS, NH = 16, 16, 256
    xn_d = ctx.dram_t("xn1_bf", [128, C * T], BF16).ap()
    g2_d = ctx.dram_t("g2_d", [128, C * T], F32).ap().rearrange(
        "p (c t) -> p c t", c=C)
    mu_d = ctx.dram_t("mu_fm", [128, C, ES], F32).ap()
    lg_g = ctx.dram_t("l1_gamma", [128, C], F32).ap()
    lg_b = ctx.dram_t("l1_beta", [128, C], F32).ap()
    sc_d = ctx.dram_t("l1_scale", [1, 1], F32).ap()
    w1_d = ctx.dram_t("w1_l1", [NE, 128, C, NH], F32).ap()
    b1_d = ctx.dram_t("b1_l1", [NE, NH], F32).ap()
    w2_d = ctx.dram_t("w2_l1", [NE, 128, 2, D], F32).ap()
    b2_d = ctx.dram_t("b2_l1", [NE, D], F32).ap()
    s16_d = ctx.dram_t("sel16", [128, 8 * 16], BF16).ap()
    idb_d = ctx.dram_t("ident_bf", [128, 128], BF16).ap()
    idf_d = ctx.dram_t("ident_f32", [128, 128], F32).ap()

    xn = sb.alloc("xn", [128, C, T], BF16)
    xnl = sb.alloc("xnl", [128, C, T], BF16)
    small = sb.alloc("small", [128, 2 * C], F32)
    rows = sb.alloc("rows", [1, 8 * T], F32)
    ones_bf = sb.alloc("ones_bf", [128, 1], BF16)
    ones_f = sb.alloc("ones_f", [1, 128], F32)
    sq = [sb.alloc("sq", [128, 512], BF16) for _ in range(2)]
    tm1 = [sb.alloc("tm1", [128, 512], F32) for _ in range(2)]
    idb = sb.alloc("idb", [128, 128], BF16)
    idf = sb.alloc("idf", [128, 128], F32)
    scl = sb.alloc("scl", [1, 1], F32)
    t_xn = sch.dma("Q", lambda q: q.dma_start(out=xn.ap()[:], in_=xn_d[:]))
    sch.dma("Q", lambda q: q.dma_start(out=small.ap()[:, 0:C], in_=lg_g[:]))
    sch.dma("Q", lambda q: q.dma_start(out=small.ap()[:, C:2 * C], in_=lg_b[:]))
    sch.dma("Q", lambda q: q.dma_start(out=idb.ap()[:], in_=idb_d[:]))
    sch.dma("Q", lambda q: q.dma_start(out=idf.ap()[:], in_=idf_d[:]))
    sch.dma("Q", lambda q: q.dma_start(out=scl.ap()[:], in_=sc_d[:]))
    sch.barrier()
    sch.op("V", lambda v: v.memset(ones_bf.ap()[:], 1.0), ms=True)
    sch.op("V", lambda v: v.memset(ones_f.ap()[:], 1.0), ms=True)
    sch.barrier()
    gcol = lambda c: small.ap()[:, c:c + 1]
    bcol = lambda c: small.ap()[:, C + c:C + c + 1]
    row = lambda r, tt=None, w=T, n=512: (
        rows.ap()[:, r * T:r * T + w] if tt is None
        else rows.ap()[:, r * T + n * tt:r * T + n * (tt + 1)])

    def fm_stats(src, r0, r1):
        for tt in range(4):
            bs_, bq = 4 + (tt % 2) * 2, 5 + (tt % 2) * 2
            mqs = {}
            t_mm = t_mq = None
            for c in range(C):
                a = c % 2
                dd = list(ctx.ps_tok[bs_]) if c == 0 else []
                t_mm = sch.op("T", lambda t, c=c, tt=tt, bs_=bs_: t.matmul(
                    ctx.bank(bs_, [1, 512]), ones_bf.ap()[:],
                    src.ap()[:, c, 512 * tt:512 * (tt + 1)],
                    start=(c == 0), stop=(c == C - 1)), deps=dd,
                    ms=(c == C - 1))
                t_sq = sch.op("S", lambda s, c=c, tt=tt, a=a: s.activation(
                    sq[a].ap()[:], src.ap()[:, c, 512 * tt:512 * (tt + 1)],
                    AF.Square), deps=[mqs[c - 2]] if c >= 2 else [], ms=True)
                dd = list(ctx.ps_tok[bq]) if c == 0 else []
                t_mq = sch.op("T", lambda t, c=c, bq=bq, a=a: t.matmul(
                    ctx.bank(bq, [1, 512]), ones_bf.ap()[:], sq[a].ap()[:],
                    start=(c == 0), stop=(c == C - 1)),
                    deps=[t_sq] + dd, ms=True)
                mqs[c] = t_mq
            e1 = sch.op("S", lambda s, tt=tt, bs_=bs_, r0=r0: s.copy(
                row(r0, tt), ctx.bank(bs_, [1, 512])), deps=[t_mm], ms=True)
            e2 = sch.op("S", lambda s, tt=tt, bq=bq, r1=r1: s.copy(
                row(r1, tt), ctx.bank(bq, [1, 512])), deps=[t_mq], ms=True)
            ctx.ps_tok[bs_] = (e1,)
            ctx.ps_tok[bq] = (e2,)

    fm_stats(xn, 0, 1)
    sch.barrier()
    _row_chain(ctx, [
        ("V", lambda v: v.tensor_scalar(row(2), row(0), 1.0 / D, None,
                                        ALU.mult)),
        ("V", lambda v: v.tensor_mul(row(3), row(2), row(2))),
        ("V", lambda v: v.tensor_scalar(row(4), row(1), 1.0 / D, None,
                                        ALU.mult)),
        ("V", lambda v: v.tensor_sub(row(4), row(4), row(3))),
        ("V", lambda v: v.tensor_scalar(row(4), row(4), float(LN_EPS), None,
                                        ALU.add)),
        ("S", lambda s: s.activation(row(5), row(4), AF.Ln)),
        ("V", lambda v: v.tensor_scalar(row(5), row(5), -0.5, None, ALU.mult)),
        ("S", lambda s: s.activation(row(6), row(5), AF.Exp)),
    ])
    sch.barrier()
    for tt in range(4):
        bm, br = (tt % 2) * 2, (tt % 2) * 2 + 1
        dd = list(ctx.ps_tok[bm]) + list(ctx.ps_tok[br])
        t_bm = sch.op("T", lambda t, tt=tt, bm=bm: t.matmul(
            ctx.bank(bm), ones_f.ap()[:], row(2, tt), start=True, stop=True),
            deps=dd, ms=True)
        t_br = sch.op("T", lambda t, tt=tt, br=br: t.matmul(
            ctx.bank(br), ones_f.ap()[:], row(6, tt), start=True, stop=True),
            ms=True)
        apl = []
        for c in range(C):
            a = c % 2
            w1t = sch.op("V", lambda v, c=c, tt=tt, bm=bm, a=a: v.tensor_sub(
                tm1[a].ap()[:], xn.ap()[:, c, 512 * tt:512 * (tt + 1)],
                ctx.bank(bm)), deps=[t_bm, t_br], ms=True)
            sch.op("V", lambda v, a=a, br=br: v.tensor_mul(
                tm1[a].ap()[:], tm1[a].ap()[:], ctx.bank(br)), ms=True)
            w3 = sch.op("V", lambda v, c=c, tt=tt, a=a: v.tensor_scalar(
                xnl.ap()[:, c, 512 * tt:512 * (tt + 1)], tm1[a].ap()[:],
                gcol(c), bcol(c), ALU.mult, ALU.add), ms=True)
            apl.append(w3)
        ctx.ps_tok[bm] = (apl[-1],)
        ctx.ps_tok[br] = (apl[-1],)
    sch.barrier()
    sb.free_(xn)

    mu = sb.alloc("mu", [128, C, ES], F32)
    muu = sb.alloc("muu", [128, C, ES], BF16)
    invb = sb.alloc("invb", [128, T], BF16)
    t_mu = sch.dma("Q", lambda q: q.dma_start(
        out=mu.ap().rearrange("p a b -> p (a b)"), in_=mu_d[:]))
    for c in range(C):
        a = c % 2
        t_s = sch.op("S", lambda s, c=c, a=a: s.activation(
            sq[a].ap()[:, 0:ES], mu.ap()[:, c, :], AF.Square),
            deps=[t_mu], ms=True)
        sch.op("T", lambda t, c=c, a=a: t.matmul(
            ctx.bank(6, [1, ES]), ones_bf.ap()[:], sq[a].ap()[:, 0:ES],
            start=(c == 0), stop=(c == C - 1)),
            deps=[t_s] + (list(ctx.ps_tok[6]) if c == 0 else []), ms=True)
    t_ev = sch.op("S", lambda s: s.copy(row(0, None, ES), ctx.bank(6, [1, ES])),
                  deps=[("T", sch.ms["T"])], ms=True)
    ctx.ps_tok[6] = (t_ev,)
    _row_chain(ctx, [
        ("S", lambda s: s.activation(row(1, None, ES), row(0, None, ES),
                                     AF.Ln)),
        ("V", lambda v: v.tensor_scalar(row(1, None, ES), row(1, None, ES),
                                        0.5, None, ALU.mult)),
        ("S", lambda s: s.activation(row(2, None, ES), row(1, None, ES),
                                     AF.Exp)),
        ("V", lambda v: v.tensor_scalar(row(2, None, ES), row(2, None, ES),
                                        1e-12, None, ALU.max)),
        ("V", lambda v: v.reciprocal(row(3, None, ES), row(2, None, ES))),
    ])
    sch.barrier()
    t_bc = sch.op("T", lambda t: t.matmul(
        ctx.bank(6, [128, ES]), ones_f.ap()[:], row(3, None, ES),
        start=True, stop=True), ms=True)
    mu_tok = []
    for c in range(C):
        mu_tok.append(sch.op("V", lambda v, c=c: v.tensor_mul(
            muu.ap()[:, c, :], mu.ap()[:, c, :], ctx.bank(6, [128, ES])),
            deps=[t_bc], ms=True))
    ctx.ps_tok[6] = tuple(mu_tok[-1:])
    fm_stats(xnl, 4, 5)
    sch.barrier()
    _row_chain(ctx, [
        ("S", lambda s: s.activation(row(6), row(5), AF.Ln)),
        ("V", lambda v: v.tensor_scalar(row(6), row(6), 0.5, None, ALU.mult)),
        ("S", lambda s: s.activation(row(7), row(6), AF.Exp)),
        ("V", lambda v: v.tensor_scalar(row(7), row(7), 1e-12, None, ALU.max)),
        ("V", lambda v: v.reciprocal(row(6), row(7))),
        ("V", lambda v: v.tensor_scalar(row(6), row(6), scl.ap()[0:1, 0:1],
                                        None, ALU.mult)),
    ])
    sch.barrier()
    for tt in range(4):
        b = tt % 2
        t_b = sch.op("T", lambda t, tt=tt, b=b: t.matmul(
            ctx.bank(b), ones_f.ap()[:], row(6, tt), start=True, stop=True),
            deps=list(ctx.ps_tok[b]), ms=True)
        tok = sch.op("S", lambda s, tt=tt, b=b: s.copy(
            invb.ap()[:, 512 * tt:512 * (tt + 1)], ctx.bank(b)),
            deps=[t_b], ms=True)
        ctx.ps_tok[b] = (tok,)
    sch.barrier()
    sb.free_(mu)

    lgf = sb.alloc("lgf", [128, 2, T], F32)
    for m2 in range(2):
        for tt in range(4):
            b = (m2 * 4 + tt) % 2
            t_mm = None
            for kc in range(C):
                dd = (list(ctx.ps_tok[b]) if kc == 0 else [])
                t_mm = sch.op("T", lambda t, m2=m2, tt=tt, kc=kc, b=b:
                              t.matmul(
                    ctx.bank(b), muu.ap()[:, kc, 128 * m2:128 * (m2 + 1)],
                    xnl.ap()[:, kc, 512 * tt:512 * (tt + 1)],
                    start=(kc == 0), stop=(kc == C - 1)), deps=dd,
                    ms=(kc == C - 1))
            tok = sch.op("V", lambda v, m2=m2, tt=tt, b=b: v.tensor_mul(
                lgf.ap()[:, m2, 512 * tt:512 * (tt + 1)], ctx.bank(b),
                invb.ap()[:, 512 * tt:512 * (tt + 1)]), deps=[t_mm], ms=True)
            ctx.ps_tok[b] = (tok,)
    sch.barrier()

    dspf = sb.alloc("dspf", [128, 2, T], BF16)
    dtmp = sb.alloc("dtmp", [128, T], F32)
    cols = sb.alloc("cols", [128, 8], F32)
    for m2 in range(2):
        t1 = sch.op("V", lambda v, m2=m2: v.reduce_max(
            cols.ap()[:, 0:1], lgf.ap()[:, m2, :], axis=AX.X), ms=True)
        t2 = sch.op("V", lambda v: v.tensor_scalar(
            cols.ap()[:, 1:2], cols.ap()[:, 0:1], -1.0, None, ALU.mult),
            sdeps=[t1], ms=True)
        t3 = sch.op("S", lambda s, m2=m2: s.activation(
            dtmp.ap()[:], lgf.ap()[:, m2, :], AF.Exp,
            bias=cols.ap()[:, 1:2]), deps=[t2], ms=True)
        t4 = sch.op("V", lambda v: v.reduce_sum(
            cols.ap()[:, 2:3], dtmp.ap()[:], axis=AX.X), deps=[t3], ms=True)
        t5 = sch.op("V", lambda v: v.reciprocal(
            cols.ap()[:, 3:4], cols.ap()[:, 2:3]), sdeps=[t4], ms=True)
        t6 = sch.op("V", lambda v, m2=m2: v.tensor_scalar(
            dspf.ap()[:, m2, :], dtmp.ap()[:], cols.ap()[:, 3:4], None,
            ALU.mult), sdeps=[t5], ms=True)
    sch.barrier()
    dspt = sb.alloc("dspt", [128, TT, ES], BF16)
    lgt = sb.alloc("lgt", [128, TT, ES], F32)
    for it in range(TT):
        for m2 in range(2):
            b = (it * 2 + m2) % 2
            t_tr = sch.op("T", lambda t, it=it, m2=m2, b=b: t.transpose(
                ctx.bank(b, [128, 128], BF16),
                dspf.ap()[:, m2, 128 * it:128 * (it + 1)], idb.ap()[:]),
                deps=list(ctx.ps_tok[b]), ms=True)
            tok = sch.op("S", lambda s, it=it, m2=m2, b=b: s.copy(
                dspt.ap()[:, it, 128 * m2:128 * (m2 + 1)],
                ctx.bank(b, [128, 128], BF16)), deps=[t_tr], ms=True)
            ctx.ps_tok[b] = (tok,)
            b2_ = 2 + (it * 2 + m2) % 2
            t_tr2 = sch.op("T", lambda t, it=it, m2=m2, b2_=b2_: t.transpose(
                ctx.bank(b2_, [128, 128]),
                lgf.ap()[:, m2, 128 * it:128 * (it + 1)], idf.ap()[:]),
                deps=list(ctx.ps_tok[b2_]), ms=True)
            tok2 = sch.op("S", lambda s, it=it, m2=m2, b2_=b2_: s.copy(
                lgt.ap()[:, it, 128 * m2:128 * (m2 + 1)],
                ctx.bank(b2_, [128, 128])), deps=[t_tr2], ms=True)
            ctx.ps_tok[b2_] = (tok2,)
    sch.barrier()
    sb.free_(lgf)
    sb.free_(dspf)
    sb.free_(dtmp)
    cmbt = sb.alloc("cmbt", [128, TT, ES], BF16)
    ctmp = sb.alloc("ctmp", [128, ES], F32)
    for it in range(TT):
        t1 = sch.op("V", lambda v, it=it: v.reduce_max(
            cols.ap()[:, 0:1], lgt.ap()[:, it, :], axis=AX.X), ms=True)
        t2 = sch.op("V", lambda v: v.tensor_scalar(
            cols.ap()[:, 1:2], cols.ap()[:, 0:1], -1.0, None, ALU.mult),
            sdeps=[t1], ms=True)
        t3 = sch.op("S", lambda s, it=it: s.activation(
            ctmp.ap()[:], lgt.ap()[:, it, :], AF.Exp,
            bias=cols.ap()[:, 1:2]), deps=[t2], ms=True)
        t4 = sch.op("V", lambda v: v.reduce_sum(
            cols.ap()[:, 2:3], ctmp.ap()[:], axis=AX.X), deps=[t3], ms=True)
        t5 = sch.op("V", lambda v: v.reciprocal(
            cols.ap()[:, 3:4], cols.ap()[:, 2:3]), sdeps=[t4], ms=True)
        t6 = sch.op("V", lambda v, it=it: v.tensor_scalar(
            cmbt.ap()[:, it, :], ctmp.ap()[:], cols.ap()[:, 3:4], None,
            ALU.mult), sdeps=[t5], ms=True)
    sch.barrier()
    sb.free_(lgt)
    sb.free_(ctmp)
    cmbf = sb.alloc("cmbf", [128, 2, T], BF16)
    for it in range(TT):
        for m2 in range(2):
            b = (it * 2 + m2) % 2
            t_tr = sch.op("T", lambda t, it=it, m2=m2, b=b: t.transpose(
                ctx.bank(b, [128, 128], BF16),
                cmbt.ap()[:, it, 128 * m2:128 * (m2 + 1)], idb.ap()[:]),
                deps=list(ctx.ps_tok[b]), ms=True)
            tok = sch.op("S", lambda s, it=it, m2=m2, b=b: s.copy(
                cmbf.ap()[:, m2, 128 * it:128 * (it + 1)],
                ctx.bank(b, [128, 128], BF16)), deps=[t_tr], ms=True)
            ctx.ps_tok[b] = (tok,)
    sch.barrier()
    sb.free_(cmbt)

    xnlt = sb.alloc("xnlt", [128, TT, D], BF16)
    for it in range(TT):
        for c in range(C):
            b = (it * C + c) % 2
            t_tr = sch.op("T", lambda t, it=it, c=c, b=b: t.transpose(
                ctx.bank(b, [128, 128], BF16),
                xnl.ap()[:, c, 128 * it:128 * (it + 1)], idb.ap()[:]),
                deps=list(ctx.ps_tok[b]), ms=True)
            tok = sch.op("S", lambda s, it=it, c=c, b=b: s.copy(
                xnlt.ap()[:, it, 128 * c:128 * (c + 1)],
                ctx.bank(b, [128, 128], BF16)), deps=[t_tr], ms=True)
            ctx.ps_tok[b] = (tok,)
    sch.barrier()
    sb.free_(xnl)
    slin = sb.alloc("slin", [128, C, ES], BF16)
    for mm in range(C):
        b = 2 + mm % 2
        t_mm = None
        for it in range(TT):
            dd = (list(ctx.ps_tok[b]) if it == 0 else [])
            t_mm = sch.op("T", lambda t, mm=mm, it=it, b=b: t.matmul(
                ctx.bank(b, [128, ES]), xnlt.ap()[:, it, 128 * mm:128 * (mm + 1)],
                dspt.ap()[:, it, :], start=(it == 0), stop=(it == TT - 1)),
                deps=dd, ms=(it == TT - 1))
        tok = sch.op("S", lambda s, mm=mm, b=b: s.copy(
            slin.ap()[:, mm, :], ctx.bank(b, [128, ES])), deps=[t_mm], ms=True)
        ctx.ps_tok[b] = (tok,)
    sch.barrier()
    sb.free_(xnlt)
    sb.free_(dspt)

    w1s = [sb.alloc("w1s", [128, C, NH], BF16) for _ in range(2)]
    w2s = [sb.alloc("w2s", [128, 2, D], BF16) for _ in range(2)]
    wst = sb.alloc("wst", [128, 2 * D], F32)
    brow = sb.alloc("brow", [1, NH + D], F32)
    browb = sb.alloc("browb", [1, NH + D], BF16)
    onesr16 = sb.alloc("onesr16", [1, 16], BF16)
    t_o16 = sch.op("V", lambda v: v.memset(onesr16.ap()[:], 1.0), ms=True)
    hsb = sb.alloc("hsb", [128, 2, NS], BF16)
    soall = sb.alloc("soall", [16, NE, D], BF16)
    sel16 = sb.alloc("sel16", [128, 8, 16], BF16)
    sch.dma("Q", lambda q: q.dma_start(
        out=sel16.ap().rearrange("p a b -> p (a b)"), in_=s16_d[:]))
    wtoks = {}
    btoks = {}
    st_use = [None]

    def load_we(e, buf):
        sd = [st_use[0]] if st_use[0] else []
        l1 = sch.dma("Q", lambda q, e=e: q.dma_start(
            out=wst.ap()[:], in_=w1_d[e].rearrange("p a b -> p (a b)")),
            deps=sd)
        c1 = sch.op("V", lambda v, buf=buf: v.tensor_copy(
            w1s[buf].ap().rearrange("p a b -> p (a b)"), wst.ap()[:]),
            deps=[l1] + ([wtoks.get(("u1", buf))] if wtoks.get(("u1", buf))
                         else []), ms=True)
        l2 = sch.dma("Q", lambda q, e=e: q.dma_start(
            out=wst.ap()[:], in_=w2_d[e].rearrange("p a b -> p (a b)")),
            deps=[c1])
        c2 = sch.op("V", lambda v, buf=buf: v.tensor_copy(
            w2s[buf].ap().rearrange("p a b -> p (a b)"), wst.ap()[:]),
            deps=[l2] + ([wtoks.get(("u2", buf))] if wtoks.get(("u2", buf))
                         else []), ms=True)
        st_use[0] = c2
        lb1 = sch.dma("Q", lambda q, e=e: q.dma_start(
            out=brow.ap()[:, 0:NH], in_=b1_d[e:e + 1, :]),
            deps=[btoks[e - 1]] if (e - 1) in btoks else [])
        lb2 = sch.dma("Q", lambda q, e=e: q.dma_start(
            out=brow.ap()[:, NH:NH + D], in_=b2_d[e:e + 1, :]))
        cb = sch.op("V", lambda v: v.tensor_copy(browb.ap()[:], brow.ap()[:]),
                    deps=[lb1, lb2], ms=True)
        return c1, c2, cb

    tok_w = load_we(0, 0)
    hs_war = None
    for e in range(NE):
        buf = e % 2
        c1, c2, cb = tok_w
        for mh in range(2):
            b = 4 + mh
            t_mm = None
            for kc in range(C):
                dd = ([c1] if kc == 0 else []) + \
                    (list(ctx.ps_tok[b]) if kc == 0 else [])
                t_mm = sch.op("T", lambda t, e=e, mh=mh, kc=kc, b=b, buf=buf:
                              t.matmul(
                    ctx.bank(b, [128, 16]),
                    w1s[buf].ap()[:, kc, 128 * mh:128 * (mh + 1)],
                    slin.ap()[:, kc, 16 * e:16 * (e + 1)],
                    start=(kc == 0), stop=False), deps=dd, ms=False)
            t_mm = sch.op("T", lambda t, e=e, mh=mh, b=b: t.matmul(
                ctx.bank(b, [128, 16]),
                browb.ap()[0:1, 128 * mh:128 * (mh + 1)], onesr16.ap()[:],
                start=False, stop=True), deps=[cb, t_o16], ms=True)
            tok = sch.op("S", lambda s, mh=mh, b=b: s.activation(
                hsb.ap()[:, mh, :], ctx.bank(b, [128, 16]), AF.Gelu),
                deps=[t_mm] + ([hs_war] if hs_war else []), ms=True)
            ctx.ps_tok[b] = (tok,)
        wtoks[("u1", buf)] = t_mm
        for nn in range(2):
            b = 6 + nn
            t_mm = None
            for kh in range(2):
                dd = ([c2, ("S", sch.ms["S"])] if kh == 0 else []) + \
                    (list(ctx.ps_tok[b]) if kh == 0 else [])
                t_mm = sch.op("T", lambda t, e=e, nn=nn, kh=kh, b=b, buf=buf:
                              t.matmul(
                    ctx.bank(b, [16, 512]), hsb.ap()[:, kh, :],
                    w2s[buf].ap()[:, kh, 512 * nn:512 * (nn + 1)],
                    start=(kh == 0), stop=False), deps=dd, ms=False)
            t_mm = sch.op("T", lambda t, e=e, nn=nn, b=b: t.matmul(
                ctx.bank(b, [16, 512]), onesr16.ap()[:],
                browb.ap()[0:1, NH + 512 * nn:NH + 512 * (nn + 1)],
                start=False, stop=True), deps=[cb], ms=True)
            tok = sch.op("S", lambda s, e=e, nn=nn, b=b: s.copy(
                soall.ap()[0:16, e, 512 * nn:512 * (nn + 1)],
                ctx.bank(b, [16, 512])), deps=[t_mm], ms=True)
            ctx.ps_tok[b] = (tok,)
        wtoks[("u2", buf)] = t_mm
        hs_war = t_mm
        btoks[e] = tok
        if e + 1 < NE:
            tok_w = load_we(e + 1, 1 - buf)
    sch.barrier()
    sb.free_(slin)
    for t_ in w1s + w2s + [wst, hsb]:
        sb.free_(t_)

    call = sb.alloc("call", [16, NE, 512], BF16)
    gtmp = [sb.alloc("gtmp", [128, 512], F32) for _ in range(2)]
    gt_use = [None, None]
    for tt in range(4):
        exs = []
        for e in range(NE):
            b = e % 2
            dd = list(ctx.ps_tok[b])
            t_mm = sch.op("T", lambda t, e=e, tt=tt, b=b: t.matmul(
                ctx.bank(b, [16, 512]), sel16.ap()[:, e % 8, :],
                cmbf.ap()[:, e // 8, 512 * tt:512 * (tt + 1)],
                start=True, stop=True), deps=dd, ms=True)
            tok = sch.op("S", lambda s, e=e, b=b: s.copy(
                call.ap()[0:16, e, :], ctx.bank(b, [16, 512])),
                deps=[t_mm], ms=True)
            ctx.ps_tok[b] = (tok,)
            exs.append(tok)
        for mo in range(C):
            b = 2 + mo % 2
            t_mm = None
            for e in range(NE):
                dd = ([exs[-1]] if e == 0 else []) + \
                    (list(ctx.ps_tok[b]) if e == 0 else [])
                t_mm = sch.op("T", lambda t, mo=mo, tt=tt, e=e, b=b: t.matmul(
                    ctx.bank(b), soall.ap()[0:16, e, 128 * mo:128 * (mo + 1)],
                    call.ap()[0:16, e, :], start=(e == 0), stop=(e == NE - 1)),
                    deps=dd, ms=(e == NE - 1))
            a = mo % 2
            t_ev = sch.op("V", lambda v, a=a, b=b: v.tensor_copy(
                gtmp[a].ap()[:], ctx.bank(b)),
                deps=[t_mm] + ([gt_use[a]] if gt_use[a] else []), ms=True)
            ctx.ps_tok[b] = (t_ev,)
            gt_use[a] = sch.dma("Q", lambda q, mo=mo, tt=tt, a=a: q.dma_start(
                out=g2_d[:, mo, 512 * tt:512 * (tt + 1)], in_=gtmp[a].ap()[:]),
                deps=[t_ev])
    sch.barrier()
    for t_ in [small, rows, ones_bf, ones_f, idb, idf, scl, muu, invb,
               cols, cmbf, soall, sel16, call, brow, onesr16] + sq + tm1 + \
            gtmp:
        sb.free_(t_)


PHASES = {"ln0": phase_ln0, "mprep": phase_mprep, "moescan": phase_moescan,
          "mout": phase_mout, "combine1": phase_combine1,
          "combine2": phase_combine2, "l1attn": phase_l1attn,
          "l1ffn": phase_l1ffn}


def build(cfg):
    nc = bass.Bass("TRN2", target_bir_lowering=False)
    ctx = Ctx(nc, cfg)
    for ph in cfg["phases"]:
        PHASES[ph](ctx)
    toks = ctx.sch.barrier()
    ctx.sch.op("P", lambda p: p.nop(), deps=toks)
    ctx.sch.build(nc)
    return nc


def _fm(v, inner=None):
    d = v.shape[0]
    c = d // 128
    out = v.reshape(c, 128, *v.shape[1:]).transpose(
        1, 0, *range(2, v.ndim + 1))
    return np.ascontiguousarray(out)


def stage_common(inp):
    f32 = np.float32

    def vec(v):
        return np.ascontiguousarray(np.asarray(v, f32).reshape(-1, 128).T)
    cm = {}
    cm["gate_w"] = _fm(inp["L0_ffn_gate_w"].astype(f32))
    cm["ident_f32"] = np.eye(128, dtype=f32)
    w_in = np.asarray(inp["L0_attn_W_in"], f32)
    cm["w_in_ssm"] = _fm(w_in[:, :D])
    cm["w_in_gate"] = _fm(w_in[:, D:])
    cm["w_dt"] = _fm(np.asarray(inp["L0_attn_W_dt"], f32))
    cm["w_bc"] = _fm(np.asarray(inp["L0_attn_W_bc"], f32))
    cm["b_in_ssm"] = vec(inp["L0_attn_b_in"][:D])
    cm["b_in_gate"] = vec(inp["L0_attn_b_in"][D:])
    cm["b_dt"] = vec(inp["L0_attn_b_dt"])
    cm["conv_w"] = _fm(np.asarray(inp["L0_attn_conv_w"], f32)[:, 0, :])
    cm["conv_b"] = vec(inp["L0_attn_conv_b"])
    cm["b_bc"] = np.ascontiguousarray(
        np.asarray(inp["L0_attn_b_bc"], f32).reshape(32, 1))
    cm["A_log"] = _fm(np.asarray(inp["L0_attn_A_log"], f32))
    cm["w_out"] = _fm(np.asarray(inp["L0_attn_W_out"], f32))
    cm["b_out"] = vec(inp["L0_attn_b_out"])
    cm["ln_g"] = vec(inp["L0_attn_ln_g"])
    cm["ln_b"] = vec(inp["L0_attn_ln_b"])
    cm["Dvec"] = vec(inp["L0_attn_D"])
    cm["ident_bf"] = np.eye(128, dtype=BF)

    def exp_fm(w):
        w = np.asarray(w, f32)
        E, d_, f_ = w.shape
        return np.ascontiguousarray(
            w.reshape(E, d_ // 128, 128, f_).transpose(0, 2, 1, 3))
    cm["exp_gate"] = exp_fm(inp["L0_ffn_exp_gate"])
    cm["exp_up"] = exp_fm(inp["L0_ffn_exp_up"])
    cm["exp_down"] = exp_fm(inp["L0_ffn_exp_down"])
    cm["sh_gate"] = exp_fm(inp["L0_ffn_sh_gate"])
    cm["sh_up"] = exp_fm(inp["L0_ffn_sh_up"])
    cm["sh_down"] = exp_fm(inp["L0_ffn_sh_down"])
    cm["wqT"] = _fm(np.ascontiguousarray(np.asarray(inp["L1_attn_Wq"], f32).T))
    cm["wkT"] = _fm(np.ascontiguousarray(np.asarray(inp["L1_attn_Wk"], f32).T))
    cm["wvT"] = _fm(np.ascontiguousarray(np.asarray(inp["L1_attn_Wv"], f32).T))
    cm["woT"] = _fm(np.ascontiguousarray(np.asarray(inp["L1_attn_Wo"], f32).T))
    cm["bq"] = vec(inp["L1_attn_bq"])
    cm["bk"] = vec(inp["L1_attn_bk"])
    cm["bo"] = vec(inp["L1_attn_bo"])
    cm["bk_row"] = np.asarray(inp["L1_attn_bk"], f32).reshape(1, D)
    cm["bv_row"] = np.asarray(inp["L1_attn_bv"], f32).reshape(1, D)
    cm["maskUT"] = np.triu(np.ones((128, 128), dtype=BF))
    cm["mu_fm"] = _fm(np.asarray(inp["L1_ffn_mu"], f32).reshape(D, 256))
    cm["l1_gamma"] = vec(inp["L1_ffn_ln_gamma"])
    cm["l1_beta"] = vec(inp["L1_ffn_ln_beta"])
    cm["l1_scale"] = np.asarray(inp["L1_ffn_scale"], f32).reshape(1, 1)
    w1 = np.asarray(inp["L1_ffn_w1"], f32)
    cm["w1_l1"] = np.ascontiguousarray(
        w1.reshape(16, 8, 128, 256).transpose(0, 2, 1, 3))
    cm["b1_l1"] = np.asarray(inp["L1_ffn_b1"], f32)
    w2 = np.asarray(inp["L1_ffn_w2"], f32)
    cm["w2_l1"] = np.ascontiguousarray(
        w2.reshape(16, 2, 128, 1024).transpose(0, 2, 1, 3))
    cm["b2_l1"] = np.asarray(inp["L1_ffn_b2"], f32)
    s16 = np.zeros((128, 8, 16), dtype=BF)
    for j in range(8):
        for m in range(16):
            s16[16 * j + m, j, m] = 1
    cm["sel16"] = np.ascontiguousarray(s16.reshape(128, 8 * 16))
    return cm


def stage_inputs(inputs):
    inputs = {k: np.asarray(v) for k, v in inputs.items()}
    cm = stage_common(inputs)
    x = np.ascontiguousarray(inputs["L0_attn_x"].astype(np.float32))
    in_maps = []
    for b in range(8):
        m = dict(cm)
        m["x"] = x[b]
        in_maps.append(m)
    return in_maps


ALL_INS = [
    "x", "gate_w", "ident_f32", "ident_bf", "w_in_ssm", "w_in_gate", "w_dt",
    "w_bc", "b_in_ssm", "b_in_gate", "b_dt", "conv_w", "conv_b", "b_bc",
    "A_log", "Dvec", "w_out", "b_out", "ln_g", "ln_b",
    "exp_gate", "exp_up", "exp_down", "sh_gate", "sh_up", "sh_down",
    "wqT", "wkT", "wvT", "woT", "bq", "bk", "bo", "bk_row", "bv_row",
    "maskUT", "mu_fm", "l1_gamma", "l1_beta", "l1_scale", "w1_l1", "b1_l1",
    "w2_l1", "b2_l1", "sel16",
]
ALL_PHASES = ["ln0", "mprep", "moescan", "mout", "combine1",
              "l1attn", "l1ffn", "combine2"]


def kernel(**inputs):
    cfg = {"phases": ALL_PHASES, "ins": ALL_INS, "outs": ["out"]}
    nc = build(cfg)
    in_maps = stage_inputs(inputs)
    res = run_bass_kernel_spmd(nc, in_maps, core_ids=list(range(8)))
    return np.stack([r["out"] for r in res.results]).astype(np.float32)


# version 29
# speedup vs baseline: 1.0111x; 1.0111x over previous
import numpy as np
import ml_dtypes
import concourse.bass as bass
import concourse.mybir as mybir
from concourse.bass_utils import run_bass_kernel_spmd

F32 = mybir.dt.float32
BF16 = mybir.dt.bfloat16
AF = mybir.ActivationFunctionType
ALU = mybir.AluOpType
AX = mybir.AxisListType
BF = ml_dtypes.bfloat16

D = 1024
T = 2048
C = 8
TT = 16
NST = 16
DCONV = 4
F_FF = 1408
FC = 11
NEXP = 8
NSH = 2
NHEAD = 16
DH = 64
LN_EPS = 1e-5

SB_BASE = 16896
SBUF_BYTES = 229376 - 512 - SB_BASE


class _SBWrap:
    __slots__ = ("t", "_region")

    def __init__(self, t, region):
        self.t = t
        self._region = region

    def ap(self):
        return self.t.ap()


class SBAlloc:

    def __init__(self, nc):
        self.nc = nc
        self.free = [(SB_BASE, SBUF_BYTES)]
        self.high = 0
        self.n = 0

    def alloc(self, name, shape, dtype):
        nbytes = int(np.prod(shape[1:])) * mybir.dt.size(dtype)
        nbytes = (nbytes + 31) & ~31
        for i, (off, sz) in enumerate(self.free):
            if sz >= nbytes:
                self.free[i] = (off + nbytes, sz - nbytes)
                if self.free[i][1] == 0:
                    self.free.pop(i)
                self.n += 1
                t = self.nc.alloc_sbuf_tensor_at(
                    f"{name}_{self.n}", shape, dtype, offset=off)
                t = _SBWrap(t, (off, nbytes))
                self.high = max(self.high, off + nbytes)
                return t
        raise MemoryError(f"SBUF alloc failed for {name} {shape} ({nbytes}B); "
                          f"free={self.free}")

    def free_(self, t):
        off, nbytes = t._region
        self.free.append((off, nbytes))
        self.free.sort()
        merged = []
        for o, s in self.free:
            if merged and merged[-1][0] + merged[-1][1] == o:
                merged[-1] = (merged[-1][0], merged[-1][1] + s)
            else:
                merged.append((o, s))
        self.free = [tuple(x) for x in merged]


class Sched:
    COMPUTE = ("T", "V", "S", "P")
    QUEUES = {"Q": "QD", "P": "PD", "S": "SD"}

    def __init__(self):
        self.q = {e: [] for e in "TVSPQ"}
        self.ms = {e: 0 for e in self.COMPUTE}
        self.dmacnt = {k: 0 for k in self.QUEUES.values()}
        self.pending = {e: [] for e in "TVSPQ"}
        self.active = {e: False for e in "TVSPQ"}

    def op(self, e, fn, deps=(), ms=False, sdeps=()):
        deps = list(deps) + self.pending[e]
        sdeps = list(sdeps)
        self.pending[e] = []
        self.active[e] = True
        tok = None
        if ms:
            self.ms[e] += 1
            tok = (e, self.ms[e])

        def emit(eng, sems, waited, own):
            for (sk, val) in deps:
                if sk == own:
                    continue
                if waited.get(sk, 0) >= val:
                    continue
                eng.wait_ge(sems[sk], val)
                waited[sk] = val
            for (sk, val) in sdeps:
                if waited.get(sk, 0) >= val:
                    continue
                eng.wait_ge(sems[sk], val)
                waited[sk] = val
            inst = fn(eng)
            if tok is not None:
                inst.then_inc(sems[e], 1)
        self.q[e].append(emit)
        return tok

    def dma(self, e, fn, deps=(), fence=True):
        dk = self.QUEUES[e]
        deps = list(deps) + self.pending[e]
        self.pending[e] = []
        self.active[e] = True
        prev = self.dmacnt[dk] if fence else 0
        self.dmacnt[dk] += 16
        tok = (dk, self.dmacnt[dk])

        def emit(eng, sems, waited, own):
            for (sk, val) in [(dk, prev)] + list(deps):
                if sk == own:
                    continue
                if val <= 0 or waited.get(sk, 0) >= val:
                    continue
                eng.wait_ge(sems[sk], val)
                waited[sk] = val
            fn(eng).then_inc(sems[dk], 16)
        self.q[e].append(emit)
        return tok

    def barrier(self):
        toks = []
        for e in self.COMPUTE:
            if self.active[e]:
                self.ms[e] += 1
                v = self.ms[e]

                def emit(eng, sems, waited, own, _e=e, _v=v):
                    eng.drain().then_inc(sems[_e], 1)
                self.q[e].append(emit)
                toks.append((e, v))
        for qe, dk in self.QUEUES.items():
            if self.dmacnt[dk] > 0:
                toks.append((dk, self.dmacnt[dk]))
        for e in "TVSPQ":
            self.pending[e] = list(toks)
            self.active[e] = False
        return toks

    def build(self, nc):
        sems = {}
        ctxs = []
        for key in list(self.COMPUTE) + list(self.QUEUES.values()):
            ctxs.append(nc.semaphore(f"sem_{key}"))
        handles = [c.__enter__() for c in ctxs]
        for key, h in zip(list(self.COMPUTE) + list(self.QUEUES.values()), handles):
            sems[key] = h
        blk_ctx = nc.Block()
        blk = blk_ctx.__enter__()
        own_of = {"T": "T", "V": "V", "S": "S", "P": "P", "Q": None}

        def mk_runner(e):
            def run(eng):
                waited = {}
                for emit in self.q[e]:
                    emit(eng, sems, waited, own_of[e])
            return run
        if self.q["T"]:
            blk.tensor(mk_runner("T"))
        if self.q["V"]:
            blk.vector(mk_runner("V"))
        if self.q["S"]:
            blk.scalar(mk_runner("S"))
        if self.q["P"]:
            blk.gpsimd(mk_runner("P"))
        if self.q["Q"]:
            blk.sync(mk_runner("Q"))
        blk_ctx.__exit__(None, None, None)
        for c in ctxs:
            c.__exit__(None, None, None)


class Ctx:
    def __init__(self, nc, cfg):
        self.nc = nc
        self.cfg = cfg
        self.sch = Sched()
        self.sb = SBAlloc(nc)
        self.dram = {}
        self.psum = [nc.alloc_psum_tensor(f"pb{i}", [128, 512], F32)
                     for i in range(8)]
        self.ps_tok = [()] * 8
        self.const = {}

    def dram_t(self, name, shape, dtype):
        if name in self.dram:
            return self.dram[name]
        kind = "Internal"
        if name in self.cfg.get("ins", ()):
            kind = "ExternalInput"
        elif name in self.cfg.get("outs", ()):
            kind = "ExternalOutput"
        t = self.nc.dram_tensor(name, shape, dtype, kind=kind)
        self.dram[name] = t
        return t

    def bank(self, i, shape=None, dtype=F32):
        ap = self.psum[i].ap()
        if dtype != F32:
            ap = ap.bitcast(dtype)
        if shape is not None:
            ap = ap[:shape[0], :int(np.prod(shape[1:]))]
            if len(shape) > 2:
                ap = ap.rearrange(
                    "p (a b) -> p a b", a=shape[1])
        return ap


def phase_ln0(ctx):
    nc, sch, sb = ctx.nc, ctx.sch, ctx.sb
    x_in = ctx.dram_t("x", [T, D], F32).ap()
    xn_d = ctx.dram_t("xn_bf", [128, C * T], BF16).ap()
    lg_d = ctx.dram_t("logits_tm", [128, TT * NEXP], F32).ap()
    gate_w_d = ctx.dram_t("gate_w", [128, C, NEXP], F32).ap()
    id_d = ctx.dram_t("ident_f32", [128, 128], F32).ap()

    x0t = [sb.alloc("x0t", [128, D], F32) for _ in range(2)]
    sqb = sb.alloc("sqb", [128, D], F32)
    stat = sb.alloc("stat", [128, 8], F32)
    xn_tm = sb.alloc("xn_tm", [128, D], F32)
    tmpT = [sb.alloc("tmpT", [128, 128], F32) for _ in range(4)]
    xn_fm = sb.alloc("xn_fm", [128, C, T], BF16)
    gate_w = sb.alloc("gate_w", [128, C, NEXP], F32)
    ident = sb.alloc("ident", [128, 128], F32)
    logits_sb = sb.alloc("logits_sb", [128, TT, NEXP], F32)
    wc_st = [sb.alloc("wc_st", [128, 2048], F32) for _ in range(2)]
    wc_bf = [sb.alloc("wc_bf", [128, 2048], BF16) for _ in range(2)]
    wc_pieces = []
    for wname in ("wq", "wk", "wv", "wo"):
        wsrc = ctx.dram_t(wname + "T", [128, C, D], F32).ap().rearrange(
            "p a b -> p (a b)")
        wdst = ctx.dram_t(wname + "bf", [128, C * D], BF16).ap()
        for pc in range(4):
            wc_pieces.append((wsrc, wdst, pc))
    wc_out_tok = [None, None]

    def emit_wcast(idx):
        if idx >= len(wc_pieces):
            return
        wsrc, wdst, pc = wc_pieces[idx]
        a = idx % 2
        dd = [wc_out_tok[a]] if wc_out_tok[a] else []
        ldw = sch.dma("Q", lambda q, wsrc=wsrc, pc=pc, a=a: q.dma_start(
            out=wc_st[a].ap()[:], in_=wsrc[:, 2048 * pc:2048 * (pc + 1)]),
            deps=dd)
        cst = sch.op("V", lambda v, a=a: v.tensor_copy(
            wc_bf[a].ap()[:], wc_st[a].ap()[:]), deps=[ldw], ms=True)
        wc_out_tok[a] = sch.dma("Q", lambda q, wdst=wdst, pc=pc, a=a:
                                q.dma_start(
            out=wdst[:, 2048 * pc:2048 * (pc + 1)], in_=wc_bf[a].ap()[:]),
            deps=[cst])

    sch.dma("Q", lambda q: q.dma_start(out=gate_w.ap()[:], in_=gate_w_d[:]))
    sch.dma("Q", lambda q: q.dma_start(out=ident.ap()[:], in_=id_d[:]))
    ld_tok = [None] * TT
    ld_tok[0] = sch.dma("Q", lambda q: q.dma_start(
        out=x0t[0].ap()[:], in_=x_in[0:128, :]))
    sch.barrier()

    for i in range(TT):
        emit_wcast(i)
        bi = i % 2
        if i + 1 < TT:
            ld_tok[i + 1] = sch.dma("Q", lambda q, i=i: q.dma_start(
                out=x0t[(i + 1) % 2].ap()[:],
                in_=x_in[128 * (i + 1):128 * (i + 2), :]))
        dl = [ld_tok[i]]
        t_sum = sch.op("V", lambda v, bi=bi: v.reduce_sum(
            stat.ap()[:, 0:1], x0t[bi].ap()[:], axis=AX.X), deps=dl, ms=True)
        t_sq0 = sch.op("S", lambda s, bi=bi: s.activation(
            sqb.ap()[:], x0t[bi].ap()[:], AF.Square), deps=dl, ms=True)
        t_sq = sch.op("V", lambda v: v.reduce_sum(
            stat.ap()[:, 1:2], sqb.ap()[:], axis=AX.X), deps=[t_sq0], ms=True)
        t_mean = sch.op("V", lambda v: v.tensor_scalar(
            stat.ap()[:, 2:3], stat.ap()[:, 0:1], 1.0 / D, None, ALU.mult),
            sdeps=[t_sum], ms=True)
        t_msq = sch.op("V", lambda v: v.tensor_mul(
            stat.ap()[:, 3:4], stat.ap()[:, 2:3], stat.ap()[:, 2:3]),
            sdeps=[t_mean], ms=True)
        t_v0 = sch.op("V", lambda v: v.tensor_scalar(
            stat.ap()[:, 4:5], stat.ap()[:, 1:2], 1.0 / D, None, ALU.mult),
            deps=[t_sq], sdeps=[t_sq], ms=True)
        t_var = sch.op("V", lambda v: v.tensor_scalar(
            stat.ap()[:, 5:6], stat.ap()[:, 4:5], stat.ap()[:, 3:4],
            float(LN_EPS), ALU.subtract, ALU.add),
            sdeps=[t_msq, t_v0], ms=True)
        t_ln = sch.op("S", lambda s: s.activation(
            stat.ap()[:, 6:7], stat.ap()[:, 5:6], AF.Ln),
            deps=[t_var], ms=True)
        t_half = sch.op("V", lambda v: v.tensor_scalar(
            stat.ap()[:, 6:7], stat.ap()[:, 6:7], -0.5, None, ALU.mult),
            deps=[t_ln], ms=True)
        t_rstd = sch.op("S", lambda s: s.activation(
            stat.ap()[:, 7:8], stat.ap()[:, 6:7], AF.Exp),
            deps=[t_half], ms=True)
        t_xn = sch.op("V", lambda v, bi=bi: v.tensor_scalar(
            xn_tm.ap()[:], x0t[bi].ap()[:], stat.ap()[:, 2:3],
            stat.ap()[:, 7:8], ALU.subtract, ALU.mult), deps=[t_rstd],
            sdeps=[t_mean], ms=True)
        if ctx.cfg.get("dbg_ln0"):
            dbg = ctx.dram_t("xn_tm_dbg", [T, D], F32).ap()
            sch.dma("Q", lambda q, i=i: q.dma_start(
                out=dbg[128 * i:128 * (i + 1), :], in_=xn_tm.ap()[:]),
                deps=[t_xn])
            dbg2 = ctx.dram_t("stat_dbg", [T, 8], F32).ap()
            sch.dma("Q", lambda q, i=i: q.dma_start(
                out=dbg2[128 * i:128 * (i + 1), :], in_=stat.ap()[:]),
                deps=[t_xn])
        ev = {}
        casts = {}
        for c in range(C):
            pb = c % 2
            tb = c % 4
            t_tr = sch.op("T", lambda t, c=c, pb=pb: t.transpose(
                ctx.bank(pb, [128, 128]), xn_tm.ap()[:, 128 * c:128 * (c + 1)],
                ident.ap()[:]),
                deps=[t_xn] + ([ev[c - 2]] if c >= 2 else []), ms=True)
            t_ev = sch.op("S", lambda s, pb=pb, tb=tb: s.copy(
                tmpT[tb].ap()[:], ctx.bank(pb, [128, 128])),
                deps=[t_tr] + ([casts[c - 4]] if c >= 4 else []), ms=True)
            ev[c] = t_ev
            casts[c] = sch.op("V", lambda v, i=i, c=c, tb=tb: v.tensor_copy(
                xn_fm.ap()[:, c, 128 * i:128 * (i + 1)], tmpT[tb].ap()[:]),
                deps=[t_ev], ms=True)
            sch.op("T", lambda t, i=i, c=c, tb=tb: t.matmul(
                ctx.bank(2, [128, TT, NEXP])[:, i, :], tmpT[tb].ap()[:],
                gate_w.ap()[:, c, :], start=(c == 0), stop=(c == C - 1)),
                deps=[t_ev], ms=True)
        sch.op("S", lambda s, i=i: s.copy(
            logits_sb.ap()[:, i, :], ctx.bank(2, [128, TT, NEXP])[:, i, :]),
            deps=[("T", sch.ms["T"])], ms=True)
        sch.barrier()
    sch.dma("Q", lambda q: q.dma_start(out=xn_d[:], in_=xn_fm.ap()[:]))
    sch.dma("Q", lambda q: q.dma_start(
        out=lg_d[:], in_=logits_sb.ap().rearrange("p a b -> p (a b)")))
    sch.barrier()
    for t in (x0t + [sqb, stat, xn_tm, gate_w, ident, logits_sb, xn_fm]
              + tmpT + wc_st + wc_bf):
        sb.free_(t)


def _load_cast_w(ctx, dname, cchunks, xwidth, name, deps=()):
    sch, sb = ctx.sch, ctx.sb
    half = cchunks // 2
    wd = ctx.dram_t(dname, [128, cchunks, xwidth], F32).ap()
    stage = [sb.alloc(f"{name}_st", [128, half * xwidth], F32)
             for _ in range(2)]
    wbf = sb.alloc(f"{name}_bf", [128, cchunks, xwidth], BF16)
    ld0 = sch.dma("Q", lambda q: q.dma_start(
        out=stage[0].ap()[:],
        in_=wd[:, 0:half, :].rearrange("p a b -> p (a b)")), deps=list(deps))
    ld1 = sch.dma("Q", lambda q: q.dma_start(
        out=stage[1].ap()[:],
        in_=wd[:, half:cchunks, :].rearrange("p a b -> p (a b)")),
        fence=False)
    t0 = sch.op("V", lambda v: v.tensor_copy(
        wbf.ap()[:, 0:half, :].rearrange("p a b -> p (a b)"),
        stage[0].ap()[:]), deps=[ld1], ms=True)
    t1 = sch.op("V", lambda v: v.tensor_copy(
        wbf.ap()[:, half:cchunks, :].rearrange("p a b -> p (a b)"),
        stage[1].ap()[:]), deps=[ld1], ms=True)
    for s in stage:
        sb.free_(s)
    return wbf, t1, None


def phase_mprep(ctx):
    nc, sch, sb = ctx.nc, ctx.sch, ctx.sb
    xn_d = ctx.dram_t("xn_bf", [128, C * T], BF16).ap()
    gg_d = ctx.dram_t("ggate_d", [128, C * T], BF16).ap()
    xc_d = ctx.dram_t("xconv_d", [128, C * T], BF16).ap()
    dt_d = ctx.dram_t("dt_d", [128, C * T], BF16).ap()
    u_d = ctx.dram_t("u_d", [128, C * T], BF16).ap()
    bc_d = ctx.dram_t("bc_d", [32, T], F32).ap()

    xn = sb.alloc("xn", [128, C, T], BF16)
    t_xn = sch.dma("Q", lambda q: q.dma_start(out=xn.ap()[:], in_=xn_d[:]))
    bg_d = ctx.dram_t("b_in_gate", [128, C], F32).ap()
    bs_d = ctx.dram_t("b_in_ssm", [128, C], F32).ap()
    bdt_d = ctx.dram_t("b_dt", [128, C], F32).ap()
    cw_d = ctx.dram_t("conv_w", [128, C, DCONV], F32).ap()
    cb_d = ctx.dram_t("conv_b", [128, C], F32).ap()
    bbc_d = ctx.dram_t("b_bc", [32, 1], F32).ap()
    small = sb.alloc("small", [128, 8 * C], F32)
    cw = sb.alloc("cw", [128, C, DCONV], F32)
    bbc = sb.alloc("bbc", [32, 1], F32)
    t_sm = [sch.dma("Q", lambda q, i=i, dd=dd: q.dma_start(
        out=small.ap()[:, i * C:(i + 1) * C], in_=dd))
        for i, dd in enumerate([bg_d[:], bs_d[:], bdt_d[:], cb_d[:]])]
    t_sm.append(sch.dma("Q", lambda q: q.dma_start(out=cw.ap()[:], in_=cw_d[:])))
    t_sm.append(sch.dma("Q", lambda q: q.dma_start(out=bbc.ap()[:], in_=bbc_d[:])))
    bg = lambda c: small.ap()[:, 0 * C + c:0 * C + c + 1]
    bs = lambda c: small.ap()[:, 1 * C + c:1 * C + c + 1]
    bdt = lambda c: small.ap()[:, 2 * C + c:2 * C + c + 1]
    cb = lambda c: small.ap()[:, 3 * C + c:3 * C + c + 1]

    def fm_matmul_out(wbf, wtok, rhs_tile, rhs_tok, n_mo, evac_fn):
        for mo in range(n_mo):
            for tt in range(4):
                b = (mo * 4 + tt) % 4
                t_mm = None
                for kc in range(C):
                    dd = [wtok, rhs_tok] if kc == 0 else []
                    if kc == 0:
                        dd += list(ctx.ps_tok[b])
                    t_mm = sch.op("T", lambda t, mo=mo, tt=tt, kc=kc, b=b: t.matmul(
                        ctx.bank(b), wbf.ap()[:, kc, 128 * mo:128 * (mo + 1)],
                        rhs_tile.ap()[:, kc, 512 * tt:512 * (tt + 1)],
                        start=(kc == 0), stop=(kc == C - 1)),
                        deps=dd, ms=(kc == C - 1))
                tok = evac_fn(mo, tt, b, [t_mm])
                ctx.ps_tok[b] = (tok,)

    wg, wg_tok, _ = _load_cast_w(ctx, "w_in_gate", C, 1024, "wg")
    if ctx.cfg.get("dbg_wg"):
        wg_d = ctx.dram_t("wg_dbg", [128, C * 1024], BF16).ap()
        sch.dma("Q", lambda q: q.dma_start(
            out=wg_d[:], in_=wg.ap().rearrange("p a b -> p (a b)")),
            deps=[wg_tok])
    ggate = sb.alloc("ggate", [128, C, T], BF16)

    def evac_gate(mo, tt, b, deps):
        return sch.op("S", lambda s, mo=mo, tt=tt, b=b: s.activation(
            ggate.ap()[:, mo, 512 * tt:512 * (tt + 1)], ctx.bank(b)[:, 0:512],
            AF.Silu, bias=bg(mo)), deps=deps + t_sm, ms=True)
    fm_matmul_out(wg, wg_tok, xn, t_xn, C, evac_gate)
    sch.dma("Q", lambda q: q.dma_start(out=gg_d[:], in_=ggate.ap()[:]),
            deps=[("S", sch.ms["S"])])
    sch.barrier()
    sb.free_(ggate)
    sb.free_(wg)

    ws, ws_tok, _ = _load_cast_w(ctx, "w_in_ssm", C, 1024, "ws")
    TP = T + DCONV - 1 + 1
    xsp = sb.alloc("xsp", [128, C, TP], BF16)
    t_z = sch.op("V", lambda v: v.memset(xsp.ap()[:, :, 0:DCONV - 1], 0.0),
                 ms=True)

    def evac_ssm(mo, tt, b, deps):
        return sch.op("V", lambda v, mo=mo, tt=tt, b=b: v.tensor_scalar(
            xsp.ap()[:, mo, 3 + 512 * tt:3 + 512 * (tt + 1)],
            ctx.bank(b)[:, 0:512], bs(mo), None, ALU.add),
            deps=deps + t_sm + [t_z], ms=True)
    fm_matmul_out(ws, ws_tok, xn, t_xn, C, evac_ssm)
    sch.barrier()
    sb.free_(ws)
    sb.free_(xn)

    xconv = sb.alloc("xconv", [128, C, T], BF16)
    acc = [sb.alloc("acc", [128, T], F32) for _ in range(2)]
    acc_use = [None, None]
    for c in range(C):
        a = c % 2
        dd = [acc_use[a]] if acc_use[a] else []
        t0 = sch.op("V", lambda v, c=c, a=a: v.tensor_scalar(
            acc[a].ap()[:], xsp.ap()[:, c, 0:T], cw.ap()[:, c, 0:1], None,
            ALU.mult), deps=dd, ms=True)
        for j in range(1, DCONV):
            t0 = sch.op("V", lambda v, c=c, j=j, a=a: v.scalar_tensor_tensor(
                acc[a].ap()[:], xsp.ap()[:, c, j:j + T], cw.ap()[:, c, j:j + 1],
                acc[a].ap()[:], ALU.mult, ALU.add), sdeps=[t0], ms=True)
        t_cv = sch.op("S", lambda s, c=c, a=a: s.activation(
            xconv.ap()[:, c, :], acc[a].ap()[:], AF.Silu, bias=cb(c)),
            deps=[t0] + t_sm, ms=True)
        acc_use[a] = t_cv
    sch.dma("Q", lambda q: q.dma_start(out=xc_d[:], in_=xconv.ap()[:]),
            deps=[("S", sch.ms["S"])])
    sch.barrier()
    sb.free_(xsp)
    for t in acc:
        sb.free_(t)

    wdt, wdt_tok, _ = _load_cast_w(ctx, "w_dt", C, 1024, "wdt")
    dt = sb.alloc("dt", [128, C, T], BF16)
    sp = [sb.alloc("sp", [128, 512], F32) for _ in range(2)]
    sp_use = [None, None]

    def evac_dt(mo, tt, b, deps):
        a = (mo * 4 + tt) % 2
        dd = deps + t_sm + ([sp_use[a]] if sp_use[a] else [])
        t_e = sch.op("S", lambda s, b=b, a=a, mo=mo: s.activation(
            sp[a].ap()[:], ctx.bank(b)[:, 0:512], AF.Exp, bias=bdt(mo)),
            deps=dd, ms=True)
        t_p = sch.op("V", lambda v, a=a: v.tensor_scalar(
            sp[a].ap()[:], sp[a].ap()[:], 1.0, None, ALU.add),
            deps=[t_e], ms=True)
        t_l = sch.op("S", lambda s, mo=mo, tt=tt, a=a: s.activation(
            dt.ap()[:, mo, 512 * tt:512 * (tt + 1)], sp[a].ap()[:], AF.Ln),
            deps=[t_p], ms=True)
        sp_use[a] = t_l
        return t_e
    fm_matmul_out(wdt, wdt_tok, xconv, ("S", 0), C, evac_dt)
    sch.dma("Q", lambda q: q.dma_start(out=dt_d[:], in_=dt.ap()[:]),
            deps=[("S", sch.ms["S"])])
    sch.barrier()
    sb.free_(wdt)
    for t in sp:
        sb.free_(t)

    wbc_d = ctx.dram_t("w_bc", [128, C, 32], F32).ap()
    bcbf_d = ctx.dram_t("bc_bf", [32, T], BF16).ap()
    wbc_st = sb.alloc("wbc_st", [128, C, 32], F32)
    wbc = sb.alloc("wbc", [128, C, 32], BF16)
    t_wb = sch.dma("Q", lambda q: q.dma_start(out=wbc_st.ap()[:], in_=wbc_d[:]))
    t_wbc = sch.op("V", lambda v: v.tensor_copy(wbc.ap()[:], wbc_st.ap()[:]),
                   deps=[t_wb], ms=True)
    bc_sb = sb.alloc("bc_sb", [32, T], F32)
    bcbf_sb = sb.alloc("bcbf_sb", [32, T], BF16)
    for tt in range(4):
        b = tt % 2
        t_mm = None
        for kc in range(C):
            dd = ([t_wbc] if kc == 0 else []) + \
                (list(ctx.ps_tok[b]) if kc == 0 else [])
            t_mm = sch.op("T", lambda t, tt=tt, kc=kc, b=b: t.matmul(
                ctx.bank(b, [32, 512]), wbc.ap()[:, kc, :],
                xconv.ap()[:, kc, 512 * tt:512 * (tt + 1)],
                start=(kc == 0), stop=(kc == C - 1)), deps=dd,
                ms=(kc == C - 1))
        t_ev = sch.op("V", lambda v, tt=tt, b=b: v.tensor_scalar(
            bc_sb.ap()[:, 512 * tt:512 * (tt + 1)], ctx.bank(b, [32, 512]),
            bbc.ap()[:], None, ALU.add), deps=[t_mm] + t_sm, ms=True)
        sch.op("S", lambda s, tt=tt: s.copy(
            bcbf_sb.ap()[:, 512 * tt:512 * (tt + 1)],
            bc_sb.ap()[:, 512 * tt:512 * (tt + 1)]), deps=[t_ev], ms=True)
        ctx.ps_tok[b] = (t_ev,)
    sch.dma("Q", lambda q: q.dma_start(out=bc_d[:], in_=bc_sb.ap()[:]),
            deps=[("V", sch.ms["V"])])
    sch.dma("Q", lambda q: q.dma_start(out=bcbf_d[:], in_=bcbf_sb.ap()[:]),
            deps=[("S", sch.ms["S"])])

    u = sb.alloc("u", [128, C, T], BF16)
    t_u = None
    for c in range(C):
        t_u = sch.op("V", lambda v, c=c: v.tensor_mul(
            u.ap()[:, c, :], dt.ap()[:, c, :], xconv.ap()[:, c, :]),
            ms=(c == C - 1))
    sch.dma("Q", lambda q: q.dma_start(out=u_d[:], in_=u.ap()[:]),
            deps=[t_u])
    sch.barrier()
    for t in (dt, xconv, u, bc_sb, bcbf_sb, wbc, wbc_st, small, cw, bbc):
        sb.free_(t)


def phase_moescan(ctx):
    nc, sch, sb = ctx.nc, ctx.sch, ctx.sb
    TQ = 512
    NQ = T // TQ
    dt_d = ctx.dram_t("dt_d", [128, C * T], BF16).ap().rearrange(
        "p (c t) -> p c t", c=C)
    u_d = ctx.dram_t("u_d", [128, C * T], BF16).ap().rearrange(
        "p (c t) -> p c t", c=C)
    bcbf_d = ctx.dram_t("bc_bf", [32, T], BF16).ap()
    al_d = ctx.dram_t("A_log", [128, C, NST], F32).ap()
    y_dr = ctx.dram_t("y32_d", [128, C * T], F32).ap().rearrange(
        "p (c t) -> p c t", c=C)
    xn_d = ctx.dram_t("xn_bf", [128, C * T], BF16).ap()
    lg_d = ctx.dram_t("logits_tm", [128, TT * NEXP], F32).ap()
    moe_d = ctx.dram_t("moe_d", [128, C * T], BF16).ap()
    eg_d = ctx.dram_t("exp_gate", [NEXP, 128, C, F_FF], F32).ap()
    eu_d = ctx.dram_t("exp_up", [NEXP, 128, C, F_FF], F32).ap()
    ed_d = ctx.dram_t("exp_down", [NEXP, 128, FC, D], F32).ap()
    sg_d = ctx.dram_t("sh_gate", [NSH, 128, C, F_FF], F32).ap()
    su_d = ctx.dram_t("sh_up", [NSH, 128, C, F_FF], F32).ap()
    sd_d = ctx.dram_t("sh_down", [NSH, 128, FC, D], F32).ap()
    wte_r = ctx.dram_t("wte_r", [8, T], BF16).ap()

    xn = sb.alloc("xn", [128, C, T], BF16)
    acc = sb.alloc("acc", [128, C, T], BF16)
    hsb = sb.alloc("hsb", [128, FC, 512], BF16)
    gsil = [sb.alloc("gsil", [128, 512], BF16) for _ in range(4)]
    dtmp = [sb.alloc("dtmp", [128, 512], BF16) for _ in range(2)]
    wteb = sb.alloc("wteb", [128, 512], BF16)
    dtq = sb.alloc("dtq", [128, C, TQ], BF16)
    uq = sb.alloc("uq", [128, C, TQ], BF16)
    y_acc = sb.alloc("y_acc", [128, C, TQ], F32)
    ab = [sb.alloc("a", [128, TQ], F32) for _ in range(3)]
    bb = [sb.alloc("b", [128, TQ], BF16) for _ in range(2)]
    hb = [sb.alloc("h", [128, TQ], BF16) for _ in range(2)]
    zb = [sb.alloc("z", [128, TQ], BF16)]
    Bb = [sb.alloc("Bb", [128, TQ], BF16) for _ in range(2)]
    Cb = [sb.alloc("Cb", [128, TQ], BF16) for _ in range(2)]
    alog = sb.alloc("alog", [128, C, NST], F32)
    asb = alog
    hlast = sb.alloc("hlast", [128, NST * C], F32)

    sch.dma("Q", lambda q: q.dma_start(out=xn.ap()[:], in_=xn_d[:]))
    sch.dma("Q", lambda q: q.dma_start(out=alog.ap()[:], in_=al_d[:]),
            fence=False)

    lg = sb.alloc("lg", [128, TT, NEXP], F32)
    rt = sb.alloc("rt", [128, 16 * TT], F32)
    eq1 = sb.alloc("eq1", [128, TT, NEXP], F32)
    eq2 = sb.alloc("eq2", [128, TT, NEXP], F32)
    msk = sb.alloc("msk", [128, TT, NEXP], F32)
    wte = sb.alloc("wte", [128, TT, NEXP], BF16)
    wteT = sb.alloc("wteT", [8, TT, 128], BF16)
    idb = sb.alloc("idb", [128, 128], BF16)
    idb_d = ctx.dram_t("ident_bf", [128, 128], BF16).ap()
    sch.dma("Q", lambda q: q.dma_start(
        out=lg.ap().rearrange("p a b -> p (a b)"), in_=lg_d[:]), fence=False)
    t_all = sch.dma("Q", lambda q: q.dma_start(out=idb.ap()[:], in_=idb_d[:]),
                    fence=False)
    t_xn = t_idb = t_all

    m1 = rt.ap()[:, 0:TT]
    m2 = rt.ap()[:, TT:2 * TT]
    d12 = rt.ap()[:, 2 * TT:3 * TT]
    w1 = rt.ap()[:, 3 * TT:4 * TT]
    w2 = rt.ap()[:, 4 * TT:5 * TT]
    lg3 = lg.ap()
    r0 = sch.op("V", lambda v: v.tensor_reduce(
        m1.rearrange("p (a b) -> p a b", b=1), lg3, AX.X, ALU.max),
        deps=[t_idb], ms=True)
    r1 = sch.op("V", lambda v: v.tensor_tensor(
        eq1.ap()[:], lg3, m1.rearrange("p (a b) -> p a b", b=1)
        .broadcast_to([128, TT, NEXP]), ALU.is_equal), sdeps=[r0], ms=True)
    r2 = sch.op("V", lambda v: v.scalar_tensor_tensor(
        msk.ap()[:], eq1.ap()[:], -1e30, lg3, ALU.mult, ALU.add),
        sdeps=[r1], ms=True)
    r3 = sch.op("V", lambda v: v.tensor_reduce(
        m2.rearrange("p (a b) -> p a b", b=1), msk.ap()[:], AX.X, ALU.max),
        sdeps=[r2], ms=True)
    r4 = sch.op("V", lambda v: v.tensor_sub(d12, m1, m2), sdeps=[r3], ms=True)
    r5 = sch.op("S", lambda s: s.activation(w1, d12, AF.Sigmoid),
                deps=[r4], ms=True)
    r6 = sch.op("V", lambda v: v.tensor_scalar(
        w2, w1, -1.0, 1.0, ALU.mult, ALU.add), deps=[r5], ms=True)
    r7 = sch.op("V", lambda v: v.tensor_tensor(
        eq2.ap()[:], lg3, m2.rearrange("p (a b) -> p a b", b=1)
        .broadcast_to([128, TT, NEXP]), ALU.is_equal), sdeps=[r3], ms=True)
    r8 = sch.op("V", lambda v: v.tensor_scalar(
        msk.ap()[:], eq1.ap()[:], -1.0, 1.0, ALU.mult, ALU.add),
        sdeps=[r2], ms=True)
    r9 = sch.op("V", lambda v: v.tensor_mul(
        eq2.ap()[:], eq2.ap()[:], msk.ap()[:]), sdeps=[r7, r8], ms=True)
    rA = sch.op("V", lambda v: v.tensor_tensor(
        eq1.ap()[:], eq1.ap()[:], w1.rearrange("p (a b) -> p a b", b=1)
        .broadcast_to([128, TT, NEXP]), ALU.mult), sdeps=[r6, r9], ms=True)
    rB = sch.op("V", lambda v: v.tensor_tensor(
        eq2.ap()[:], eq2.ap()[:], w2.rearrange("p (a b) -> p a b", b=1)
        .broadcast_to([128, TT, NEXP]), ALU.mult), sdeps=[rA], ms=True)
    rC = sch.op("V", lambda v: v.tensor_add(
        wte.ap()[:], eq1.ap()[:], eq2.ap()[:]), sdeps=[rB], ms=True)
    wtr_ev = []
    for i in range(TT):
        pb = i % 2
        t_tr = sch.op("T", lambda t, i=i, pb=pb: t.transpose(
            ctx.bank(pb, [8, 128], BF16), wte.ap()[:, i, :], idb.ap()[:]),
            deps=[rC] + list(ctx.ps_tok[pb]), ms=True)
        tok = sch.op("S", lambda s, i=i, pb=pb: s.copy(
            wteT.ap()[:, i, :], ctx.bank(pb, [8, 128], BF16)),
            deps=[t_tr], ms=True)
        ctx.ps_tok[pb] = (tok,)
        wtr_ev.append(tok)
    t_wtr = sch.dma("P", lambda q: q.dma_start(
        out=wte_r[:], in_=wteT.ap().rearrange("p a b -> p (a b)")),
        deps=[wtr_ev[-1]])
    rt_done = [rC, wtr_ev[-1], t_wtr]
    for t_ in [lg, rt, eq1, eq2, msk, wte, wteT, idb]:
        sb.free_(t_)

    t_e = sch.op("S", lambda s: s.activation(
        asb.ap().rearrange("p a b -> p (a b)"),
        alog.ap().rearrange("p a b -> p (a b)"), AF.Exp),
        deps=[t_all], ms=True)
    t_A = sch.op("V", lambda v: v.tensor_scalar(
        asb.ap().rearrange("p a b -> p (a b)"),
        asb.ap().rearrange("p a b -> p (a b)"), -1.0, None, ALU.mult),
        deps=[t_e], ms=True)
    t_hl0 = sch.op("V", lambda v: v.memset(hlast.ap()[:], 0.0), ms=True)
    ld_tok = {}
    ld_tok[0] = [
        sch.dma("P", lambda qq: qq.dma_start(
            out=dtq.ap()[:], in_=dt_d[:, :, 0:TQ])),
        sch.dma("P", lambda qq: qq.dma_start(
            out=uq.ap()[:], in_=u_d[:, :, 0:TQ]), fence=False)]

    wg = sb.alloc("wg", [128, C, F_FF], BF16)
    wu = sb.alloc("wu", [128, C, F_FF], BF16)
    wd = sb.alloc("wd", [128, FC, D], BF16)
    stg = [sb.alloc("stg", [128, 1408], F32) for _ in range(2)]
    stg_use = [None, None]
    free_tok = {"g": {}, "u": {}, "d": {}}
    load_tok = {"g": {}, "u": {}, "d": {}}
    first_load = [True]
    GU_FC = {hh: (176 * (hh + 1) - 1) // 128 for hh in range(8)}
    FC_HH = {fc: (128 * (fc + 1) - 1) // 176 for fc in range(FC)}

    SRCS = {"eg": eg_d, "eu": eu_d, "ed": ed_d,
            "sg": sg_d, "su": su_d, "sd": sd_d}

    def load_expert(kind, e):
        plan = {"g": (("sg" if kind == "sh" else "eg"), wg, C, F_FF // 8),
                "u": (("su" if kind == "sh" else "eu"), wu, C, F_FF // 8),
                "d": (("sd" if kind == "sh" else "ed"), wd, FC, D // 8)}
        pieces = []
        for hh in range(8):
            for key in ("g", "u"):
                src, dst, nch, qw = plan[key]
                pieces.append((src, dst, nch, qw, hh,
                               free_tok[key].get(GU_FC[hh])))
        for hh in range(8):
            src, dst, nch, qw = plan["d"]
            pieces.append((src, dst, nch, qw, hh, free_tok["d"].get(hh)))
        for gi in range(0, len(pieces), 2):
            pair = pieces[gi:gi + 2]
            toks = []
            for j, (src, dst, nch, qw, hh, dep) in enumerate(pair):
                a = (gi + j) % 2
                dd = [dep] if dep else []
                if first_load[0]:
                    dd += rt_done
                    first_load[0] = False
                if stg_use[a]:
                    dd += [stg_use[a]]
                toks.append(sch.dma(
                    "Q", lambda q, src=src, e=e, nch=nch, qw=qw, hh=hh, a=a:
                    q.dma_start(
                        out=stg[a].ap().rearrange("p (a b) -> p a b", a=nch),
                        in_=SRCS[src][e, :, :, qw * hh:qw * (hh + 1)]),
                    deps=dd, fence=(j == 0)))
            grp = toks[-1]
            for j, (src, dst, nch, qw, hh, dep) in enumerate(pair):
                a = (gi + j) % 2
                tok = sch.op("V", lambda v, dst=dst, nch=nch, qw=qw, hh=hh,
                             a=a: v.tensor_copy(
                                 dst.ap()[:, :, qw * hh:qw * (hh + 1)],
                                 stg[a].ap().rearrange("p (a b) -> p a b",
                                                       a=nch)),
                             deps=[grp], ms=True)
                stg_use[a] = tok
                key = "g" if dst is wg else ("u" if dst is wu else "d")
                load_tok[key][hh] = tok

    load_expert("sh", 0)

    t_scan, t_hl, t_z, t_y = {}, {}, {}, {}
    bc_tok = {}
    last_tb = {}
    last_tz = {}
    ystore = {}
    ta_last = {}

    def emit_bcast(g):
        q_, n = divmod(g, 16)
        slot = g % 2
        depB = [last_tb[g - 2]] if g - 2 in last_tb else []
        depC = [last_tz[g - 2]] if g - 2 in last_tz else []
        tB = sch.dma("P", lambda qq, q_=q_, n=n, slot=slot: qq.dma_start(
            out=Bb[slot].ap()[:],
            in_=bcbf_d[n:n + 1, TQ * q_:TQ * (q_ + 1)]
            .broadcast_to([128, TQ])), deps=depB)
        tC = sch.dma("P", lambda qq, q_=q_, n=n, slot=slot: qq.dma_start(
            out=Cb[slot].ap()[:],
            in_=bcbf_d[NST + n:NST + n + 1, TQ * q_:TQ * (q_ + 1)]
            .broadcast_to([128, TQ])), deps=depC)
        bc_tok[g] = (tB, tC)

    def emit_granule(g, clo=0, chi=C):
        q_, n = divmod(g, 16)
        slot = g % 2
        for c in range(clo, chi):
            i = g * C + c
            pi = i % 2
            col = hlast.ap()[:, n * C + c:n * C + c + 1]
            dd_a = [t_A] + ([t_scan[i - 3]] if i >= 3 else [])
            t_a = sch.op("S", lambda s, c=c, n=n, i=i: s.activation(
                ab[i % 3].ap()[:], dtq.ap()[:, c, :], AF.Exp,
                scale=asb.ap()[:, c, n:n + 1]),
                deps=dd_a + ld_tok[q_], ms=True)
            ta_last[q_] = t_a
            t_b = sch.op("V", lambda v, c=c, pi=pi, slot=slot: v.tensor_mul(
                bb[pi].ap()[:], uq.ap()[:, c, :], Bb[slot].ap()[:]),
                deps=[bc_tok[g][0]] + ld_tok[q_], ms=True)
            last_tb[g] = t_b
            dd_s = [t_a, t_hl0]
            if q_ > 0:
                dd_s.append(t_hl[(q_ - 1) * NST * C + n * C + c])
            if i >= 2:
                dd_s.append(t_hl[i - 2])
            t_scan[i] = sch.op("V", lambda v, pi=pi, i=i, col=col:
                               v.tensor_tensor_scan(
                                   hb[pi].ap()[:], ab[i % 3].ap()[:],
                                   bb[pi].ap()[:], col,
                                   ALU.mult, ALU.add), deps=dd_s, ms=True)
            t_hl[i] = sch.op("S", lambda s, pi=pi, col=col: s.copy(
                col, hb[pi].ap()[:, TQ - 1:TQ]), deps=[t_scan[i]], ms=True)
            dd_z = [bc_tok[g][1]]
            t_z[i] = sch.op("V", lambda v, pi=pi, slot=slot: v.tensor_mul(
                zb[0].ap()[:], hb[pi].ap()[:], Cb[slot].ap()[:]),
                deps=dd_z, ms=True)
            last_tz[g] = t_z[i]
            if n == 0:
                ddy = []
                if q_ - 1 in ystore:
                    ddy.append(ystore[q_ - 1])
                t_y[i] = sch.op("V", lambda v, c=c: v.tensor_copy(
                    y_acc.ap()[:, c, :], zb[0].ap()[:]), deps=ddy, ms=True)
            else:
                t_y[i] = sch.op("V", lambda v, c=c: v.tensor_add(
                    y_acc.ap()[:, c, :], y_acc.ap()[:, c, :],
                    zb[0].ap()[:]), ms=True)
        if n == NST - 1 and chi == C:
            ystore[q_] = sch.dma("P", lambda qq, q_=q_: qq.dma_start(
                out=y_dr[:, :, TQ * q_:TQ * (q_ + 1)], in_=y_acc.ap()[:]),
                deps=[t_y[g * C + c] for c in range(C)])
            if q_ + 1 < NQ:
                dd = [last_tb[g], ta_last[q_]]
                ld_tok[q_ + 1] = [
                    sch.dma("P", lambda qq, q_=q_: qq.dma_start(
                        out=dtq.ap()[:],
                        in_=dt_d[:, :, TQ * (q_ + 1):TQ * (q_ + 2)]),
                        deps=dd),
                    sch.dma("P", lambda qq, q_=q_: qq.dma_start(
                        out=uq.ap()[:],
                        in_=u_d[:, :, TQ * (q_ + 1):TQ * (q_ + 2)]),
                        fence=False)]

    NG = NQ * NST
    NHG = 2 * NG
    emitted_b = [0]
    emitted_hg = [0]

    def pump_scan(target):
        target = min(target, NHG)
        while emitted_hg[0] < target:
            hg = emitted_hg[0]
            g, part = hg // 2, hg % 2
            if part == 0:
                while emitted_b[0] < min(g + 2, NG):
                    emit_bcast(emitted_b[0])
                    emitted_b[0] += 1
                emit_granule(g, 0, C // 2)
            else:
                emit_granule(g, C // 2, C)
            emitted_hg[0] += 1

    experts = [("sh", 0), ("sh", 1)] + [("exp", e) for e in range(NEXP)]
    prev_exp_d = None
    h_tok = [None] * 4
    acc_tok = {}
    wteb_rd = [None] * 4
    wteb_tok = [None] * 4
    unit = 0
    pstep = [0]
    NPUMP = 72

    def pump_tick():
        pstep[0] += 1
        pump_scan((pstep[0] * NHG + NPUMP - 1) // NPUMP)

    for ei, (kind, e) in enumerate(experts):
        routed = kind == "exp"
        wg_tok, wu_tok, wd_tok = (dict(load_tok["g"]), dict(load_tok["u"]),
                                  dict(load_tok["d"]))
        def emit_wteb(tt):
            dd = [t_wtr] + ([wteb_rd[0]] if wteb_rd[0] else [])
            wteb_tok[tt] = sch.dma(
                "P", lambda qq, e=e, tt=tt: qq.dma_start(
                    out=wteb.ap()[:],
                    in_=wte_r[e:e + 1, 512 * tt:512 * (tt + 1)]
                    .broadcast_to([128, 512])), deps=dd)
        if routed:
            emit_wteb(0)
        g_last = u_last = d_last = None
        g_fc, u_fc, d_mo = {}, {}, {}
        prev_half_d = prev_exp_d
        for half in range(4):
            tt = half
            for fc in range(FC):
                bg = (fc + tt) % 2
                bu = 2 + (fc + tt) % 2
                t_g = t_u = None
                for kc in range(C):
                    dd = ([wg_tok[FC_HH[fc]], t_xn] if kc == 0 else []) + \
                        (list(ctx.ps_tok[bg]) if kc == 0 else [])
                    t_g = sch.op("T", lambda t, fc=fc, tt=tt, kc=kc,
                                 bg=bg: t.matmul(
                        ctx.bank(bg), wg.ap()[:, kc, 128 * fc:128 * (fc + 1)],
                        xn.ap()[:, kc, 512 * tt:512 * (tt + 1)],
                        start=(kc == 0), stop=(kc == C - 1)),
                        deps=dd, ms=(kc == C - 1))
                    dd = ([wu_tok[FC_HH[fc]]] if kc == 0 else []) + \
                        (list(ctx.ps_tok[bu]) if kc == 0 else [])
                    t_u = sch.op("T", lambda t, fc=fc, tt=tt, kc=kc,
                                 bu=bu: t.matmul(
                        ctx.bank(bu), wu.ap()[:, kc, 128 * fc:128 * (fc + 1)],
                        xn.ap()[:, kc, 512 * tt:512 * (tt + 1)],
                        start=(kc == 0), stop=(kc == C - 1)),
                        deps=dd, ms=(kc == C - 1))
                g_last, u_last = t_g, t_u
                if tt == 3:
                    g_fc[fc] = t_g
                    u_fc[fc] = t_u
                a = (fc + tt) % 4
                t_sil = sch.op("S", lambda s, a=a, bg=bg: s.activation(
                    gsil[a].ap()[:], ctx.bank(bg), AF.Silu),
                    deps=[t_g] + ([h_tok[a]] if h_tok[a] else []), ms=True)
                if routed:
                    t_s2 = sch.op("V", lambda v, a=a: v.tensor_mul(
                        gsil[a].ap()[:], gsil[a].ap()[:], wteb.ap()[:]),
                        deps=[t_sil, wteb_tok[tt]], ms=True)
                    wteb_rd[0] = t_s2
                    hsrc = gsil[a]
                    hdep = [t_s2, t_u]
                else:
                    hsrc = gsil[a]
                    hdep = [t_sil, t_u]
                t_h = sch.op("V", lambda v, fc=fc, a=a, bu=bu, hsrc=hsrc:
                             v.tensor_mul(
                    hsb.ap()[:, fc, :], hsrc.ap()[:], ctx.bank(bu)),
                    deps=hdep + ([prev_half_d] if prev_half_d else []),
                    ms=True)
                h_tok[a] = t_h
                ctx.ps_tok[bg] = (t_sil,)
                ctx.ps_tok[bu] = (t_h,)
            if routed and half < 3:
                emit_wteb(half + 1)
            pump_tick()
            prev_half_d = None
            for mo in range(C):
                bd = 4 + (mo + tt) % 2
                t_d = None
                for fc in range(FC):
                    dd = ([wd_tok[mo], t_h] if fc == 0 else []) + \
                        (list(ctx.ps_tok[bd]) if fc == 0 else [])
                    t_d = sch.op("T", lambda t, mo=mo, fc=fc, bd=bd: t.matmul(
                        ctx.bank(bd), wd.ap()[:, fc, 128 * mo:128 * (mo + 1)],
                        hsb.ap()[:, fc, :],
                        start=(fc == 0), stop=(fc == FC - 1)),
                        deps=dd, ms=(fc == FC - 1))
                d_last = t_d
                if tt == 3:
                    d_mo[mo] = t_d
                prev_half_d = t_d
                a = (mo + tt) % 2
                prev = acc_tok.get((mo, tt))
                if ei == 0:
                    t_acc = sch.op("S", lambda s, mo=mo, tt=tt, bd=bd: s.copy(
                        acc.ap()[:, mo, 512 * tt:512 * (tt + 1)],
                        ctx.bank(bd)), deps=[t_d], ms=True)
                    ctx.ps_tok[bd] = (t_acc,)
                else:
                    t_cp = sch.op("S", lambda s, a=a, bd=bd: s.copy(
                        dtmp[a].ap()[:], ctx.bank(bd)),
                        deps=[t_d] + ([acc_tok[("tmp", a)]]
                                      if ("tmp", a) in acc_tok else []),
                        ms=True)
                    t_acc = sch.op("P", lambda p, mo=mo, tt=tt, a=a:
                                   p.tensor_add(
                        acc.ap()[:, mo, 512 * tt:512 * (tt + 1)],
                        acc.ap()[:, mo, 512 * tt:512 * (tt + 1)],
                        dtmp[a].ap()[:]),
                        deps=[t_cp] + ([prev] if prev else []), ms=True)
                    acc_tok[("tmp", a)] = t_acc
                    ctx.ps_tok[bd] = (t_cp,)
                acc_tok[(mo, tt)] = t_acc
            unit += 1
            if half < 3:
                pump_tick()
        prev_exp_d = d_last
        if ei + 1 < len(experts):
            nkind, ne = experts[ei + 1]
            free_tok["g"] = g_fc
            free_tok["u"] = u_fc
            free_tok["d"] = d_mo
            load_expert(nkind, ne)
        pump_tick()
    pump_scan(NHG)
    sch.dma("Q", lambda q: q.dma_start(
        out=moe_d[:], in_=acc.ap().rearrange("p a b -> p (a b)")),
        deps=[tok for k, tok in acc_tok.items() if not isinstance(k, tuple)
              or k[0] != "tmp"])
    sch.barrier()
    for t_ in ([xn, acc, hsb, wteb, dtq, uq, y_acc, alog, hlast]
               + gsil + dtmp + ab + bb + hb + zb + Bb + Cb
               + [wg, wu, wd] + stg):
        sb.free_(t_)


def phase_scan_old(ctx):
    nc, sch, sb = ctx.nc, ctx.sch, ctx.sb
    TQ = 512
    NQ = T // TQ
    dt_d = ctx.dram_t("dt_d", [128, C * T], BF16).ap().rearrange(
        "p (c t) -> p c t", c=C)
    u_d = ctx.dram_t("u_d", [128, C * T], BF16).ap().rearrange(
        "p (c t) -> p c t", c=C)
    bc_d = ctx.dram_t("bc_d", [32, T], F32).ap()
    al_d = ctx.dram_t("A_log", [128, C, NST], F32).ap()
    sel_d = ctx.dram_t("sel32", [32, 32 * 128], BF16).ap()
    y_dr = ctx.dram_t("y_d", [128, C * T], BF16).ap().rearrange(
        "p (c t) -> p c t", c=C)

    bcb = sb.alloc("bcb", [32, T], BF16)
    alog = sb.alloc("alog", [128, C, NST], F32)
    asb = sb.alloc("asb", [128, C, NST], F32)
    sel = sb.alloc("sel", [32, 32, 128], BF16)
    hlast = sb.alloc("hlast", [128, NST * C], F32)
    y_acc = sb.alloc("y_acc", [128, C, TQ], F32)
    ytmp = [sb.alloc("ytmp", [128, C, TQ], BF16) for _ in range(2)]
    Bball = sb.alloc("Bball", [128, NST, TQ], BF16)
    Cball = sb.alloc("Cball", [128, NST, TQ], BF16)
    dtq = [sb.alloc("dtq", [128, C, TQ], BF16) for _ in range(2)]
    uq = [sb.alloc("uq", [128, C, TQ], BF16) for _ in range(2)]
    ab = [sb.alloc("a", [128, TQ], F32) for _ in range(2)]
    bb = [sb.alloc("b", [128, TQ], BF16) for _ in range(2)]
    hb = [sb.alloc("h", [128, TQ], BF16) for _ in range(2)]
    zb = [sb.alloc("z", [128, TQ], BF16)]

    bct = sb.alloc("bct", [32, T], F32)
    for t_, dd in [(bct, bc_d), (alog, al_d)]:
        sch.dma("Q", lambda q, t_=t_, dd=dd: q.dma_start(
            out=t_.ap()[:], in_=dd[:]))
    sch.dma("Q", lambda q: q.dma_start(
        out=sel.ap().rearrange("p a b -> p (a b)"), in_=sel_d[:]))
    sch.barrier()
    t_e = sch.op("S", lambda s: s.activation(
        asb.ap().rearrange("p a b -> p (a b)"),
        alog.ap().rearrange("p a b -> p (a b)"), AF.Exp), ms=True)
    t_A = sch.op("V", lambda v: v.tensor_scalar(
        asb.ap().rearrange("p a b -> p (a b)"),
        asb.ap().rearrange("p a b -> p (a b)"), -1.0, None, ALU.mult),
        deps=[t_e], ms=True)
    t_bcb = sch.op("V", lambda v: v.tensor_copy(bcb.ap()[:], bct.ap()[:]),
                   ms=True)
    t_hl0 = sch.op("V", lambda v: v.memset(hlast.ap()[:], 0.0), ms=True)
    sch.barrier()
    sb.free_(bct)

    t_scan = {}
    t_z = {}
    t_y = {}
    t_hl = {}
    ld_tok = {}
    bball_read = None
    cball_read = None
    ytmp_read = [None, None]
    prev_st = {}

    def load_q(q):
        qb = q % 2
        dd = []
        if q >= 2:
            dd = [t_scan.get((q - 2) * NST * C + NST * C - 1)]
            dd = [x for x in dd if x]
        ld_tok[q] = [
            sch.dma("Q", lambda qq, q=q, qb=qb: qq.dma_start(
                out=dtq[qb].ap()[:], in_=dt_d[:, :, TQ * q:TQ * (q + 1)]),
                deps=dd),
            sch.dma("Q", lambda qq, q=q, qb=qb: qq.dma_start(
                out=uq[qb].ap()[:], in_=u_d[:, :, TQ * q:TQ * (q + 1)]),
                deps=dd)]

    load_q(0)
    for q in range(NQ):
        qb = q % 2
        if q + 1 < NQ:
            load_q(q + 1)
        evs = []
        for n in range(NST):
            for (row, dst) in ((n, Bball), (NST + n, Cball)):
                pb = len(evs) % 2
                dd = [t_bcb] + list(ctx.ps_tok[pb])
                if q > 0:
                    dd += [bball_read, cball_read]
                t_mm = sch.op("T", lambda t, row=row, q=q, pb=pb: t.matmul(
                    ctx.bank(pb), sel.ap()[:, row, :],
                    bcb.ap()[0:32, TQ * q:TQ * (q + 1)],
                    start=True, stop=True), deps=dd, ms=True)
                tok = sch.op("S", lambda s, dst=dst, n=n, pb=pb: s.copy(
                    dst.ap()[:, n, :], ctx.bank(pb)), deps=[t_mm], ms=True)
                ctx.ps_tok[pb] = (tok,)
                evs.append(tok)
        ev_all = evs[-1]
        for n in range(NST):
            for c in range(C):
                i = (q * NST + n * 0 + 0) * 0 + q * NST * C + n * C + c
                pi = i % 2
                col = hlast.ap()[:, n * C + c:n * C + c + 1]
                dd_a = [t_A] + ([t_scan[i - 2]] if i >= 2 else [])
                t_a = sch.op("S", lambda s, c=c, n=n, pi=pi, qb=qb:
                             s.activation(
                                 ab[pi].ap()[:], dtq[qb].ap()[:, c, :], AF.Exp,
                                 scale=asb.ap()[:, c, n:n + 1]),
                             deps=dd_a + ld_tok[q], ms=True)
                t_b = sch.op("V", lambda v, c=c, n=n, pi=pi, qb=qb:
                             v.tensor_mul(
                                 bb[pi].ap()[:], uq[qb].ap()[:, c, :],
                                 Bball.ap()[:, n, :]),
                             deps=[ev_all] + ld_tok[q], ms=True)
                dd_s = [t_a, t_hl0]
                if q > 0:
                    dd_s.append(t_hl[(q - 1) * NST * C + n * C + c])
                if i >= 2:
                    dd_s.append(t_hl[i - 2])
                t_scan[i] = sch.op("V", lambda v, pi=pi, col=col:
                                   v.tensor_tensor_scan(
                                       hb[pi].ap()[:], ab[pi].ap()[:],
                                       bb[pi].ap()[:], col,
                                       ALU.mult, ALU.add), deps=dd_s, ms=True)
                t_hl[i] = sch.op("S", lambda s, pi=pi, col=col: s.copy(
                    col, hb[pi].ap()[:, TQ - 1:TQ]), deps=[t_scan[i]], ms=True)
                dd_z = [ev_all] + ([t_y[i - 2]] if i >= 2 else [])
                t_z[i] = sch.op("V", lambda v, n=n, pi=pi: v.tensor_mul(
                    zb[pi].ap()[:], hb[pi].ap()[:], Cball.ap()[:, n, :]),
                    deps=dd_z, ms=True)
                if n == 0:
                    ddy = [t_z[i]]
                    if q > 0:
                        ddy.append(prev_st[c])
                    t_y[i] = sch.op("P", lambda p, c=c, pi=pi: p.tensor_copy(
                        y_acc.ap()[:, c, :], zb[pi].ap()[:]),
                        deps=ddy, ms=True)
                else:
                    t_y[i] = sch.op("P", lambda p, c=c, pi=pi: p.tensor_add(
                        y_acc.ap()[:, c, :], y_acc.ap()[:, c, :],
                        zb[pi].ap()[:]), deps=[t_z[i]], ms=True)
            bball_read = t_b
            cball_read = t_z[i]
        st_dep = []
        for c in range(C):
            dd = [t_y[q * NST * C + (NST - 1) * C + c]]
            if ytmp_read[qb]:
                dd.append(ytmp_read[qb])
            st_dep.append(sch.op("V", lambda v, c=c, qb=qb: v.tensor_copy(
                ytmp[qb].ap()[:, c, :], y_acc.ap()[:, c, :]), deps=dd,
                ms=True))
        ytmp_read[qb] = sch.dma("Q", lambda qq, q=q, qb=qb: qq.dma_start(
            out=y_dr[:, :, TQ * q:TQ * (q + 1)], in_=ytmp[qb].ap()[:]),
            deps=st_dep)
        prev_st = {c: st_dep[c] for c in range(C)}
    sch.barrier()
    for t_ in ([bcb, alog, asb, sel, hlast, y_acc, Bball, Cball]
               + ytmp + dtq + uq + ab + bb + hb + zb):
        sb.free_(t_)


def phase_mout(ctx):
    nc, sch, sb = ctx.nc, ctx.sch, ctx.sb
    y_d = ctx.dram_t("y32_d", [128, C * T], F32).ap().rearrange(
        "p (c t) -> p c t", c=C)
    gg_d = ctx.dram_t("ggate_d", [128, C * T], BF16).ap()
    xc_d = ctx.dram_t("xconv_d", [128, C * T], BF16).ap()
    xn_d = ctx.dram_t("xn_bf", [128, C * T], BF16).ap()
    dv_d = ctx.dram_t("Dvec", [128, C], F32).ap()
    s1_d = ctx.dram_t("s1_d", [128, C * T], BF16).ap()
    bo_d = ctx.dram_t("b_out", [128, C], F32).ap()
    lg_d = ctx.dram_t("ln_g", [128, C], F32).ap()
    lb_d = ctx.dram_t("ln_b", [128, C], F32).ap()

    gg = sb.alloc("gg", [128, C, T], BF16)
    xc = sb.alloc("xc", [128, C, T], BF16)
    xn = sb.alloc("xn", [128, C, T], BF16)
    small = sb.alloc("small", [128, 4 * C], F32)
    ld = [sch.dma("Q", lambda q, t_=t_, dd=dd: q.dma_start(
        out=t_.ap()[:], in_=dd[:]))
        for t_, dd in [(gg, gg_d), (xc, xc_d), (xn, xn_d)]]
    for i, dd in enumerate([dv_d, bo_d, lg_d, lb_d]):
        ld.append(sch.dma("Q", lambda q, i=i, dd=dd: q.dma_start(
            out=small.ap()[:, i * C:(i + 1) * C], in_=dd[:])))
    dvc = lambda c: small.ap()[:, 0 * C + c:0 * C + c + 1]
    boc = lambda c: small.ap()[:, 1 * C + c:1 * C + c + 1]
    lgc = lambda c: small.ap()[:, 2 * C + c:2 * C + c + 1]
    lbc = lambda c: small.ap()[:, 3 * C + c:3 * C + c + 1]
    ygate = sb.alloc("ygate", [128, C, T], BF16)
    yf = [sb.alloc("yf", [128, T], F32) for _ in range(2)]
    sch.barrier()

    yf_use = [None, None]
    for c in range(C):
        a = c % 2
        ldy = sch.dma("Q", lambda q, c=c, a=a: q.dma_start(
            out=yf[a].ap()[:], in_=y_d[:, c, :]),
            deps=[yf_use[a]] if yf_use[a] else [])
        t1 = sch.op("V", lambda v, c=c, a=a: v.scalar_tensor_tensor(
            yf[a].ap()[:], xc.ap()[:, c, :], dvc(c), yf[a].ap()[:],
            ALU.mult, ALU.add), deps=[ldy], ms=True)
        yf_use[a] = sch.op("V", lambda v, c=c, a=a: v.tensor_mul(
            ygate.ap()[:, c, :], yf[a].ap()[:], gg.ap()[:, c, :]),
            sdeps=[t1], ms=True)
    sch.barrier()
    for t_ in [gg, xc] + yf:
        sb.free_(t_)

    wout, wtok, _ = _load_cast_w(ctx, "w_out", C, 1024, "wout")
    res = sb.alloc("res", [128, C, T], BF16)
    for mo in range(C):
        for tt in range(4):
            b = (mo * 4 + tt) % 4
            t_mm = None
            for kc in range(C):
                dd = ([wtok] if kc == 0 else []) + \
                    (list(ctx.ps_tok[b]) if kc == 0 else [])
                t_mm = sch.op("T", lambda t, mo=mo, tt=tt, kc=kc, b=b: t.matmul(
                    ctx.bank(b), wout.ap()[:, kc, 128 * mo:128 * (mo + 1)],
                    ygate.ap()[:, kc, 512 * tt:512 * (tt + 1)],
                    start=(kc == 0), stop=(kc == C - 1)),
                    deps=dd, ms=(kc == C - 1))
            tok = sch.op("V", lambda v, mo=mo, tt=tt, b=b: v.scalar_tensor_tensor(
                res.ap()[:, mo, 512 * tt:512 * (tt + 1)], ctx.bank(b), boc(mo),
                xn.ap()[:, mo, 512 * tt:512 * (tt + 1)], ALU.add, ALU.add),
                deps=[t_mm], ms=True)
            ctx.ps_tok[b] = (tok,)
    sch.barrier()
    sb.free_(ygate)
    sb.free_(wout)

    ones_bf = sb.alloc("ones_bf", [128, 1], BF16)
    ones_f = sb.alloc("ones_f", [1, 128], F32)
    rows = sb.alloc("rows", [1, 8 * T], F32)
    sq = [sb.alloc("sq", [128, 512], BF16) for _ in range(2)]
    t_o1 = sch.op("V", lambda v: v.memset(ones_bf.ap()[:], 1.0), ms=True)
    t_o2 = sch.op("V", lambda v: v.memset(ones_f.ap()[:], 1.0), ms=True)
    row = lambda r, tt=None: (rows.ap()[:, r * T:(r + 1) * T] if tt is None
                              else rows.ap()[:, r * T + 512 * tt:
                                             r * T + 512 * (tt + 1)])
    for tt in range(4):
        bs_, bq = 4 + (tt % 2) * 2, 5 + (tt % 2) * 2
        t_mm = t_mq = None
        mqs = {}
        for c in range(C):
            a = c % 2
            dd = [list(ctx.ps_tok[bs_]) if c == 0 else []][0]
            t_mm = sch.op("T", lambda t, c=c, tt=tt, bs_=bs_: t.matmul(
                ctx.bank(bs_, [1, 512]), ones_bf.ap()[:],
                res.ap()[:, c, 512 * tt:512 * (tt + 1)],
                start=(c == 0), stop=(c == C - 1)),
                deps=[t_o1] + dd, ms=(c == C - 1))
            t_sq = sch.op("S", lambda s, c=c, tt=tt, a=a: s.activation(
                sq[a].ap()[:], res.ap()[:, c, 512 * tt:512 * (tt + 1)],
                AF.Square), deps=[mqs[c - 2]] if c >= 2 else [], ms=True)
            dd = [list(ctx.ps_tok[bq]) if c == 0 else []][0]
            t_mq = sch.op("T", lambda t, c=c, tt=tt, bq=bq, a=a: t.matmul(
                ctx.bank(bq, [1, 512]), ones_bf.ap()[:], sq[a].ap()[:],
                start=(c == 0), stop=(c == C - 1)),
                deps=[t_sq] + dd, ms=True)
            mqs[c] = t_mq
        e1 = sch.op("S", lambda s, tt=tt, bs_=bs_: s.copy(
            row(0, tt), ctx.bank(bs_, [1, 512])), deps=[t_mm], ms=True)
        e2 = sch.op("S", lambda s, tt=tt, bq=bq: s.copy(
            row(1, tt), ctx.bank(bq, [1, 512])), deps=[t_mq], ms=True)
        ctx.ps_tok[bs_] = (e1,)
        ctx.ps_tok[bq] = (e2,)
    sch.barrier()
    t1 = sch.op("V", lambda v: v.tensor_scalar(
        row(2), row(0), 1.0 / D, None, ALU.mult), ms=True)
    t2 = sch.op("V", lambda v: v.tensor_mul(row(3), row(2), row(2)),
                sdeps=[t1], ms=True)
    t3 = sch.op("V", lambda v: v.tensor_scalar(
        row(4), row(1), 1.0 / D, None, ALU.mult), sdeps=[t2], ms=True)
    t4 = sch.op("V", lambda v: v.tensor_sub(row(4), row(4), row(3)),
                sdeps=[t3], ms=True)
    t5 = sch.op("V", lambda v: v.tensor_scalar(
        row(4), row(4), float(LN_EPS), None, ALU.add), sdeps=[t4], ms=True)
    t6 = sch.op("S", lambda s: s.activation(row(5), row(4), AF.Ln),
                deps=[t5], ms=True)
    t7 = sch.op("V", lambda v: v.tensor_scalar(
        row(5), row(5), -0.5, None, ALU.mult), deps=[t6], ms=True)
    t8 = sch.op("S", lambda s: s.activation(row(6), row(5), AF.Exp),
                deps=[t7], ms=True)
    s1 = sb.alloc("s1", [128, C, T], BF16)
    tm1 = [sb.alloc("tm1", [128, 512], F32) for _ in range(2)]
    for tt in range(4):
        bm, br = (tt % 2) * 2, (tt % 2) * 2 + 1
        dd = [t8, t_o2] + list(ctx.ps_tok[bm]) + list(ctx.ps_tok[br])
        t_bm = sch.op("T", lambda t, tt=tt, bm=bm: t.matmul(
            ctx.bank(bm), ones_f.ap()[:], row(2, tt), start=True, stop=True),
            deps=dd, ms=True)
        t_br = sch.op("T", lambda t, tt=tt, br=br: t.matmul(
            ctx.bank(br), ones_f.ap()[:], row(6, tt), start=True, stop=True),
            ms=True)
        apl = []
        for c in range(C):
            a = c % 2
            w1 = sch.op("V", lambda v, c=c, tt=tt, bm=bm, a=a: v.tensor_sub(
                tm1[a].ap()[:], res.ap()[:, c, 512 * tt:512 * (tt + 1)],
                ctx.bank(bm)), deps=[t_bm, t_br], ms=True)
            w2 = sch.op("V", lambda v, a=a, br=br: v.tensor_mul(
                tm1[a].ap()[:], tm1[a].ap()[:], ctx.bank(br)), ms=True)
            w3 = sch.op("V", lambda v, c=c, tt=tt, a=a: v.tensor_scalar(
                s1.ap()[:, c, 512 * tt:512 * (tt + 1)], tm1[a].ap()[:],
                lgc(c), lbc(c), ALU.mult, ALU.add), ms=True)
            apl.append(w3)
        ctx.ps_tok[bm] = (apl[-1],)
        ctx.ps_tok[br] = (apl[-1],)
    sch.dma("Q", lambda q: q.dma_start(out=s1_d[:], in_=s1.ap()[:]),
            deps=[("V", sch.ms["V"])])
    sch.barrier()
    for t_ in [xn, small, res, ones_bf, ones_f, rows, s1] + sq + tm1:
        sb.free_(t_)


def phase_moe(ctx):
    nc, sch, sb = ctx.nc, ctx.sch, ctx.sb
    xn_d = ctx.dram_t("xn_bf", [128, C * T], BF16).ap()
    lg_d = ctx.dram_t("logits_tm", [128, TT * NEXP], F32).ap()
    moe_d = ctx.dram_t("moe_d", [128, C * T], F32).ap()
    eg_d = ctx.dram_t("exp_gate", [NEXP, 128, C, F_FF], F32).ap()
    eu_d = ctx.dram_t("exp_up", [NEXP, 128, C, F_FF], F32).ap()
    ed_d = ctx.dram_t("exp_down", [NEXP, 128, FC, D], F32).ap()
    sg_d = ctx.dram_t("sh_gate", [NSH, 128, C, F_FF], F32).ap()
    su_d = ctx.dram_t("sh_up", [NSH, 128, C, F_FF], F32).ap()
    sd_d = ctx.dram_t("sh_down", [NSH, 128, FC, D], F32).ap()
    sel8_d = ctx.dram_t("sel8", [8, 8 * 128], BF16).ap()

    xn = sb.alloc("xn", [128, C, T], BF16)
    lg = sb.alloc("lg", [128, TT, NEXP], F32)
    sel8 = sb.alloc("sel8", [8, 8, 128], BF16)
    t_xn = sch.dma("Q", lambda q: q.dma_start(out=xn.ap()[:], in_=xn_d[:]))
    t_lg = sch.dma("Q", lambda q: q.dma_start(
        out=lg.ap().rearrange("p a b -> p (a b)"), in_=lg_d[:]))
    sch.dma("Q", lambda q: q.dma_start(
        out=sel8.ap().rearrange("p a b -> p (a b)"), in_=sel8_d[:]))
    sch.barrier()

    rt = sb.alloc("rt", [128, 16 * TT], F32)
    m1 = rt.ap()[:, 0:TT]
    m2 = rt.ap()[:, TT:2 * TT]
    d12 = rt.ap()[:, 2 * TT:3 * TT]
    w1 = rt.ap()[:, 3 * TT:4 * TT]
    w2 = rt.ap()[:, 4 * TT:5 * TT]
    eq1 = sb.alloc("eq1", [128, TT, NEXP], F32)
    eq2 = sb.alloc("eq2", [128, TT, NEXP], F32)
    msk = sb.alloc("msk", [128, TT, NEXP], F32)
    wte = sb.alloc("wte", [128, TT, NEXP], BF16)
    lg3 = lg.ap()
    r0 = sch.op("V", lambda v: v.tensor_reduce(
        m1.rearrange("p (a b) -> p a b", b=1), lg3, AX.X, ALU.max), ms=True)
    r1 = sch.op("V", lambda v: v.tensor_tensor(
        eq1.ap()[:], lg3, m1.rearrange("p (a b) -> p a b", b=1)
        .broadcast_to([128, TT, NEXP]), ALU.is_equal), sdeps=[r0], ms=True)
    r2 = sch.op("V", lambda v: v.scalar_tensor_tensor(
        msk.ap()[:], eq1.ap()[:], -1e30, lg3, ALU.mult, ALU.add),
        sdeps=[r1], ms=True)
    r3 = sch.op("V", lambda v: v.tensor_reduce(
        m2.rearrange("p (a b) -> p a b", b=1), msk.ap()[:], AX.X, ALU.max),
        sdeps=[r2], ms=True)
    r4 = sch.op("V", lambda v: v.tensor_sub(d12, m1, m2), sdeps=[r3], ms=True)
    r5 = sch.op("S", lambda s: s.activation(w1, d12, AF.Sigmoid),
                deps=[r4], ms=True)
    r6 = sch.op("V", lambda v: v.tensor_scalar(
        w2, w1, -1.0, 1.0, ALU.mult, ALU.add), deps=[r5], ms=True)
    r7 = sch.op("V", lambda v: v.tensor_tensor(
        eq2.ap()[:], lg3, m2.rearrange("p (a b) -> p a b", b=1)
        .broadcast_to([128, TT, NEXP]), ALU.is_equal), sdeps=[r3], ms=True)
    r8 = sch.op("V", lambda v: v.tensor_scalar(
        msk.ap()[:], eq1.ap()[:], -1.0, 1.0, ALU.mult, ALU.add),
        sdeps=[r2], ms=True)
    r9 = sch.op("V", lambda v: v.tensor_mul(
        eq2.ap()[:], eq2.ap()[:], msk.ap()[:]), sdeps=[r7, r8], ms=True)
    rA = sch.op("V", lambda v: v.tensor_tensor(
        eq1.ap()[:], eq1.ap()[:], w1.rearrange("p (a b) -> p a b", b=1)
        .broadcast_to([128, TT, NEXP]), ALU.mult), sdeps=[r6, r9], ms=True)
    rB = sch.op("V", lambda v: v.tensor_tensor(
        eq2.ap()[:], eq2.ap()[:], w2.rearrange("p (a b) -> p a b", b=1)
        .broadcast_to([128, TT, NEXP]), ALU.mult), sdeps=[rA], ms=True)
    rC = sch.op("V", lambda v: v.tensor_add(
        wte.ap()[:], eq1.ap()[:], eq2.ap()[:]), sdeps=[rB], ms=True)
    wteT = sb.alloc("wteT", [8, TT, 128], BF16)
    idb_d = ctx.dram_t("ident_bf", [128, 128], BF16).ap()
    idb = sb.alloc("idb", [128, 128], BF16)
    sch.dma("Q", lambda q: q.dma_start(out=idb.ap()[:], in_=idb_d[:]))
    sch.barrier()
    for i in range(TT):
        pb = i % 2
        t_tr = sch.op("T", lambda t, i=i, pb=pb: t.transpose(
            ctx.bank(pb, [8, 128], BF16), wte.ap()[:, i, :], idb.ap()[:]),
            deps=[rC] + list(ctx.ps_tok[pb]), ms=True)
        tok = sch.op("S", lambda s, i=i, pb=pb: s.copy(
            wteT.ap()[:, i, :], ctx.bank(pb, [8, 128], BF16)),
            deps=[t_tr], ms=True)
        ctx.ps_tok[pb] = (tok,)
    sch.barrier()
    for t_ in [rt, eq1, eq2, msk, lg]:
        sb.free_(t_)

    acc = sb.alloc("acc", [128, C, T], F32)
    wg = sb.alloc("wg", [128, C, F_FF], BF16)
    wu = sb.alloc("wu", [128, C, F_FF], BF16)
    wd = sb.alloc("wd", [128, FC, D], BF16)
    stg = [sb.alloc("stg", [128, 2 * F_FF], F32)]
    hsb = sb.alloc("hsb", [128, FC, 512], BF16)
    gsil = [sb.alloc("gsil", [128, 512], BF16) for _ in range(4)]
    tmp = [sb.alloc("tmp", [128, 512], F32) for _ in range(2)]
    wteb = sb.alloc("wteb", [128, T], BF16)

    experts = [("sh", 0), ("sh", 1)] + [("exp", e) for e in range(NEXP)]
    prev_exp_d = None
    last_tm = None
    h_tok = [None, None]
    free_tok = {"g": None, "u": None, "d": None}
    load_tok = {"g": None, "u": None, "d": None}
    stg_use = [None]

    def load_mat(kind, e, dst, nch, wid, key):
        srcs = {"eg": eg_d, "eu": eu_d, "ed": ed_d,
                "sg": sg_d, "su": su_d, "sd": sd_d}
        dd = [free_tok[key]] if free_tok[key] else []
        tok = None
        npair = (nch + 1) // 2
        for pp in range(npair):
            c0 = 2 * pp
            nn = min(2, nch - c0)
            sdep = [stg_use[0]] if stg_use[0] else []
            ld = sch.dma("Q", lambda q, kind=kind, e=e, c0=c0, nn=nn, wid=wid:
                         q.dma_start(
                             out=stg[0].ap()[:, 0:nn * wid],
                             in_=srcs[kind][e, :, c0:c0 + nn, :].rearrange(
                                 "p a b -> p (a b)")),
                         deps=dd + sdep)
            tok = sch.op("V", lambda v, c0=c0, nn=nn, wid=wid, dst=dst:
                         v.tensor_copy(
                             dst.ap()[:, c0:c0 + nn, :].rearrange(
                                 "p a b -> p (a b)"),
                             stg[0].ap()[:, 0:nn * wid]),
                         deps=[ld], ms=True)
            stg_use[0] = tok
            dd = []
        load_tok[key] = tok
        return tok

    kind0, e0 = experts[0]
    load_mat("sg" if kind0 == "sh" else "eg", e0, wg, C, F_FF, "g")
    load_mat("su" if kind0 == "sh" else "eu", e0, wu, C, F_FF, "u")
    load_mat("sd" if kind0 == "sh" else "ed", e0, wd, FC, D, "d")

    acc_tok = {}
    for ei, (kind, e) in enumerate(experts):
        routed = kind == "exp"
        wg_tok, wu_tok, wd_tok = (load_tok["g"], load_tok["u"],
                                  load_tok["d"])
        if routed:
            ebt = []
            for tt in range(4):
                dd = list(ctx.ps_tok[6]) + ([last_tm] if last_tm else [])
                t_mm = sch.op("T", lambda t, e=e, tt=tt: t.matmul(
                    ctx.bank(6), sel8.ap()[:, e, :],
                    wteT.ap().rearrange("p a b -> p (a b)")[0:8,
                        512 * tt:512 * (tt + 1)],
                    start=True, stop=True), deps=dd, ms=True)
                tok = sch.op("S", lambda s, tt=tt: s.copy(
                    wteb.ap()[:, 512 * tt:512 * (tt + 1)], ctx.bank(6)),
                    deps=[t_mm], ms=True)
                ctx.ps_tok[6] = (tok,)
                ebt.append(tok)
            wteb_tok = ebt[-1]
        g_last = u_last = d_last = None
        prev_half_d = prev_exp_d
        for half in range(4):
            tts = (half,)
            for tt in tts:
                for fc in range(FC):
                    bg = (fc + tt) % 2
                    bu = 2 + (fc + tt) % 2
                    t_g = t_u = None
                    for kc in range(C):
                        dd = ([wg_tok, t_xn] if kc == 0 else []) + \
                            (list(ctx.ps_tok[bg]) if kc == 0 else [])
                        t_g = sch.op("T", lambda t, e=e, fc=fc, tt=tt, kc=kc,
                                     bg=bg: t.matmul(
                            ctx.bank(bg), wg.ap()[:, kc, 128 * fc:128 * (fc + 1)],
                            xn.ap()[:, kc, 512 * tt:512 * (tt + 1)],
                            start=(kc == 0), stop=(kc == C - 1)),
                            deps=dd, ms=(kc == C - 1))
                        dd = ([wu_tok] if kc == 0 else []) + \
                            (list(ctx.ps_tok[bu]) if kc == 0 else [])
                        t_u = sch.op("T", lambda t, e=e, fc=fc, tt=tt, kc=kc,
                                     bu=bu: t.matmul(
                            ctx.bank(bu), wu.ap()[:, kc, 128 * fc:128 * (fc + 1)],
                            xn.ap()[:, kc, 512 * tt:512 * (tt + 1)],
                            start=(kc == 0), stop=(kc == C - 1)),
                            deps=dd, ms=(kc == C - 1))
                    g_last, u_last = t_g, t_u
                    a = (fc + tt) % 2
                    t_sil = sch.op("S", lambda s, a=a, bg=bg: s.activation(
                        gsil[a].ap()[:], ctx.bank(bg), AF.Silu),
                        deps=[t_g] + ([h_tok[a]] if h_tok[a] else []), ms=True)
                    t_h = sch.op("V", lambda v, fc=fc, tt=tt, a=a, bu=bu:
                                 v.tensor_mul(
                        hsb.ap()[:, fc, :],
                        gsil[a].ap()[:], ctx.bank(bu)),
                        deps=[t_sil, t_u] +
                        ([prev_half_d] if prev_half_d else []), ms=True)
                    h_tok[a] = t_h
                    ctx.ps_tok[bg] = (t_sil,)
                    ctx.ps_tok[bu] = (t_h,)
            prev_half_d = None
            for tt in tts:
                for mo in range(C):
                    bd = 4 + (mo + tt) % 2
                    t_d = None
                    for fc in range(FC):
                        dd = ([wd_tok, t_h] if fc == 0 else []) + \
                            (list(ctx.ps_tok[bd]) if fc == 0 else [])
                        t_d = sch.op("T", lambda t, e=e, mo=mo, tt=tt, fc=fc,
                                     bd=bd: t.matmul(
                            ctx.bank(bd), wd.ap()[:, fc, 128 * mo:128 * (mo + 1)],
                            hsb.ap()[:, fc, :],
                            start=(fc == 0), stop=(fc == FC - 1)),
                            deps=dd, ms=(fc == FC - 1))
                    d_last = t_d
                    prev_half_d = t_d
                    a = (mo + tt) % 2
                    prev = acc_tok.get((mo, tt))
                    if routed:
                        t_m = sch.op("V", lambda v, a=a, bd=bd, tt=tt:
                                     v.tensor_mul(
                            tmp[a].ap()[:], ctx.bank(bd),
                            wteb.ap()[:, 512 * tt:512 * (tt + 1)]),
                            deps=[t_d, wteb_tok] +
                            ([acc_tok[("tmp", a)]] if ("tmp", a) in acc_tok
                             else []), ms=True)
                        t_acc = sch.op("P", lambda p, mo=mo, tt=tt, a=a:
                                       p.tensor_add(
                            acc.ap()[:, mo, 512 * tt:512 * (tt + 1)],
                            acc.ap()[:, mo, 512 * tt:512 * (tt + 1)],
                            tmp[a].ap()[:]),
                            deps=[t_m] + ([prev] if prev else []), ms=True)
                        acc_tok[("tmp", a)] = t_acc
                        last_tm = t_m
                        ctx.ps_tok[bd] = (t_m,)
                    else:
                        if ei == 0:
                            t_acc = sch.op("V", lambda v, mo=mo, tt=tt, bd=bd:
                                           v.tensor_copy(
                                acc.ap()[:, mo, 512 * tt:512 * (tt + 1)],
                                ctx.bank(bd)), deps=[t_d], ms=True)
                        else:
                            t_acc = sch.op("V", lambda v, mo=mo, tt=tt, bd=bd:
                                           v.tensor_add(
                                acc.ap()[:, mo, 512 * tt:512 * (tt + 1)],
                                acc.ap()[:, mo, 512 * tt:512 * (tt + 1)],
                                ctx.bank(bd)),
                                deps=[t_d] + ([prev] if prev else []), ms=True)
                        ctx.ps_tok[bd] = (t_acc,)
                    acc_tok[(mo, tt)] = t_acc
        prev_exp_d = d_last
        if ei + 1 < len(experts):
            nkind, ne = experts[ei + 1]
            free_tok["g"] = g_last
            free_tok["u"] = u_last
            free_tok["d"] = d_last
            load_mat("sg" if nkind == "sh" else "eg", ne, wg, C, F_FF, "g")
            load_mat("su" if nkind == "sh" else "eu", ne, wu, C, F_FF, "u")
            load_mat("sd" if nkind == "sh" else "ed", ne, wd, FC, D, "d")
    t_st = sch.dma("Q", lambda q: q.dma_start(
        out=moe_d[:], in_=acc.ap().rearrange("p a b -> p (a b)")),
        deps=[tok for tok in acc_tok.values()])
    sch.barrier()
    for t_ in [xn, sel8, wte, wteT, idb, acc, wg, wu, wd, hsb, wteb] + \
            stg + gsil + tmp:
        sb.free_(t_)


def _tm_ln_stats(ctx, xt, stat, deps):
    sch = ctx.sch
    t_sum = sch.op("V", lambda v: v.reduce_sum(
        stat.ap()[:, 0:1], xt.ap()[:], axis=AX.X), deps=deps, ms=True)
    t_sq0 = sch.op("S", lambda s: s.activation(
        ctx.sq_buf.ap()[:], xt.ap()[:], AF.Square), deps=deps, ms=True)
    t_sq = sch.op("V", lambda v: v.reduce_sum(
        stat.ap()[:, 1:2], ctx.sq_buf.ap()[:], axis=AX.X), deps=[t_sq0],
        ms=True)
    t_mean = sch.op("V", lambda v: v.tensor_scalar(
        stat.ap()[:, 2:3], stat.ap()[:, 0:1], 1.0 / D, None, ALU.mult),
        sdeps=[t_sum], ms=True)
    t_msq = sch.op("V", lambda v: v.tensor_mul(
        stat.ap()[:, 3:4], stat.ap()[:, 2:3], stat.ap()[:, 2:3]),
        sdeps=[t_mean], ms=True)
    t_v0 = sch.op("V", lambda v: v.tensor_scalar(
        stat.ap()[:, 4:5], stat.ap()[:, 1:2], 1.0 / D, None, ALU.mult),
        sdeps=[t_sq], ms=True)
    t_var = sch.op("V", lambda v: v.tensor_scalar(
        stat.ap()[:, 5:6], stat.ap()[:, 4:5], stat.ap()[:, 3:4],
        float(LN_EPS), ALU.subtract, ALU.add), sdeps=[t_msq, t_v0], ms=True)
    t_ln = sch.op("S", lambda s: s.activation(
        stat.ap()[:, 6:7], stat.ap()[:, 5:6], AF.Ln), deps=[t_var], ms=True)
    t_half = sch.op("V", lambda v: v.tensor_scalar(
        stat.ap()[:, 6:7], stat.ap()[:, 6:7], -0.5, None, ALU.mult),
        deps=[t_ln], ms=True)
    t_rstd = sch.op("S", lambda s: s.activation(
        stat.ap()[:, 7:8], stat.ap()[:, 6:7], AF.Exp), deps=[t_half], ms=True)
    return stat.ap()[:, 2:3], stat.ap()[:, 7:8], (t_mean, t_rstd)


def _tm_ln_stats2(ctx, xt, stat, deps, sq_war=()):
    sch = ctx.sch
    t_sum = sch.op("V", lambda v: v.reduce_sum(
        stat.ap()[:, 0:1], xt.ap()[:], axis=AX.X), deps=deps, ms=True)
    t_sq0 = sch.op("S", lambda s: s.activation(
        ctx.sq_buf.ap()[:], xt.ap()[:], AF.Square),
        deps=list(deps) + [d for d in sq_war if d], ms=True)
    t_sqred = sch.op("V", lambda v: v.reduce_sum(
        stat.ap()[:, 1:2], ctx.sq_buf.ap()[:], axis=AX.X), deps=[t_sq0],
        ms=True)
    t_mean = sch.op("V", lambda v: v.tensor_scalar(
        stat.ap()[:, 2:3], stat.ap()[:, 0:1], 1.0 / D, None, ALU.mult),
        sdeps=[t_sum], ms=True)
    t_msq = sch.op("V", lambda v: v.tensor_mul(
        stat.ap()[:, 3:4], stat.ap()[:, 2:3], stat.ap()[:, 2:3]),
        sdeps=[t_mean], ms=True)
    t_v0 = sch.op("V", lambda v: v.tensor_scalar(
        stat.ap()[:, 4:5], stat.ap()[:, 1:2], 1.0 / D, None, ALU.mult),
        sdeps=[t_sqred], ms=True)
    t_var = sch.op("V", lambda v: v.tensor_scalar(
        stat.ap()[:, 5:6], stat.ap()[:, 4:5], stat.ap()[:, 3:4],
        float(LN_EPS), ALU.subtract, ALU.add), sdeps=[t_msq, t_v0], ms=True)
    t_ln = sch.op("S", lambda s: s.activation(
        stat.ap()[:, 6:7], stat.ap()[:, 5:6], AF.Ln), deps=[t_var], ms=True)
    t_half = sch.op("V", lambda v: v.tensor_scalar(
        stat.ap()[:, 6:7], stat.ap()[:, 6:7], -0.5, None, ALU.mult),
        deps=[t_ln], ms=True)
    t_rstd = sch.op("S", lambda s: s.activation(
        stat.ap()[:, 7:8], stat.ap()[:, 6:7], AF.Exp), deps=[t_half], ms=True)
    return stat.ap()[:, 2:3], stat.ap()[:, 7:8], (t_mean, t_rstd, t_sqred)


def phase_combine1(ctx):
    nc, sch, sb = ctx.nc, ctx.sch, ctx.sb
    x_in = ctx.dram_t("x", [T, D], F32).ap()
    s1_d = ctx.dram_t("s1_d", [128, C * T], BF16).ap()
    moe_d = ctx.dram_t("moe_d", [128, C * T], BF16).ap().rearrange(
        "p (c t) -> p c t", c=C)
    x1_d = ctx.dram_t("x1_d", [T, D], F32).ap()
    xn1_d = ctx.dram_t("xn1_bf", [128, C * T], BF16).ap()
    idb_d = ctx.dram_t("ident_bf", [128, 128], BF16).ap()

    p = sb.alloc("p", [128, C, T], BF16)
    mt = [sb.alloc("mt", [128, T], BF16) for _ in range(2)]
    gt = [sb.alloc("gt", [128, T], BF16) for _ in range(2)]
    idb = sb.alloc("idb", [128, 128], BF16)
    t_s1 = sch.dma("Q", lambda q: q.dma_start(out=p.ap()[:], in_=s1_d[:]))
    sch.dma("Q", lambda q: q.dma_start(out=idb.ap()[:], in_=idb_d[:]))
    mt_use = [None, None]
    p_tok = []
    for c in range(C):
        a = c % 2
        ld = sch.dma("Q", lambda q, c=c, a=a: q.dma_start(
            out=mt[a].ap()[:], in_=moe_d[:, c, :]),
            deps=[mt_use[a]] if mt_use[a] else [])
        t_g = sch.op("S", lambda s, a=a: s.activation(
            gt[a].ap()[:], mt[a].ap()[:], AF.Silu), deps=[ld], ms=True)
        mt_use[a] = t_g
        t_p = sch.op("V", lambda v, c=c, a=a: v.tensor_mul(
            p.ap()[:, c, :], p.ap()[:, c, :], gt[a].ap()[:]),
            deps=[t_g, t_s1], ms=True)
        p_tok.append(t_p)
    sch.barrier()
    for t_ in mt + gt:
        sb.free_(t_)

    x0t = [sb.alloc("x0t", [128, D], F32) for _ in range(2)]
    x1t = [sb.alloc("x1t", [128, D], F32) for _ in range(2)]
    xn1_tm = [sb.alloc("xn1_tm", [128, D], BF16) for _ in range(2)]
    stat = [sb.alloc("stat", [128, 8], F32) for _ in range(2)]
    ctx.sq_buf = sb.alloc("sqb", [128, D], F32)
    xn1_fm = sb.alloc("xn1_fm", [128, C, T], BF16)
    ld_tok = [None] * TT
    x0_use = [None, None]
    x1_use = [None, None]
    x1_dma = [None, None]
    xn1_use = [None, None]
    stat_use = [None, None]
    sq_use = None
    ld_tok[0] = sch.dma("Q", lambda q: q.dma_start(
        out=x0t[0].ap()[:], in_=x_in[0:128, :]))
    for i in range(TT):
        bi = i % 2
        if i + 1 < TT:
            ld_tok[i + 1] = sch.dma("Q", lambda q, i=i: q.dma_start(
                out=x0t[(i + 1) % 2].ap()[:],
                in_=x_in[128 * (i + 1):128 * (i + 2), :]),
                deps=[x0_use[(i + 1) % 2]] if x0_use[(i + 1) % 2] else [])
        adds = []
        for c in range(C):
            pb = c % 2
            t_tr = sch.op("T", lambda t, i=i, c=c, pb=pb: t.transpose(
                ctx.bank(pb, [128, 128], BF16), p.ap()[:, c, 128 * i:128 * (i + 1)],
                idb.ap()[:]), deps=list(ctx.ps_tok[pb]) + p_tok, ms=True)
            t_add = sch.op("V", lambda v, c=c, bi=bi, pb=pb: v.tensor_add(
                x1t[bi].ap()[:, 128 * c:128 * (c + 1)],
                x0t[bi].ap()[:, 128 * c:128 * (c + 1)],
                ctx.bank(pb, [128, 128], BF16)),
                deps=[t_tr, ld_tok[i]] +
                ([x1_use[bi], x1_dma[bi]] if (c == 0 and x1_use[bi]) else []),
                ms=True)
            ctx.ps_tok[pb] = (t_add,)
            adds.append(t_add)
        x0_use[bi] = adds[-1]
        x1_dma[bi] = sch.dma("Q", lambda q, i=i, bi=bi: q.dma_start(
            out=x1_d[128 * i:128 * (i + 1), :], in_=x1t[bi].ap()[:]),
            deps=[adds[-1]])
        sdep = [stat_use[bi]] if stat_use[bi] else []
        mean, rstd, (tm_, tr_, tsq_) = _tm_ln_stats2(
            ctx, x1t[bi], stat[bi], [adds[-1]] + sdep,
            [sq_use] if sq_use else [])
        sq_use = tsq_
        t_xn = sch.op("V", lambda v, bi=bi, mean=mean, rstd=rstd:
                      v.tensor_scalar(
            xn1_tm[bi].ap()[:], x1t[bi].ap()[:], mean, rstd,
            ALU.subtract, ALU.mult), deps=[tr_],
            sdeps=[tm_] + ([xn1_use[bi]] if xn1_use[bi] else []), ms=True)
        x1_use[bi] = t_xn
        stat_use[bi] = t_xn
        trs = []
        for c in range(C):
            pb = c % 2
            t_tr = sch.op("T", lambda t, c=c, bi=bi, pb=pb: t.transpose(
                ctx.bank(pb, [128, 128], BF16),
                xn1_tm[bi].ap()[:, 128 * c:128 * (c + 1)], idb.ap()[:]),
                deps=[t_xn] + list(ctx.ps_tok[pb]), ms=True)
            tok = sch.op("S", lambda s, i=i, c=c, pb=pb: s.copy(
                xn1_fm.ap()[:, c, 128 * i:128 * (i + 1)],
                ctx.bank(pb, [128, 128], BF16)), deps=[t_tr], ms=True)
            ctx.ps_tok[pb] = (tok,)
            trs.append(t_tr)
        xn1_use[bi] = trs[-1]
    sch.dma("Q", lambda q: q.dma_start(out=xn1_d[:], in_=xn1_fm.ap()[:]),
            deps=[("S", sch.ms["S"])])
    sch.barrier()
    for t_ in [p, idb, ctx.sq_buf, xn1_fm] + x0t + x1t + xn1_tm + stat:
        sb.free_(t_)


def phase_combine2(ctx):
    nc, sch, sb = ctx.nc, ctx.sch, ctx.sb
    x1_d = ctx.dram_t("x1_d", [T, D], F32).ap()
    s2_d = ctx.dram_t("s2_d", [128, C * T], BF16).ap()
    g2_d = ctx.dram_t("g2_d", [128, C * T], F32).ap().rearrange(
        "p (c t) -> p c t", c=C)
    out_d = ctx.dram_t("out", [T, D], F32).ap()
    idb_d = ctx.dram_t("ident_bf", [128, 128], BF16).ap()

    p = sb.alloc("p", [128, C, T], BF16)
    mt = [sb.alloc("mt", [128, T], F32) for _ in range(2)]
    gt = [sb.alloc("gt", [128, T], BF16) for _ in range(2)]
    idb = sb.alloc("idb", [128, 128], BF16)
    t_s2 = sch.dma("Q", lambda q: q.dma_start(out=p.ap()[:], in_=s2_d[:]))
    sch.dma("Q", lambda q: q.dma_start(out=idb.ap()[:], in_=idb_d[:]))
    mt_use = [None, None]
    for c in range(C):
        a = c % 2
        ld = sch.dma("Q", lambda q, c=c, a=a: q.dma_start(
            out=mt[a].ap()[:], in_=g2_d[:, c, :]),
            deps=[mt_use[a]] if mt_use[a] else [])
        t_g = sch.op("S", lambda s, a=a: s.activation(
            gt[a].ap()[:], mt[a].ap()[:], AF.Silu), deps=[ld], ms=True)
        mt_use[a] = t_g
        sch.op("V", lambda v, c=c, a=a: v.tensor_mul(
            p.ap()[:, c, :], p.ap()[:, c, :], gt[a].ap()[:]),
            deps=[t_g, t_s2], ms=True)
    sch.barrier()
    for t_ in mt + gt:
        sb.free_(t_)

    x1t = [sb.alloc("x1t", [128, D], F32) for _ in range(2)]
    ot = [sb.alloc("ot", [128, D], F32) for _ in range(2)]
    ld_tok = [None] * TT
    ot_use = [None, None]
    x1_use = [None, None]
    ld_tok[0] = sch.dma("Q", lambda q: q.dma_start(
        out=x1t[0].ap()[:], in_=x1_d[0:128, :]))
    st_tok = []
    for i in range(TT):
        bi = i % 2
        if i + 1 < TT:
            ld_tok[i + 1] = sch.dma("Q", lambda q, i=i: q.dma_start(
                out=x1t[(i + 1) % 2].ap()[:],
                in_=x1_d[128 * (i + 1):128 * (i + 2), :]),
                deps=[x1_use[(i + 1) % 2]] if x1_use[(i + 1) % 2] else [])
        adds = []
        for c in range(C):
            pb = c % 2
            t_tr = sch.op("T", lambda t, i=i, c=c, pb=pb: t.transpose(
                ctx.bank(pb, [128, 128], BF16), p.ap()[:, c, 128 * i:128 * (i + 1)],
                idb.ap()[:]), deps=list(ctx.ps_tok[pb]), ms=True)
            t_add = sch.op("V", lambda v, c=c, bi=bi, pb=pb: v.tensor_add(
                ot[bi].ap()[:, 128 * c:128 * (c + 1)],
                x1t[bi].ap()[:, 128 * c:128 * (c + 1)],
                ctx.bank(pb, [128, 128], BF16)),
                deps=[t_tr, ld_tok[i]] + ([ot_use[bi]] if ot_use[bi] else []),
                ms=True)
            ctx.ps_tok[pb] = (t_add,)
            adds.append(t_add)
        x1_use[bi] = adds[-1]
        ot_use[bi] = sch.dma("Q", lambda q, i=i, bi=bi: q.dma_start(
            out=out_d[128 * i:128 * (i + 1), :], in_=ot[bi].ap()[:]),
            deps=[adds[-1]])
        st_tok.append(ot_use[bi])
    sch.barrier()
    for t_ in [p, idb] + x1t + ot:
        sb.free_(t_)


def phase_l1attn(ctx):
    nc, sch, sb = ctx.nc, ctx.sch, ctx.sb
    xn_d = ctx.dram_t("xn1_bf", [128, C * T], BF16).ap()
    s2_d = ctx.dram_t("s2_d", [128, C * T], BF16).ap()
    wq_d = ctx.dram_t("wqbf", [128, C * D], BF16).ap()
    wk_d = ctx.dram_t("wkbf", [128, C * D], BF16).ap()
    wv_d = ctx.dram_t("wvbf", [128, C * D], BF16).ap()
    wo_d = ctx.dram_t("wobf", [128, C * D], BF16).ap()
    bq_d = ctx.dram_t("bq", [128, C], F32).ap()
    bk_d = ctx.dram_t("bk", [128, C], F32).ap()
    bkr_d = ctx.dram_t("bk_row", [1, D], F32).ap()
    bvr_d = ctx.dram_t("bv_row", [1, D], F32).ap()
    bo_d = ctx.dram_t("bo", [128, C], F32).ap()
    msk_d = ctx.dram_t("maskUT", [128, 128], BF16).ap()

    HT = 1024
    xn = sb.alloc("xn", [128, C, T], BF16)
    qT = sb.alloc("qT", [128, C, HT], BF16)
    kT = sb.alloc("kT", [128, C, HT], BF16)
    ktm = sb.alloc("ktm", [128, 8, HT], BF16)
    vtm = sb.alloc("vtm", [128, 8, HT], BF16)
    Ofm = sb.alloc("Ofm", [128, C, HT], BF16)
    s2 = sb.alloc("s2", [128, C, T], BF16)
    S_sb = sb.alloc("S_sb", [128, 8, DH], BF16)
    small = sb.alloc("small", [128, 3 * C], F32)
    rows = sb.alloc("rows", [1, 2 * D], F32)
    mask = sb.alloc("mask", [128, 128], BF16)
    zero = sb.alloc("zero", [128, 512], F32)
    wbuf = [sb.alloc("wbuf", [128, C, D], BF16) for _ in range(2)]
    phis = [sb.alloc("phis", [128, 512], F32) for _ in range(4)]
    ATm = [sb.alloc("ATm", [128, 128], BF16) for _ in range(2)]
    otmp = [sb.alloc("otmp", [128, 128], BF16) for _ in range(2)]

    t_xn = sch.dma("Q", lambda q: q.dma_start(out=xn.ap()[:], in_=xn_d[:]))
    for i, dd in enumerate([bq_d, bk_d, bo_d]):
        sch.dma("Q", lambda q, i=i, dd=dd: q.dma_start(
            out=small.ap()[:, i * C:(i + 1) * C], in_=dd[:]))
    sch.dma("Q", lambda q: q.dma_start(out=rows.ap()[:, 0:D], in_=bkr_d[:]))
    sch.dma("Q", lambda q: q.dma_start(out=rows.ap()[:, D:2 * D], in_=bvr_d[:]))
    sch.dma("Q", lambda q: q.dma_start(out=mask.ap()[:], in_=msk_d[:]))
    rowbf = sb.alloc("rowbf", [1, 2 * D], BF16)
    sch.barrier()
    sch.op("V", lambda v: v.memset(zero.ap()[:], 0.0), ms=True)
    t_rbf = sch.op("V", lambda v: v.tensor_copy(rowbf.ap()[:], rows.ap()[:]),
                   ms=True)
    ones1 = sb.alloc("ones1", [1, 128], BF16)
    t_o1 = sch.op("V", lambda v: v.memset(ones1.ap()[:], 1.0), ms=True)
    sch.barrier()
    bqc = lambda c: small.ap()[:, 0 * C + c:0 * C + c + 1]
    bkc = lambda c: small.ap()[:, 1 * C + c:1 * C + c + 1]
    boc = lambda c: small.ap()[:, 2 * C + c:2 * C + c + 1]

    wbuf_rd = [None, None]

    def loadw(j, dram):
        dd = [wbuf_rd[j]] if wbuf_rd[j] else []
        return sch.dma("Q", lambda q, j=j, dram=dram: q.dma_start(
            out=wbuf[j].ap().rearrange("p a b -> p (a b)"), in_=dram[:]),
            deps=dd)

    phi_war = {}

    def phi_evac(dst_ap, b, bias_col, deps, pp):
        t1 = sch.op("V", lambda v, b=b, pp=pp: v.scalar_tensor_tensor(
            phis[pp].ap()[:], ctx.bank(b), bias_col, zero.ap()[:],
            ALU.add, ALU.min),
            deps=deps + ([phi_war[pp]] if pp in phi_war else []), ms=True)
        t2 = sch.op("S", lambda s, pp=pp: s.activation(
            phis[pp + 1].ap()[:], phis[pp].ap()[:], AF.Exp),
            deps=[t1], ms=True)
        phi_war[pp] = t2
        t3 = sch.op("V", lambda v, b=b, pp=pp, dst_ap=dst_ap:
                    v.scalar_tensor_tensor(
            dst_ap, ctx.bank(b), bias_col, zero.ap()[:],
            ALU.add, ALU.max), ms=True)
        t4 = sch.op("V", lambda v, pp=pp, dst_ap=dst_ap: v.tensor_add(
            dst_ap, dst_ap, phis[pp + 1].ap()[:]), deps=[t2], ms=True)
        return t4

    s2_tok = []
    chT = {}
    for half in range(2):
        t0 = half * HT
        wtok = loadw(0, wq_d)
        for mo in range(C):
            for tt in range(2):
                b = (mo * 2 + tt) % 2
                t_mm = None
                for kc in range(C):
                    dd = ([wtok, t_xn] if kc == 0 else []) + \
                        (list(ctx.ps_tok[b]) if kc == 0 else [])
                    t_mm = sch.op("T", lambda t, mo=mo, tt=tt, kc=kc, b=b,
                                  t0=t0: t.matmul(
                        ctx.bank(b),
                        wbuf[0].ap()[:, kc, 128 * mo:128 * (mo + 1)],
                        xn.ap()[:, kc, t0 + 512 * tt:t0 + 512 * (tt + 1)],
                        start=(kc == 0), stop=(kc == C - 1)),
                        deps=dd, ms=(kc == C - 1))
                tok = phi_evac(qT.ap()[:, mo, 512 * tt:512 * (tt + 1)], b,
                               bqc(mo), [t_mm], (mo * 2 + tt) % 2 * 2)
                ctx.ps_tok[b] = (tok,)
        wbuf_rd[0] = t_mm
        wtok = loadw(1, wk_d)
        for mo in range(C):
            for tt in range(2):
                b = (mo * 2 + tt) % 2
                t_mm = None
                for kc in range(C):
                    dd = ([wtok] if kc == 0 else []) + \
                        (list(ctx.ps_tok[b]) if kc == 0 else [])
                    t_mm = sch.op("T", lambda t, mo=mo, tt=tt, kc=kc, b=b,
                                  t0=t0: t.matmul(
                        ctx.bank(b),
                        wbuf[1].ap()[:, kc, 128 * mo:128 * (mo + 1)],
                        xn.ap()[:, kc, t0 + 512 * tt:t0 + 512 * (tt + 1)],
                        start=(kc == 0), stop=(kc == C - 1)),
                        deps=dd, ms=(kc == C - 1))
                tok = phi_evac(kT.ap()[:, mo, 512 * tt:512 * (tt + 1)], b,
                               bkc(mo), [t_mm], (mo * 2 + tt) % 2 * 2)
                ctx.ps_tok[b] = (tok,)
        wtok_v = loadw(0, wv_d)
        for it in range(8):
            for nn in range(2):
                b = (it * 2 + nn) % 2
                t_mm = None
                for kc in range(C):
                    dd = (list(ctx.ps_tok[b]) if kc == 0 else [])
                    t_mm = sch.op("T", lambda t, it=it, nn=nn, kc=kc, b=b,
                                  t0=t0: t.matmul(
                        ctx.bank(b), xn.ap()[:, kc,
                                             t0 + 128 * it:t0 + 128 * (it + 1)],
                        wbuf[1].ap()[:, kc, 512 * nn:512 * (nn + 1)],
                        start=(kc == 0), stop=False),
                        deps=dd, ms=False)
                t_mm = sch.op("T", lambda t, nn=nn, b=b: t.matmul(
                    ctx.bank(b), ones1.ap()[:],
                    rowbf.ap()[:, 512 * nn:512 * (nn + 1)],
                    start=False, stop=True), deps=[t_rbf, t_o1], ms=True)
                tok = phi_evac(
                    ktm.ap()[:, it, 512 * nn:512 * (nn + 1)], b,
                    zero.ap()[:, 0:1], [t_mm], (it * 2 + nn) % 2 * 2)
                ctx.ps_tok[b] = (tok,)
        wbuf_rd[1] = t_mm
        for it in range(8):
            for nn in range(2):
                b = (it * 2 + nn) % 2
                t_mm = None
                for kc in range(C):
                    dd = ([wtok_v] if kc == 0 else []) + \
                        (list(ctx.ps_tok[b]) if kc == 0 else [])
                    t_mm = sch.op("T", lambda t, it=it, nn=nn, kc=kc, b=b,
                                  t0=t0: t.matmul(
                        ctx.bank(b), xn.ap()[:, kc,
                                             t0 + 128 * it:t0 + 128 * (it + 1)],
                        wbuf[0].ap()[:, kc, 512 * nn:512 * (nn + 1)],
                        start=(kc == 0), stop=False),
                        deps=dd, ms=False)
                t_mm = sch.op("T", lambda t, nn=nn, b=b: t.matmul(
                    ctx.bank(b), ones1.ap()[:],
                    rowbf.ap()[:, D + 512 * nn:D + 512 * (nn + 1)],
                    start=False, stop=True), ms=True)
                tok = sch.op("S", lambda s, it=it, nn=nn, b=b: s.copy(
                    vtm.ap()[:, it, 512 * nn:512 * (nn + 1)], ctx.bank(b)),
                    deps=[t_mm], ms=True)
                ctx.ps_tok[b] = (tok,)
        wbuf_rd[0] = t_mm
        sch.barrier()
        for cc in range(8):
            gc = half * 8 + cc
            for h in range(NHEAD):
                i = gc * NHEAD + h
                bh = (h % 2) * 64
                ch = h // 2
                hsl = slice(64 * h - 1024 * (h // 16), 64 * h + 64 - 0)
                bA = i % 2
                bI = 2 + i % 2
                bJ = 4 + i % 2
                bS = 6 + i % 2
                t_at = sch.op("T", lambda t, cc=cc, ch=ch, bh=bh, bA=bA:
                              t.matmul(
                    ctx.bank(bA)[:, 0:128],
                    kT.ap()[:, ch, 128 * cc:128 * (cc + 1)]
                    [bh:bh + 64, :],
                    qT.ap()[:, ch, 128 * cc:128 * (cc + 1)][bh:bh + 64, :],
                    start=True, stop=True),
                    deps=list(ctx.ps_tok[bA]), ms=True)
                t_msk = sch.op("V", lambda v, i=i, bA=bA: v.tensor_mul(
                    ATm[i % 2].ap()[:], ctx.bank(bA)[:, 0:128], mask.ap()[:]),
                    deps=[t_at] + ([chT.get(("atm", i % 2))] if
                                   chT.get(("atm", i % 2)) else []), ms=True)
                chT[("atm", i % 2)] = t_msk
                ctx.ps_tok[bA] = (t_msk,)
                t_in = sch.op("T", lambda t, cc=cc, h=h, bh=bh, bI=bI, i=i:
                              t.matmul(
                    ctx.bank(bI)[bh:bh + 64, 0:128],
                    vtm.ap()[:, cc, 64 * h:64 * h + 64], ATm[i % 2].ap()[:],
                    start=True, stop=True),
                    deps=[t_msk] + list(ctx.ps_tok[bI]), ms=True)
                prev_s = chT.get(("s", h))
                if gc > 0:
                    t_it = sch.op("T", lambda t, cc=cc, ch=ch, bh=bh, bJ=bJ,
                                  h=h: t.matmul(
                        ctx.bank(bJ)[bh:bh + 64, 0:128],
                        S_sb.ap()[:, ch, :][bh:bh + 64, :],
                        qT.ap()[:, ch, 128 * cc:128 * (cc + 1)][bh:bh + 64, :],
                        start=True, stop=True),
                        deps=[prev_s] + list(ctx.ps_tok[bJ]), ms=True)
                    t_oc = sch.op("S", lambda s, bh=bh, bJ=bJ, i=i: s.copy(
                        otmp[i % 2].ap()[bh:bh + 64, :],
                        ctx.bank(bJ)[bh:bh + 64, 0:128]),
                        deps=[t_it] + ([chT.get(("ot", i % 2))] if
                                       chT.get(("ot", i % 2)) else []),
                        ms=True)
                    chT[("ot", i % 2)] = t_oc
                    ctx.ps_tok[bJ] = (t_oc,)
                    t_o = sch.op("V", lambda v, cc=cc, h=h, bh=bh, bI=bI, i=i:
                                 v.tensor_add(
                        Ofm.ap()[:, h // 2, 128 * cc:128 * (cc + 1)]
                        [bh:bh + 64, :], ctx.bank(bI)[bh:bh + 64, 0:128],
                        otmp[i % 2].ap()[bh:bh + 64, :]),
                        deps=[t_in, t_oc], ms=True)
                else:
                    t_o = sch.op("V", lambda v, cc=cc, h=h, bh=bh, bI=bI:
                                 v.tensor_copy(
                        Ofm.ap()[:, h // 2, 128 * cc:128 * (cc + 1)]
                        [bh:bh + 64, :], ctx.bank(bI)[bh:bh + 64, 0:128]),
                        deps=[t_in], ms=True)
                ctx.ps_tok[bI] = (t_o,)
                t_ds = sch.op("T", lambda t, cc=cc, h=h, bh=bh, bS=bS:
                              t.matmul(
                    ctx.bank(bS)[bh:bh + 64, 0:64],
                    ktm.ap()[:, cc, 64 * h:64 * h + 64],
                    vtm.ap()[:, cc, 64 * h:64 * h + 64],
                    start=True, stop=True),
                    deps=list(ctx.ps_tok[bS]), ms=True)
                if gc > 0:
                    t_s = sch.op("V", lambda v, h=h, ch=ch, bh=bh, bS=bS:
                                 v.tensor_add(
                        S_sb.ap()[:, ch, :][bh:bh + 64, :],
                        S_sb.ap()[:, ch, :][bh:bh + 64, :],
                        ctx.bank(bS)[bh:bh + 64, 0:64]),
                        deps=[t_ds, t_it], ms=True)
                else:
                    t_s = sch.op("V", lambda v, h=h, ch=ch, bh=bh, bS=bS:
                                 v.tensor_copy(
                        S_sb.ap()[:, ch, :][bh:bh + 64, :],
                        ctx.bank(bS)[bh:bh + 64, 0:64]),
                        deps=[t_ds], ms=True)
                chT[("s", h)] = t_s
                ctx.ps_tok[bS] = (t_s,)
        sch.barrier()
        wtok = loadw(1, wo_d)
        for mo in range(C):
            for tt in range(2):
                b = (mo * 2 + tt) % 2
                t_mm = None
                for kc in range(C):
                    dd = ([wtok] if kc == 0 else []) + \
                        (list(ctx.ps_tok[b]) if kc == 0 else [])
                    t_mm = sch.op("T", lambda t, mo=mo, tt=tt, kc=kc, b=b:
                                  t.matmul(
                        ctx.bank(b),
                        wbuf[1].ap()[:, kc, 128 * mo:128 * (mo + 1)],
                        Ofm.ap()[:, kc, 512 * tt:512 * (tt + 1)],
                        start=(kc == 0), stop=(kc == C - 1)),
                        deps=dd, ms=(kc == C - 1))
                tok = sch.op("V", lambda v, mo=mo, tt=tt, b=b, t0=t0:
                             v.tensor_scalar(
                    s2.ap()[:, mo, t0 + 512 * tt:t0 + 512 * (tt + 1)],
                    ctx.bank(b), boc(mo), None, ALU.add),
                    deps=[t_mm], ms=True)
                ctx.ps_tok[b] = (tok,)
                s2_tok.append(tok)
        wbuf_rd[1] = t_mm
        sch.barrier()
    sch.dma("Q", lambda q: q.dma_start(out=s2_d[:], in_=s2.ap()[:]),
            deps=[s2_tok[-1]])
    sch.barrier()
    for t_ in ([xn, qT, kT, ktm, vtm, Ofm, s2, S_sb, small, rows, mask, zero,
                rowbf, ones1] + wbuf + phis + ATm + otmp):
        sb.free_(t_)


def _row_chain(ctx, ops):
    sch = ctx.sch
    tok = None
    for eng, fn in ops:
        if eng == "S":
            tok = sch.op("S", fn, deps=[tok] if tok else [], ms=True)
        else:
            tok = sch.op("V", fn, deps=[tok] if tok else [],
                         sdeps=[tok] if tok else [], ms=True)
    return tok


def phase_l1ffn(ctx):
    nc, sch, sb = ctx.nc, ctx.sch, ctx.sb
    ES = 256
    NE, NS, NH = 16, 16, 256
    xn_d = ctx.dram_t("xn1_bf", [128, C * T], BF16).ap()
    g2_d = ctx.dram_t("g2_d", [128, C * T], F32).ap().rearrange(
        "p (c t) -> p c t", c=C)
    mu_d = ctx.dram_t("mu_fm", [128, C, ES], F32).ap()
    lg_g = ctx.dram_t("l1_gamma", [128, C], F32).ap()
    lg_b = ctx.dram_t("l1_beta", [128, C], F32).ap()
    sc_d = ctx.dram_t("l1_scale", [1, 1], F32).ap()
    w1_d = ctx.dram_t("w1_l1", [NE, 128, C, NH], F32).ap()
    b1_d = ctx.dram_t("b1_l1", [NE, NH], F32).ap()
    w2_d = ctx.dram_t("w2_l1", [NE, 128, 2, D], F32).ap()
    b2_d = ctx.dram_t("b2_l1", [NE, D], F32).ap()
    s16_d = ctx.dram_t("sel16", [128, 8 * 16], BF16).ap()
    idb_d = ctx.dram_t("ident_bf", [128, 128], BF16).ap()
    idf_d = ctx.dram_t("ident_f32", [128, 128], F32).ap()

    xn = sb.alloc("xn", [128, C, T], BF16)
    xnl = sb.alloc("xnl", [128, C, T], BF16)
    small = sb.alloc("small", [128, 2 * C], F32)
    rows = sb.alloc("rows", [1, 8 * T], F32)
    ones_bf = sb.alloc("ones_bf", [128, 1], BF16)
    ones_f = sb.alloc("ones_f", [1, 128], F32)
    sq = [sb.alloc("sq", [128, 512], BF16) for _ in range(2)]
    tm1 = [sb.alloc("tm1", [128, 512], F32) for _ in range(2)]
    idb = sb.alloc("idb", [128, 128], BF16)
    idf = sb.alloc("idf", [128, 128], F32)
    scl = sb.alloc("scl", [1, 1], F32)
    t_xn = sch.dma("Q", lambda q: q.dma_start(out=xn.ap()[:], in_=xn_d[:]))
    sch.dma("Q", lambda q: q.dma_start(out=small.ap()[:, 0:C], in_=lg_g[:]))
    sch.dma("Q", lambda q: q.dma_start(out=small.ap()[:, C:2 * C], in_=lg_b[:]))
    sch.dma("Q", lambda q: q.dma_start(out=idb.ap()[:], in_=idb_d[:]))
    sch.dma("Q", lambda q: q.dma_start(out=idf.ap()[:], in_=idf_d[:]))
    sch.dma("Q", lambda q: q.dma_start(out=scl.ap()[:], in_=sc_d[:]))
    sch.barrier()
    sch.op("V", lambda v: v.memset(ones_bf.ap()[:], 1.0), ms=True)
    sch.op("V", lambda v: v.memset(ones_f.ap()[:], 1.0), ms=True)
    sch.barrier()
    gcol = lambda c: small.ap()[:, c:c + 1]
    bcol = lambda c: small.ap()[:, C + c:C + c + 1]
    row = lambda r, tt=None, w=T, n=512: (
        rows.ap()[:, r * T:r * T + w] if tt is None
        else rows.ap()[:, r * T + n * tt:r * T + n * (tt + 1)])

    def fm_stats(src, r0, r1):
        for tt in range(4):
            bs_, bq = 4 + (tt % 2) * 2, 5 + (tt % 2) * 2
            mqs = {}
            t_mm = t_mq = None
            for c in range(C):
                a = c % 2
                dd = list(ctx.ps_tok[bs_]) if c == 0 else []
                t_mm = sch.op("T", lambda t, c=c, tt=tt, bs_=bs_: t.matmul(
                    ctx.bank(bs_, [1, 512]), ones_bf.ap()[:],
                    src.ap()[:, c, 512 * tt:512 * (tt + 1)],
                    start=(c == 0), stop=(c == C - 1)), deps=dd,
                    ms=(c == C - 1))
                t_sq = sch.op("S", lambda s, c=c, tt=tt, a=a: s.activation(
                    sq[a].ap()[:], src.ap()[:, c, 512 * tt:512 * (tt + 1)],
                    AF.Square), deps=[mqs[c - 2]] if c >= 2 else [], ms=True)
                dd = list(ctx.ps_tok[bq]) if c == 0 else []
                t_mq = sch.op("T", lambda t, c=c, bq=bq, a=a: t.matmul(
                    ctx.bank(bq, [1, 512]), ones_bf.ap()[:], sq[a].ap()[:],
                    start=(c == 0), stop=(c == C - 1)),
                    deps=[t_sq] + dd, ms=True)
                mqs[c] = t_mq
            e1 = sch.op("S", lambda s, tt=tt, bs_=bs_, r0=r0: s.copy(
                row(r0, tt), ctx.bank(bs_, [1, 512])), deps=[t_mm], ms=True)
            e2 = sch.op("S", lambda s, tt=tt, bq=bq, r1=r1: s.copy(
                row(r1, tt), ctx.bank(bq, [1, 512])), deps=[t_mq], ms=True)
            ctx.ps_tok[bs_] = (e1,)
            ctx.ps_tok[bq] = (e2,)

    fm_stats(xn, 0, 1)
    sch.barrier()
    _row_chain(ctx, [
        ("V", lambda v: v.tensor_scalar(row(2), row(0), 1.0 / D, None,
                                        ALU.mult)),
        ("V", lambda v: v.tensor_mul(row(3), row(2), row(2))),
        ("V", lambda v: v.tensor_scalar(row(4), row(1), 1.0 / D, None,
                                        ALU.mult)),
        ("V", lambda v: v.tensor_sub(row(4), row(4), row(3))),
        ("V", lambda v: v.tensor_scalar(row(4), row(4), float(LN_EPS), None,
                                        ALU.add)),
        ("S", lambda s: s.activation(row(5), row(4), AF.Ln)),
        ("V", lambda v: v.tensor_scalar(row(5), row(5), -0.5, None, ALU.mult)),
        ("S", lambda s: s.activation(row(6), row(5), AF.Exp)),
    ])
    sch.barrier()
    for tt in range(4):
        bm, br = (tt % 2) * 2, (tt % 2) * 2 + 1
        dd = list(ctx.ps_tok[bm]) + list(ctx.ps_tok[br])
        t_bm = sch.op("T", lambda t, tt=tt, bm=bm: t.matmul(
            ctx.bank(bm), ones_f.ap()[:], row(2, tt), start=True, stop=True),
            deps=dd, ms=True)
        t_br = sch.op("T", lambda t, tt=tt, br=br: t.matmul(
            ctx.bank(br), ones_f.ap()[:], row(6, tt), start=True, stop=True),
            ms=True)
        apl = []
        for c in range(C):
            a = c % 2
            w1t = sch.op("V", lambda v, c=c, tt=tt, bm=bm, a=a: v.tensor_sub(
                tm1[a].ap()[:], xn.ap()[:, c, 512 * tt:512 * (tt + 1)],
                ctx.bank(bm)), deps=[t_bm, t_br], ms=True)
            sch.op("V", lambda v, a=a, br=br: v.tensor_mul(
                tm1[a].ap()[:], tm1[a].ap()[:], ctx.bank(br)), ms=True)
            w3 = sch.op("V", lambda v, c=c, tt=tt, a=a: v.tensor_scalar(
                xnl.ap()[:, c, 512 * tt:512 * (tt + 1)], tm1[a].ap()[:],
                gcol(c), bcol(c), ALU.mult, ALU.add), ms=True)
            apl.append(w3)
        ctx.ps_tok[bm] = (apl[-1],)
        ctx.ps_tok[br] = (apl[-1],)
    sch.barrier()
    sb.free_(xn)

    mu = sb.alloc("mu", [128, C, ES], F32)
    muu = sb.alloc("muu", [128, C, ES], BF16)
    invb = sb.alloc("invb", [128, T], BF16)
    t_mu = sch.dma("Q", lambda q: q.dma_start(
        out=mu.ap().rearrange("p a b -> p (a b)"), in_=mu_d[:]))
    for c in range(C):
        a = c % 2
        t_s = sch.op("S", lambda s, c=c, a=a: s.activation(
            sq[a].ap()[:, 0:ES], mu.ap()[:, c, :], AF.Square),
            deps=[t_mu], ms=True)
        sch.op("T", lambda t, c=c, a=a: t.matmul(
            ctx.bank(6, [1, ES]), ones_bf.ap()[:], sq[a].ap()[:, 0:ES],
            start=(c == 0), stop=(c == C - 1)),
            deps=[t_s] + (list(ctx.ps_tok[6]) if c == 0 else []), ms=True)
    t_ev = sch.op("S", lambda s: s.copy(row(0, None, ES), ctx.bank(6, [1, ES])),
                  deps=[("T", sch.ms["T"])], ms=True)
    ctx.ps_tok[6] = (t_ev,)
    _row_chain(ctx, [
        ("S", lambda s: s.activation(row(1, None, ES), row(0, None, ES),
                                     AF.Ln)),
        ("V", lambda v: v.tensor_scalar(row(1, None, ES), row(1, None, ES),
                                        0.5, None, ALU.mult)),
        ("S", lambda s: s.activation(row(2, None, ES), row(1, None, ES),
                                     AF.Exp)),
        ("V", lambda v: v.tensor_scalar(row(2, None, ES), row(2, None, ES),
                                        1e-12, None, ALU.max)),
        ("V", lambda v: v.reciprocal(row(3, None, ES), row(2, None, ES))),
    ])
    sch.barrier()
    t_bc = sch.op("T", lambda t: t.matmul(
        ctx.bank(6, [128, ES]), ones_f.ap()[:], row(3, None, ES),
        start=True, stop=True), ms=True)
    mu_tok = []
    for c in range(C):
        mu_tok.append(sch.op("V", lambda v, c=c: v.tensor_mul(
            muu.ap()[:, c, :], mu.ap()[:, c, :], ctx.bank(6, [128, ES])),
            deps=[t_bc], ms=True))
    ctx.ps_tok[6] = tuple(mu_tok[-1:])
    fm_stats(xnl, 4, 5)
    sch.barrier()
    _row_chain(ctx, [
        ("S", lambda s: s.activation(row(6), row(5), AF.Ln)),
        ("V", lambda v: v.tensor_scalar(row(6), row(6), 0.5, None, ALU.mult)),
        ("S", lambda s: s.activation(row(7), row(6), AF.Exp)),
        ("V", lambda v: v.tensor_scalar(row(7), row(7), 1e-12, None, ALU.max)),
        ("V", lambda v: v.reciprocal(row(6), row(7))),
        ("V", lambda v: v.tensor_scalar(row(6), row(6), scl.ap()[0:1, 0:1],
                                        None, ALU.mult)),
    ])
    sch.barrier()
    for tt in range(4):
        b = tt % 2
        t_b = sch.op("T", lambda t, tt=tt, b=b: t.matmul(
            ctx.bank(b), ones_f.ap()[:], row(6, tt), start=True, stop=True),
            deps=list(ctx.ps_tok[b]), ms=True)
        tok = sch.op("S", lambda s, tt=tt, b=b: s.copy(
            invb.ap()[:, 512 * tt:512 * (tt + 1)], ctx.bank(b)),
            deps=[t_b], ms=True)
        ctx.ps_tok[b] = (tok,)
    sch.barrier()
    sb.free_(mu)

    lgf = sb.alloc("lgf", [128, 2, T], F32)
    for m2 in range(2):
        for tt in range(4):
            b = (m2 * 4 + tt) % 2
            t_mm = None
            for kc in range(C):
                dd = (list(ctx.ps_tok[b]) if kc == 0 else [])
                t_mm = sch.op("T", lambda t, m2=m2, tt=tt, kc=kc, b=b:
                              t.matmul(
                    ctx.bank(b), muu.ap()[:, kc, 128 * m2:128 * (m2 + 1)],
                    xnl.ap()[:, kc, 512 * tt:512 * (tt + 1)],
                    start=(kc == 0), stop=(kc == C - 1)), deps=dd,
                    ms=(kc == C - 1))
            tok = sch.op("V", lambda v, m2=m2, tt=tt, b=b: v.tensor_mul(
                lgf.ap()[:, m2, 512 * tt:512 * (tt + 1)], ctx.bank(b),
                invb.ap()[:, 512 * tt:512 * (tt + 1)]), deps=[t_mm], ms=True)
            ctx.ps_tok[b] = (tok,)
    sch.barrier()

    dspf = sb.alloc("dspf", [128, 2, T], BF16)
    dtmp = sb.alloc("dtmp", [128, T], F32)
    cols = sb.alloc("cols", [128, 8], F32)
    for m2 in range(2):
        t1 = sch.op("V", lambda v, m2=m2: v.reduce_max(
            cols.ap()[:, 0:1], lgf.ap()[:, m2, :], axis=AX.X), ms=True)
        t2 = sch.op("V", lambda v: v.tensor_scalar(
            cols.ap()[:, 1:2], cols.ap()[:, 0:1], -1.0, None, ALU.mult),
            sdeps=[t1], ms=True)
        t3 = sch.op("S", lambda s, m2=m2: s.activation(
            dtmp.ap()[:], lgf.ap()[:, m2, :], AF.Exp,
            bias=cols.ap()[:, 1:2]), deps=[t2], ms=True)
        t4 = sch.op("V", lambda v: v.reduce_sum(
            cols.ap()[:, 2:3], dtmp.ap()[:], axis=AX.X), deps=[t3], ms=True)
        t5 = sch.op("V", lambda v: v.reciprocal(
            cols.ap()[:, 3:4], cols.ap()[:, 2:3]), sdeps=[t4], ms=True)
        t6 = sch.op("V", lambda v, m2=m2: v.tensor_scalar(
            dspf.ap()[:, m2, :], dtmp.ap()[:], cols.ap()[:, 3:4], None,
            ALU.mult), sdeps=[t5], ms=True)
    sch.barrier()
    dspt = sb.alloc("dspt", [128, TT, ES], BF16)
    lgt = sb.alloc("lgt", [128, TT, ES], F32)
    for it in range(TT):
        for m2 in range(2):
            b = (it * 2 + m2) % 2
            t_tr = sch.op("T", lambda t, it=it, m2=m2, b=b: t.transpose(
                ctx.bank(b, [128, 128], BF16),
                dspf.ap()[:, m2, 128 * it:128 * (it + 1)], idb.ap()[:]),
                deps=list(ctx.ps_tok[b]), ms=True)
            tok = sch.op("S", lambda s, it=it, m2=m2, b=b: s.copy(
                dspt.ap()[:, it, 128 * m2:128 * (m2 + 1)],
                ctx.bank(b, [128, 128], BF16)), deps=[t_tr], ms=True)
            ctx.ps_tok[b] = (tok,)
            b2_ = 2 + (it * 2 + m2) % 2
            t_tr2 = sch.op("T", lambda t, it=it, m2=m2, b2_=b2_: t.transpose(
                ctx.bank(b2_, [128, 128]),
                lgf.ap()[:, m2, 128 * it:128 * (it + 1)], idf.ap()[:]),
                deps=list(ctx.ps_tok[b2_]), ms=True)
            tok2 = sch.op("S", lambda s, it=it, m2=m2, b2_=b2_: s.copy(
                lgt.ap()[:, it, 128 * m2:128 * (m2 + 1)],
                ctx.bank(b2_, [128, 128])), deps=[t_tr2], ms=True)
            ctx.ps_tok[b2_] = (tok2,)
    sch.barrier()
    sb.free_(lgf)
    sb.free_(dspf)
    sb.free_(dtmp)
    cmbt = sb.alloc("cmbt", [128, TT, ES], BF16)
    ctmp = sb.alloc("ctmp", [128, ES], F32)
    for it in range(TT):
        t1 = sch.op("V", lambda v, it=it: v.reduce_max(
            cols.ap()[:, 0:1], lgt.ap()[:, it, :], axis=AX.X), ms=True)
        t2 = sch.op("V", lambda v: v.tensor_scalar(
            cols.ap()[:, 1:2], cols.ap()[:, 0:1], -1.0, None, ALU.mult),
            sdeps=[t1], ms=True)
        t3 = sch.op("S", lambda s, it=it: s.activation(
            ctmp.ap()[:], lgt.ap()[:, it, :], AF.Exp,
            bias=cols.ap()[:, 1:2]), deps=[t2], ms=True)
        t4 = sch.op("V", lambda v: v.reduce_sum(
            cols.ap()[:, 2:3], ctmp.ap()[:], axis=AX.X), deps=[t3], ms=True)
        t5 = sch.op("V", lambda v: v.reciprocal(
            cols.ap()[:, 3:4], cols.ap()[:, 2:3]), sdeps=[t4], ms=True)
        t6 = sch.op("V", lambda v, it=it: v.tensor_scalar(
            cmbt.ap()[:, it, :], ctmp.ap()[:], cols.ap()[:, 3:4], None,
            ALU.mult), sdeps=[t5], ms=True)
    sch.barrier()
    sb.free_(lgt)
    sb.free_(ctmp)
    cmbf = sb.alloc("cmbf", [128, 2, T], BF16)
    for it in range(TT):
        for m2 in range(2):
            b = (it * 2 + m2) % 2
            t_tr = sch.op("T", lambda t, it=it, m2=m2, b=b: t.transpose(
                ctx.bank(b, [128, 128], BF16),
                cmbt.ap()[:, it, 128 * m2:128 * (m2 + 1)], idb.ap()[:]),
                deps=list(ctx.ps_tok[b]), ms=True)
            tok = sch.op("S", lambda s, it=it, m2=m2, b=b: s.copy(
                cmbf.ap()[:, m2, 128 * it:128 * (it + 1)],
                ctx.bank(b, [128, 128], BF16)), deps=[t_tr], ms=True)
            ctx.ps_tok[b] = (tok,)
    sch.barrier()
    sb.free_(cmbt)

    xnlt = sb.alloc("xnlt", [128, TT, D], BF16)
    for it in range(TT):
        for c in range(C):
            b = (it * C + c) % 2
            t_tr = sch.op("T", lambda t, it=it, c=c, b=b: t.transpose(
                ctx.bank(b, [128, 128], BF16),
                xnl.ap()[:, c, 128 * it:128 * (it + 1)], idb.ap()[:]),
                deps=list(ctx.ps_tok[b]), ms=True)
            tok = sch.op("S", lambda s, it=it, c=c, b=b: s.copy(
                xnlt.ap()[:, it, 128 * c:128 * (c + 1)],
                ctx.bank(b, [128, 128], BF16)), deps=[t_tr], ms=True)
            ctx.ps_tok[b] = (tok,)
    sch.barrier()
    sb.free_(xnl)
    slin = sb.alloc("slin", [128, C, ES], BF16)
    for mm in range(C):
        b = 2 + mm % 2
        t_mm = None
        for it in range(TT):
            dd = (list(ctx.ps_tok[b]) if it == 0 else [])
            t_mm = sch.op("T", lambda t, mm=mm, it=it, b=b: t.matmul(
                ctx.bank(b, [128, ES]), xnlt.ap()[:, it, 128 * mm:128 * (mm + 1)],
                dspt.ap()[:, it, :], start=(it == 0), stop=(it == TT - 1)),
                deps=dd, ms=(it == TT - 1))
        tok = sch.op("S", lambda s, mm=mm, b=b: s.copy(
            slin.ap()[:, mm, :], ctx.bank(b, [128, ES])), deps=[t_mm], ms=True)
        ctx.ps_tok[b] = (tok,)
    sch.barrier()
    sb.free_(xnlt)
    sb.free_(dspt)

    w1s = [sb.alloc("w1s", [128, C, NH], BF16) for _ in range(2)]
    w2s = [sb.alloc("w2s", [128, 2, D], BF16) for _ in range(2)]
    wst = sb.alloc("wst", [128, 2 * D], F32)
    brow = sb.alloc("brow", [1, NH + D], F32)
    browb = sb.alloc("browb", [1, NH + D], BF16)
    onesr16 = sb.alloc("onesr16", [1, 16], BF16)
    t_o16 = sch.op("V", lambda v: v.memset(onesr16.ap()[:], 1.0), ms=True)
    hsb = sb.alloc("hsb", [128, 2, NS], BF16)
    soall = sb.alloc("soall", [16, NE, D], BF16)
    sel16 = sb.alloc("sel16", [128, 8, 16], BF16)
    sch.dma("Q", lambda q: q.dma_start(
        out=sel16.ap().rearrange("p a b -> p (a b)"), in_=s16_d[:]))
    wtoks = {}
    btoks = {}
    st_use = [None]

    def load_we(e, buf):
        sd = [st_use[0]] if st_use[0] else []
        l1 = sch.dma("Q", lambda q, e=e: q.dma_start(
            out=wst.ap()[:], in_=w1_d[e].rearrange("p a b -> p (a b)")),
            deps=sd)
        c1 = sch.op("V", lambda v, buf=buf: v.tensor_copy(
            w1s[buf].ap().rearrange("p a b -> p (a b)"), wst.ap()[:]),
            deps=[l1] + ([wtoks.get(("u1", buf))] if wtoks.get(("u1", buf))
                         else []), ms=True)
        l2 = sch.dma("Q", lambda q, e=e: q.dma_start(
            out=wst.ap()[:], in_=w2_d[e].rearrange("p a b -> p (a b)")),
            deps=[c1])
        c2 = sch.op("V", lambda v, buf=buf: v.tensor_copy(
            w2s[buf].ap().rearrange("p a b -> p (a b)"), wst.ap()[:]),
            deps=[l2] + ([wtoks.get(("u2", buf))] if wtoks.get(("u2", buf))
                         else []), ms=True)
        st_use[0] = c2
        lb1 = sch.dma("Q", lambda q, e=e: q.dma_start(
            out=brow.ap()[:, 0:NH], in_=b1_d[e:e + 1, :]),
            deps=[btoks[e - 1]] if (e - 1) in btoks else [])
        lb2 = sch.dma("Q", lambda q, e=e: q.dma_start(
            out=brow.ap()[:, NH:NH + D], in_=b2_d[e:e + 1, :]))
        cb = sch.op("V", lambda v: v.tensor_copy(browb.ap()[:], brow.ap()[:]),
                    deps=[lb1, lb2], ms=True)
        return c1, c2, cb

    tok_w = load_we(0, 0)
    hs_war = None
    for e in range(NE):
        buf = e % 2
        c1, c2, cb = tok_w
        for mh in range(2):
            b = 4 + mh
            t_mm = None
            for kc in range(C):
                dd = ([c1] if kc == 0 else []) + \
                    (list(ctx.ps_tok[b]) if kc == 0 else [])
                t_mm = sch.op("T", lambda t, e=e, mh=mh, kc=kc, b=b, buf=buf:
                              t.matmul(
                    ctx.bank(b, [128, 16]),
                    w1s[buf].ap()[:, kc, 128 * mh:128 * (mh + 1)],
                    slin.ap()[:, kc, 16 * e:16 * (e + 1)],
                    start=(kc == 0), stop=False), deps=dd, ms=False)
            t_mm = sch.op("T", lambda t, e=e, mh=mh, b=b: t.matmul(
                ctx.bank(b, [128, 16]),
                browb.ap()[0:1, 128 * mh:128 * (mh + 1)], onesr16.ap()[:],
                start=False, stop=True), deps=[cb, t_o16], ms=True)
            tok = sch.op("S", lambda s, mh=mh, b=b: s.activation(
                hsb.ap()[:, mh, :], ctx.bank(b, [128, 16]), AF.Gelu),
                deps=[t_mm] + ([hs_war] if hs_war else []), ms=True)
            ctx.ps_tok[b] = (tok,)
        wtoks[("u1", buf)] = t_mm
        for nn in range(2):
            b = 6 + nn
            t_mm = None
            for kh in range(2):
                dd = ([c2, ("S", sch.ms["S"])] if kh == 0 else []) + \
                    (list(ctx.ps_tok[b]) if kh == 0 else [])
                t_mm = sch.op("T", lambda t, e=e, nn=nn, kh=kh, b=b, buf=buf:
                              t.matmul(
                    ctx.bank(b, [16, 512]), hsb.ap()[:, kh, :],
                    w2s[buf].ap()[:, kh, 512 * nn:512 * (nn + 1)],
                    start=(kh == 0), stop=False), deps=dd, ms=False)
            t_mm = sch.op("T", lambda t, e=e, nn=nn, b=b: t.matmul(
                ctx.bank(b, [16, 512]), onesr16.ap()[:],
                browb.ap()[0:1, NH + 512 * nn:NH + 512 * (nn + 1)],
                start=False, stop=True), deps=[cb], ms=True)
            tok = sch.op("S", lambda s, e=e, nn=nn, b=b: s.copy(
                soall.ap()[0:16, e, 512 * nn:512 * (nn + 1)],
                ctx.bank(b, [16, 512])), deps=[t_mm], ms=True)
            ctx.ps_tok[b] = (tok,)
        wtoks[("u2", buf)] = t_mm
        hs_war = t_mm
        btoks[e] = tok
        if e + 1 < NE:
            tok_w = load_we(e + 1, 1 - buf)
    sch.barrier()
    sb.free_(slin)
    for t_ in w1s + w2s + [wst, hsb]:
        sb.free_(t_)

    call = sb.alloc("call", [16, NE, 512], BF16)
    gtmp = [sb.alloc("gtmp", [128, 512], F32) for _ in range(2)]
    gt_use = [None, None]
    for tt in range(4):
        exs = []
        for e in range(NE):
            b = e % 2
            dd = list(ctx.ps_tok[b])
            t_mm = sch.op("T", lambda t, e=e, tt=tt, b=b: t.matmul(
                ctx.bank(b, [16, 512]), sel16.ap()[:, e % 8, :],
                cmbf.ap()[:, e // 8, 512 * tt:512 * (tt + 1)],
                start=True, stop=True), deps=dd, ms=True)
            tok = sch.op("S", lambda s, e=e, b=b: s.copy(
                call.ap()[0:16, e, :], ctx.bank(b, [16, 512])),
                deps=[t_mm], ms=True)
            ctx.ps_tok[b] = (tok,)
            exs.append(tok)
        for mo in range(C):
            b = 2 + mo % 2
            t_mm = None
            for e in range(NE):
                dd = ([exs[-1]] if e == 0 else []) + \
                    (list(ctx.ps_tok[b]) if e == 0 else [])
                t_mm = sch.op("T", lambda t, mo=mo, tt=tt, e=e, b=b: t.matmul(
                    ctx.bank(b), soall.ap()[0:16, e, 128 * mo:128 * (mo + 1)],
                    call.ap()[0:16, e, :], start=(e == 0), stop=(e == NE - 1)),
                    deps=dd, ms=(e == NE - 1))
            a = mo % 2
            t_ev = sch.op("V", lambda v, a=a, b=b: v.tensor_copy(
                gtmp[a].ap()[:], ctx.bank(b)),
                deps=[t_mm] + ([gt_use[a]] if gt_use[a] else []), ms=True)
            ctx.ps_tok[b] = (t_ev,)
            gt_use[a] = sch.dma("Q", lambda q, mo=mo, tt=tt, a=a: q.dma_start(
                out=g2_d[:, mo, 512 * tt:512 * (tt + 1)], in_=gtmp[a].ap()[:]),
                deps=[t_ev])
    sch.barrier()
    for t_ in [small, rows, ones_bf, ones_f, idb, idf, scl, muu, invb,
               cols, cmbf, soall, sel16, call, brow, onesr16] + sq + tm1 + \
            gtmp:
        sb.free_(t_)


PHASES = {"ln0": phase_ln0, "mprep": phase_mprep, "moescan": phase_moescan,
          "mout": phase_mout, "combine1": phase_combine1,
          "combine2": phase_combine2, "l1attn": phase_l1attn,
          "l1ffn": phase_l1ffn}


def build(cfg):
    nc = bass.Bass("TRN2", target_bir_lowering=False)
    ctx = Ctx(nc, cfg)
    for ph in cfg["phases"]:
        PHASES[ph](ctx)
    toks = ctx.sch.barrier()
    ctx.sch.op("P", lambda p: p.nop(), deps=toks)
    ctx.sch.build(nc)
    return nc


def _fm(v, inner=None):
    d = v.shape[0]
    c = d // 128
    out = v.reshape(c, 128, *v.shape[1:]).transpose(
        1, 0, *range(2, v.ndim + 1))
    return np.ascontiguousarray(out)


def stage_common(inp):
    f32 = np.float32

    def vec(v):
        return np.ascontiguousarray(np.asarray(v, f32).reshape(-1, 128).T)
    cm = {}
    cm["gate_w"] = _fm(inp["L0_ffn_gate_w"].astype(f32))
    cm["ident_f32"] = np.eye(128, dtype=f32)
    w_in = np.asarray(inp["L0_attn_W_in"], f32)
    cm["w_in_ssm"] = _fm(w_in[:, :D])
    cm["w_in_gate"] = _fm(w_in[:, D:])
    cm["w_dt"] = _fm(np.asarray(inp["L0_attn_W_dt"], f32))
    cm["w_bc"] = _fm(np.asarray(inp["L0_attn_W_bc"], f32))
    cm["b_in_ssm"] = vec(inp["L0_attn_b_in"][:D])
    cm["b_in_gate"] = vec(inp["L0_attn_b_in"][D:])
    cm["b_dt"] = vec(inp["L0_attn_b_dt"])
    cm["conv_w"] = _fm(np.asarray(inp["L0_attn_conv_w"], f32)[:, 0, :])
    cm["conv_b"] = vec(inp["L0_attn_conv_b"])
    cm["b_bc"] = np.ascontiguousarray(
        np.asarray(inp["L0_attn_b_bc"], f32).reshape(32, 1))
    cm["A_log"] = _fm(np.asarray(inp["L0_attn_A_log"], f32))
    cm["w_out"] = _fm(np.asarray(inp["L0_attn_W_out"], f32))
    cm["b_out"] = vec(inp["L0_attn_b_out"])
    cm["ln_g"] = vec(inp["L0_attn_ln_g"])
    cm["ln_b"] = vec(inp["L0_attn_ln_b"])
    cm["Dvec"] = vec(inp["L0_attn_D"])
    cm["ident_bf"] = np.eye(128, dtype=BF)

    def exp_fm(w):
        w = np.asarray(w, f32)
        E, d_, f_ = w.shape
        return np.ascontiguousarray(
            w.reshape(E, d_ // 128, 128, f_).transpose(0, 2, 1, 3))
    cm["exp_gate"] = exp_fm(inp["L0_ffn_exp_gate"])
    cm["exp_up"] = exp_fm(inp["L0_ffn_exp_up"])
    cm["exp_down"] = exp_fm(inp["L0_ffn_exp_down"])
    cm["sh_gate"] = exp_fm(inp["L0_ffn_sh_gate"])
    cm["sh_up"] = exp_fm(inp["L0_ffn_sh_up"])
    cm["sh_down"] = exp_fm(inp["L0_ffn_sh_down"])
    cm["wqT"] = _fm(np.ascontiguousarray(np.asarray(inp["L1_attn_Wq"], f32).T))
    cm["wkT"] = _fm(np.ascontiguousarray(np.asarray(inp["L1_attn_Wk"], f32).T))
    cm["wvT"] = _fm(np.ascontiguousarray(np.asarray(inp["L1_attn_Wv"], f32).T))
    cm["woT"] = _fm(np.ascontiguousarray(np.asarray(inp["L1_attn_Wo"], f32).T))
    cm["bq"] = vec(inp["L1_attn_bq"])
    cm["bk"] = vec(inp["L1_attn_bk"])
    cm["bo"] = vec(inp["L1_attn_bo"])
    cm["bk_row"] = np.asarray(inp["L1_attn_bk"], f32).reshape(1, D)
    cm["bv_row"] = np.asarray(inp["L1_attn_bv"], f32).reshape(1, D)
    cm["maskUT"] = np.triu(np.ones((128, 128), dtype=BF))
    cm["mu_fm"] = _fm(np.asarray(inp["L1_ffn_mu"], f32).reshape(D, 256))
    cm["l1_gamma"] = vec(inp["L1_ffn_ln_gamma"])
    cm["l1_beta"] = vec(inp["L1_ffn_ln_beta"])
    cm["l1_scale"] = np.asarray(inp["L1_ffn_scale"], f32).reshape(1, 1)
    w1 = np.asarray(inp["L1_ffn_w1"], f32)
    cm["w1_l1"] = np.ascontiguousarray(
        w1.reshape(16, 8, 128, 256).transpose(0, 2, 1, 3))
    cm["b1_l1"] = np.asarray(inp["L1_ffn_b1"], f32)
    w2 = np.asarray(inp["L1_ffn_w2"], f32)
    cm["w2_l1"] = np.ascontiguousarray(
        w2.reshape(16, 2, 128, 1024).transpose(0, 2, 1, 3))
    cm["b2_l1"] = np.asarray(inp["L1_ffn_b2"], f32)
    s16 = np.zeros((128, 8, 16), dtype=BF)
    for j in range(8):
        for m in range(16):
            s16[16 * j + m, j, m] = 1
    cm["sel16"] = np.ascontiguousarray(s16.reshape(128, 8 * 16))
    return cm


def stage_inputs(inputs):
    inputs = {k: np.asarray(v) for k, v in inputs.items()}
    cm = stage_common(inputs)
    x = np.ascontiguousarray(inputs["L0_attn_x"].astype(np.float32))
    in_maps = []
    for b in range(8):
        m = dict(cm)
        m["x"] = x[b]
        in_maps.append(m)
    return in_maps


ALL_INS = [
    "x", "gate_w", "ident_f32", "ident_bf", "w_in_ssm", "w_in_gate", "w_dt",
    "w_bc", "b_in_ssm", "b_in_gate", "b_dt", "conv_w", "conv_b", "b_bc",
    "A_log", "Dvec", "w_out", "b_out", "ln_g", "ln_b",
    "exp_gate", "exp_up", "exp_down", "sh_gate", "sh_up", "sh_down",
    "wqT", "wkT", "wvT", "woT", "bq", "bk", "bo", "bk_row", "bv_row",
    "maskUT", "mu_fm", "l1_gamma", "l1_beta", "l1_scale", "w1_l1", "b1_l1",
    "w2_l1", "b2_l1", "sel16",
]
ALL_PHASES = ["ln0", "mprep", "moescan", "mout", "combine1",
              "l1attn", "l1ffn", "combine2"]


def kernel(**inputs):
    cfg = {"phases": ALL_PHASES, "ins": ALL_INS, "outs": ["out"]}
    nc = build(cfg)
    in_maps = stage_inputs(inputs)
    res = run_bass_kernel_spmd(nc, in_maps, core_ids=list(range(8)))
    return np.stack([r["out"] for r in res.results]).astype(np.float32)
